# version 23
# baseline (speedup 1.0000x reference)
import numpy as np
import concourse.bass as bass
import concourse.mybir as mybir
from concourse.bass_utils import run_bass_kernel_spmd

F32 = mybir.dt.float32
BF16 = mybir.dt.bfloat16
AF = mybir.ActivationFunctionType
ALU = mybir.AluOpType
AX = mybir.AxisListType
AP = bass.AP

T = 2048
DM = 2048
HKV = 8
NH = 16
HD = 128
CH = 64
IN = 8192
NG = 2
KVG = HKV // NG
QHG = NH // NG
TBA = 256
NBA = T // TBA
CPB = TBA // CH
TBF = 512
NBF = T // TBF
RS128 = float(HD) ** -0.5
EPS = 1e-5

ENGS = ("tensor", "vector", "scalar", "gpsimd", "sync")


def _hadamard(n):
    i = np.arange(n)
    a = i[:, None] & i[None, :]
    pc = np.zeros((n, n), dtype=np.int64)
    for b in range(14):
        pc += (a >> b) & 1
    return np.where(pc % 2 == 0, 1.0, -1.0).astype(np.float32)


H128_NP = _hadamard(128)
BH16_NP = np.kron(np.eye(8, dtype=np.float32), _hadamard(16))
BH64_NP = np.kron(np.eye(2, dtype=np.float32), _hadamard(64))
IDENT_NP = np.eye(128, dtype=np.float32)
IDF_NP = np.eye(64, dtype=np.float32)
MASK64_NP = np.tile((np.arange(64)[None, :] >= np.arange(64)[:, None]).astype(np.float32), (2, 1))
ONES_NP = np.ones((1, 128), dtype=np.float32)


class Step:
    __slots__ = ("eng", "fn", "dma", "cval", "gid", "deps", "bg", "ch")


class Res:
    __slots__ = ("writer", "readers", "name")

    def __init__(self, name=""):
        self.writer = None
        self.readers = []
        self.name = name


class KB:
    def __init__(self):
        self.steps = []
        self.cnt = {e: 0 for e in ENGS}
        self.dcnt = {}
        self.frontier_deps = []
        self.frontier_pending = set()

    def emit(self, eng, fn, reads=(), writes=(), rmw=(), deps=(), dma=False,
             bg=False, ch=None):
        d = []
        for r in reads:
            if r.writer is not None:
                d.append(r.writer)
        for w in tuple(writes) + tuple(rmw):
            if w.writer is not None:
                d.append(w.writer)
            d.extend(w.readers)
        d.extend(deps)
        if eng in self.frontier_pending:
            d.extend(self.frontier_deps)
            self.frontier_pending.discard(eng)
        st = Step()
        st.eng = eng
        st.fn = fn
        st.dma = dma
        st.bg = bg
        st.gid = len(self.steps)
        if dma:
            st.ch = ch if ch is not None else f"_q_{eng}"
            self.dcnt[st.ch] = self.dcnt.get(st.ch, 0) + 1
            st.cval = 16 * self.dcnt[st.ch]
        else:
            st.ch = None
            self.cnt[eng] += 1
            st.cval = self.cnt[eng]
        for r in reads:
            r.readers.append(st)
        for w in tuple(writes) + tuple(rmw):
            w.writer = st
            w.readers = []
        st.deps = []
        seen = set()
        for s in d:
            if s is None or s.gid in seen or s is st:
                continue
            seen.add(s.gid)
            assert s.gid < st.gid
            st.deps.append(s)
        self.steps.append(st)
        return st

    def barrier(self):
        last = {}
        for st in self.steps:
            if st.bg:
                continue
            last[(st.eng, st.dma)] = st
        self.frontier_deps = list(last.values())
        self.frontier_pending = set(ENGS)

    def emit_all(self, nc, block):
        sems = {}
        for e in ENGS:
            if self.cnt[e]:
                sems[("c", e)] = nc.semaphore(f"sc_{e}").__enter__()
        for chn in self.dcnt:
            sems[("d", chn)] = nc.semaphore(f"sd_{chn}").__enter__()
        for h in sems.values():
            nc.gpsimd.sem_clear(h)
        nc.all_engine_barrier()
        decos = {"tensor": block.tensor, "vector": block.vector,
                 "scalar": block.scalar, "gpsimd": block.gpsimd,
                 "sync": block.sync}
        total_dma = {chn: 16 * n for chn, n in self.dcnt.items()}

        def mk(ename):
            def body(e):
                water = {}
                for st in self.steps:
                    if st.eng != ename:
                        continue
                    for dep in st.deps:
                        if (dep.eng == ename == "tensor"
                                and not dep.dma and not st.dma):
                            continue
                        key = ("d", dep.ch) if dep.dma else ("c", dep.eng)
                        if water.get(key, 0) >= dep.cval:
                            continue
                        water[key] = dep.cval
                        e.wait_ge(sems[key], dep.cval)
                    last = st.fn(e)
                    if st.dma:
                        last.then_inc(sems[("d", st.ch)], 16)
                    else:
                        last.then_inc(sems[("c", ename)], 1)
                if ename == "sync":
                    for chn, tot in total_dma.items():
                        e.wait_ge(sems[("d", chn)], tot)
            return body

        for ename in ENGS:
            decos[ename](mk(ename))


class Sbuf:
    BASE = 16544
    TOP = 229340

    def __init__(self, nc):
        self.nc = nc
        self.off = self.BASE
        self.marks = []
        self.peak = self.BASE
        self.n = 0

    def alloc(self, shape, dtype, name=None):
        esz = 4 if dtype == F32 else 2
        nb = shape[1] * esz
        off = (self.off + 31) // 32 * 32
        assert off + nb <= self.TOP, f"SBUF overflow: {name} ({nb}B at {off})"
        self.off = off + nb
        self.peak = max(self.peak, self.off)
        self.n += 1
        self.offsets = getattr(self, "offsets", {})
        self.offsets[name or f"sb{self.n}"] = off
        return self.nc.alloc_sbuf_tensor_at(name or f"sb{self.n}", list(shape),
                                            dtype, offset=off)

    def mark(self):
        self.marks.append(self.off)

    def release(self):
        self.off = self.marks.pop()


def build_program(debug=False):
    nc = bass.Bass("TRN2", target_bir_lowering=False)
    kb = KB()

    din = {}
    for name, shape in [
        ("x", [T, DM]), ("wq", [DM, DM]), ("bq", [1, DM]), ("wk", [DM, 1024]),
        ("bk", [1, 1024]), ("wv", [DM, 1024]), ("bv", [1, 1024]),
        ("wg1", [DM, 16]), ("wg2", [16, 1024]), ("bg2", [1, 1024]),
        ("gnw", [1, HD]), ("wo", [DM, DM]), ("su_ug", [1, DM]), ("su_dp", [1, IN]),
        ("wsug", [1, 2 * IN]), ("wsdp", [1, DM]), ("wup", [IN, DM]),
        ("wgate", [IN, DM]), ("wdown", [DM, IN]),
    ]:
        din[name] = nc.dram_tensor(name, shape, F32, kind="ExternalInput")
    y = nc.dram_tensor("y", [T, DM], F32, kind="ExternalOutput")

    c_h128 = nc.inline_tensor(H128_NP, name="c_h128")
    c_bh16 = nc.inline_tensor(BH16_NP, name="c_bh16")
    c_bh64 = nc.inline_tensor(BH64_NP, name="c_bh64")
    c_ident = nc.inline_tensor(IDENT_NP, name="c_ident")
    c_idf = nc.inline_tensor(IDF_NP, name="c_idf")
    c_mask = nc.inline_tensor(MASK64_NP, name="c_mask")
    c_ones = nc.inline_tensor(ONES_NP, name="c_ones")

    dkind = "ExternalOutput" if debug else "Internal"
    fulldbg = (debug == True)
    xnT_h = nc.dram_tensor("xnT_h", [128, 16 * T], BF16, kind=dkind)
    oT_h = nc.dram_tensor("oT_h", [128, NH * T], BF16, kind=dkind)
    s_h = nc.dram_tensor("s_h", [T, DM], F32, kind=dkind)
    if fulldbg:
        x1_d = nc.dram_tensor("x1_d", [128, 16 * T], BF16, kind="ExternalOutput")
        g_d = nc.dram_tensor("g_d", [T, DM], BF16, kind="ExternalOutput")
    if debug == 3:
        xt_d = nc.dram_tensor("xt_d", [128, DM], F32, kind="ExternalOutput")
        colt_d = nc.dram_tensor("colt_d", [128, 8], F32, kind="ExternalOutput")
        xn_d = nc.dram_tensor("xn_d", [128, DM], BF16, kind="ExternalOutput")
    wu_nat = nc.dram_tensor("wu_nat", [IN, DM], BF16, kind="Internal")
    wg_nat = nc.dram_tensor("wg_nat", [IN, DM], BF16, kind="Internal")
    wd_nat = nc.dram_tensor("wd_nat", [DM, IN], BF16, kind="Internal")
    wuT_h = nc.dram_tensor("wuT_h", [128, 64 * 2048], BF16, kind="Internal")
    wgT_h = nc.dram_tensor("wgT_h", [128, 64 * 2048], BF16, kind="Internal")
    wdT_h = nc.dram_tensor("wdT_h", [128, 64 * DM], BF16, kind="Internal")

    sb = Sbuf(nc)
    h128b = sb.alloc([128, 128], BF16, "h128b")
    bh16b = sb.alloc([128, 128], BF16, "bh16b")
    bh64b = sb.alloc([128, 128], BF16, "bh64b")
    identb = sb.alloc([128, 128], BF16, "identb")
    identf = sb.alloc([64, 64], F32, "identf")
    mask64 = sb.alloc([128, 64], F32, "mask64")
    onesf = sb.alloc([1, 128], F32, "onesf")
    zer512 = sb.alloc([128, 512], BF16, "zer512")
    scanmask = sb.alloc([128, TBA], F32, "scanmask")
    bq_s = sb.alloc([128, 16], F32, "bq_s")
    bk_s = sb.alloc([128, 8], F32, "bk_s")
    bg2_s = sb.alloc([128, 8], F32, "bg2_s")
    gnw_s = sb.alloc([128, 1], F32, "gnw_s")
    suug_s = sb.alloc([128, 16], F32, "suug_s")
    sup_s = sb.alloc([128, 64], F32, "sup_s")
    sgate_s = sb.alloc([128, 64], F32, "sgate_s")
    wsdb = sb.alloc([128, 2048], F32, "wsdb")
    ms_b = sb.alloc([128, QHG * CPB // 2], F32, "ms_b")
    rstd_b = sb.alloc([128, QHG * CPB // 2], F32, "rstd_b")
    colt_t = [sb.alloc([128, 8], F32, f"colt{i}") for i in range(2)]
    epsc = sb.alloc([128, 1], F32, "epsc")
    sqscr = sb.alloc([128, 128], BF16, "sqscr")

    R = Res
    r_consts = R("consts")
    r_vecs = R("vecs")
    r_vbias = R("vbias")
    r_wsdb = R("wsdb")

    ps = [nc.alloc_psum_tensor(f"ps{i}", [128, 512], F32) for i in range(8)]
    r_ps = [R(f"ps{i}") for i in range(8)]

    def dma(q, dst, src, reads=(), writes=(), rmw=(), deps=(), slow=False,
            bg=False, ch=None):
        if slow:
            fn = lambda e: e.dma_start(dst, src, allow_slow_non_contiguous=True)
        else:
            fn = lambda e: e.dma_start(dst, src)
        return kb.emit(q, fn, reads=reads, writes=writes, rmw=rmw, deps=deps,
                       dma=True, bg=bg, ch=ch)

    for dst, src in [(h128b, c_h128), (bh16b, c_bh16), (bh64b, c_bh64),
                     (identb, c_ident)]:
        dma("gpsimd", dst[:, :], src.ap(), rmw=[r_consts], ch="consts")
    dma("gpsimd", identf[:, :], c_idf.ap(), rmw=[r_consts], ch="consts")
    dma("gpsimd", mask64[:, :], c_mask.ap(), rmw=[r_consts], ch="consts")
    dma("gpsimd", onesf[:, :], c_ones.ap(), rmw=[r_consts], ch="consts")
    kb.emit("vector", lambda e: e.memset(zer512[:, :], 0.0), rmw=[r_consts])
    kb.emit("vector", lambda e: e.memset(epsc[:, :], EPS), rmw=[r_consts])
    kb.emit("vector", lambda e: e.memset(scanmask[:, :], 1.0), rmw=[r_consts])
    kb.emit("vector",
            lambda e: e.memset(AP(scanmask, 0, [[TBA, 128], [CH, CPB]]), 0.0),
            rmw=[r_consts])

    sb.mark()
    vrow = sb.alloc([64, 128], F32, "vrow")
    bvrow = sb.alloc([1, 1024], F32, "bvrow")
    wsdrow = sb.alloc([1, 2048], F32, "wsdrow")

    sb.release()
    sb.mark()
    xt = [sb.alloc([128, DM], F32, f"xt{i}") for i in range(2)]
    xsq = sb.alloc([128, DM], F32, "xsq")
    xnb16 = sb.alloc([128, DM], BF16, "xnb16")
    xnTblk = [sb.alloc([128, 16 * TBA], BF16, f"xnTblk{i}") for i in range(2)]

    r_xt = [R(), R()]
    r_xsq = R()
    r_xn = R()
    r_xnbuf = [R(), R()]
    r_xnT_h = [R(f"xnTh{b}") for b in range(NBA)]
    r_colt = [R(), R()]
    r_natb = [R() for _ in range(4)]
    r_stgb = [R() for _ in range(4)]
    r_wT = {m: R(f"wT_{m}") for m in ("u", "g", "d")}

    vbias = sb.alloc([128, 1024], F32, "vbias")
    wq_sb = sb.alloc([128, 16 * 1024], BF16, "wq_sb")
    wk_sb = sb.alloc([128, 16 * 512], BF16, "wk_sb")
    wv_sb = sb.alloc([128, 16 * 512], BF16, "wv_sb")
    wg1_sb = sb.alloc([128, 16 * 16], BF16, "wg1_sb")
    wg2_sb = sb.alloc([16, 512], BF16, "wg2_sb")
    qb = [sb.alloc([128, QHG * TBA], BF16, f"qb{i}") for i in range(2)]
    kraw = [sb.alloc([128, KVG * TBA], BF16, f"kraw{i}") for i in range(2)]
    kfm = [sb.alloc([128, KVG * TBA], BF16, f"kfm{i}") for i in range(2)]
    kpm = [sb.alloc([128, KVG * TBA], BF16, f"kpm{i}") for i in range(2)]
    vb = [sb.alloc([128, KVG * TBA], BF16, f"vb{i}") for i in range(2)]
    gt1 = sb.alloc([128, KVG * TBA], F32, "gt1")
    gt2 = sb.alloc([128, KVG * TBA], F32, "gt2")
    z1sb = sb.alloc([16, TBA], BF16, "z1sb")
    bbuf = sb.alloc([128, KVG * TBA], F32, "bbuf")
    expb = [sb.alloc([128, KVG * TBA], F32, f"expb{i}") for i in range(2)]
    expmb = sb.alloc([128, KVG * TBA], F32, "expmb")
    qs = [sb.alloc([128, QHG * CH], BF16, f"qs{i}") for i in range(2)]
    ssnap = [sb.alloc([128, KVG * 128], BF16, f"ssnap{i}") for i in range(2)]
    ats = sb.alloc([128, QHG * CH], BF16, "ats")
    osb = [sb.alloc([128, QHG * 128], BF16, f"osb{i}") for i in range(2)]
    oTb = [sb.alloc([128, QHG * 128], BF16, f"oTb{i}") for i in range(2)]

    r_wgrp = R("wgrp")
    r_qb = [R(), R()]
    r_kraw = [R(), R()]
    r_kfm = [R(), R()]
    r_kpm = [R(), R()]
    r_vb = [R(), R()]
    r_gt1 = R()
    r_gt2 = R()
    r_z1 = R()
    r_b = R()
    r_expb = [R(), R()]
    r_expmb = R()
    r_qs = [R(), R()]
    r_ssnap = [R(), R()]
    r_ats = [R(), R()]
    r_osb = [R(), R()]
    r_oTb = [R(), R()]
    r_ms = [R() for _ in range(4)]
    r_rstd = [R() for _ in range(4)]
    r_oT_h = R("oT_h")

    def vecload(dst_ap, src_t, src_off, n, scale=None, mult_ps=False):
        rows = n // 128
        dma("gpsimd", vrow[0:rows, :], AP(src_t, src_off, [[128, rows], [1, 128]]),
            reads=[r_consts], rmw=[r_vecs], ch="vec")
        kb.emit("tensor",
                lambda e, rows=rows: e.matmul(ps[6][:, 0:rows], vrow[0:rows, :],
                                              identf[0:rows, 0:rows],
                                              start=True, stop=True),
                reads=[r_vecs, r_consts], writes=[r_ps[6]])
        if mult_ps:
            fn = lambda e, rows=rows: e.tensor_mul(dst_ap, dst_ap, ps[6][:, 0:rows])
        elif scale is not None:
            fn = lambda e, rows=rows, s=float(scale): e.tensor_scalar_mul(
                dst_ap, ps[6][:, 0:rows], s)
        else:
            fn = lambda e, rows=rows: e.tensor_copy(dst_ap, ps[6][:, 0:rows])
        kb.emit("vector", fn, reads=[r_ps[6]], rmw=[r_vecs])

    c2048 = 1.0 / np.sqrt(2048.0)
    vecload(bq_s[:, :], din["bq"], 0, DM, scale=RS128)
    vecload(bk_s[:, :], din["bk"], 0, 1024)
    vecload(bg2_s[:, :], din["bg2"], 0, 1024)
    vecload(gnw_s[:, :], din["gnw"], 0, HD)
    vecload(suug_s[:, :], din["su_ug"], 0, DM)
    vecload(sup_s[:, :], din["wsug"], 0, IN, scale=c2048)
    vecload(sup_s[:, :], din["su_dp"], 0, IN, mult_ps=True)
    vecload(sgate_s[:, :], din["wsug"], IN, IN, scale=c2048)

    dma("gpsimd", bvrow[:, :], din["bv"].ap(), rmw=[r_vbias], ch="bvrow")
    dma("gpsimd", wsdrow[:, :], din["wsdp"].ap(), rmw=[r_wsdb], ch="wsdrow")
    c8192 = 1.0 / np.sqrt(8192.0)
    for i in range(2):
        kb.emit("tensor",
                lambda e, i=i: e.matmul(ps[7][:, 0:512], onesf[:, :],
                                        bvrow[:, i * 512:(i + 1) * 512],
                                        start=True, stop=True),
                reads=[r_vbias], writes=[r_ps[7]])
        kb.emit("vector",
                lambda e, i=i: e.tensor_copy(vbias[:, i * 512:(i + 1) * 512],
                                             ps[7][:, 0:512]),
                reads=[r_ps[7]], rmw=[r_vbias])
    for i in range(4):
        kb.emit("tensor",
                lambda e, i=i: e.matmul(ps[7][:, 0:512], onesf[:, :],
                                        wsdrow[:, i * 512:(i + 1) * 512],
                                        start=True, stop=True),
                reads=[r_wsdb], writes=[r_ps[7]])
        kb.emit("vector",
                lambda e, i=i: e.tensor_scalar_mul(wsdb[:, i * 512:(i + 1) * 512],
                                                   ps[7][:, 0:512], c8192),
                reads=[r_ps[7]], rmw=[r_wsdb])

    def emit_gweights(g):
        for q4 in range(4):
            dma("gpsimd",
                AP(wq_sb, q4 * 4 * 1024, [[16 * 1024, 128], [1024, 4], [1, 1024]]),
                AP(din["wq"], q4 * 4 * 128 * DM + g * 1024,
                   [[DM, 128], [128 * DM, 4], [1, 1024]]),
                rmw=[r_wgrp], ch="gw")
            dma("gpsimd",
                AP(wk_sb, q4 * 4 * 512, [[16 * 512, 128], [512, 4], [1, 512]]),
                AP(din["wk"], q4 * 4 * 128 * 1024 + g * 512,
                   [[1024, 128], [128 * 1024, 4], [1, 512]]),
                rmw=[r_wgrp], ch="gw")
            dma("gpsimd",
                AP(wv_sb, q4 * 4 * 512, [[16 * 512, 128], [512, 4], [1, 512]]),
                AP(din["wv"], q4 * 4 * 128 * 1024 + g * 512,
                   [[1024, 128], [128 * 1024, 4], [1, 512]]),
                rmw=[r_wgrp], ch="gw")
        dma("gpsimd", AP(wg1_sb, 0, [[256, 128], [16, 16], [1, 16]]),
            AP(din["wg1"], 0, [[16, 128], [128 * 16, 16], [1, 16]]),
            rmw=[r_wgrp], ch="gw")
        dma("gpsimd", wg2_sb[:, :],
            AP(din["wg2"], g * 512, [[1024, 16], [1, 512]]),
            rmw=[r_wgrp], ch="gw")

    emit_gweights(0)

    r_wnat = {m: R(f"wnat_{m}") for m in ("u", "g", "d")}
    for m, src, dst in (("d", din["wdown"], wd_nat),):
        rows, cols = src.shape
        nsp = 8
        rr = rows // nsp
        for r8 in range(nsp):
            dma("gpsimd", AP(dst, r8 * rr * cols, [[cols, rr], [1, cols]]),
                AP(src, r8 * rr * cols, [[cols, rr], [1, cols]]),
                rmw=[r_wnat[m]], bg=True, ch="wnat_" + m)

    kb.barrier()

    for i in range(T // 128):
        pi = i % 2
        blk = i // 2
        sub = i % 2
        bi = blk % 2
        dma("sync", xt[pi][:, :], AP(din["x"], i * 128 * DM, [[DM, 128], [1, DM]]),
            writes=[r_xt[pi]], ch=f"xt{pi}")
        kb.emit("vector",
                lambda e, pi=pi: e.reduce_sum(colt_t[pi][:, 0:1], xt[pi][:, :], axis=AX.X),
                reads=[r_xt[pi]], rmw=[r_colt[pi]])
        kb.emit("vector",
                lambda e, pi=pi: e.scalar_tensor_tensor(
                    xsq[:, :], xt[pi][:, :], 1.0, xt[pi][:, :], ALU.mult, ALU.mult,
                    accum_out=colt_t[pi][:, 1:2]),
                reads=[r_xt[pi]], writes=[r_xsq], rmw=[r_colt[pi]])

        def stats1(e, pi=pi):
            e.tensor_scalar_mul(colt_t[pi][:, 2:3], colt_t[pi][:, 0:1], 1.0 / DM)
            return e.tensor_scalar_mul(colt_t[pi][:, 3:4], colt_t[pi][:, 1:2], 1.0 / DM)
        kb.emit("vector", stats1, rmw=[r_colt[pi]])
        kb.emit("vector",
                lambda e, pi=pi: e.scalar_tensor_tensor(
                    colt_t[pi][:, 4:5], colt_t[pi][:, 2:3], -1.0,
                    colt_t[pi][:, 2:3], ALU.mult, ALU.mult),
                rmw=[r_colt[pi]])
        kb.emit("vector",
                lambda e, pi=pi: e.tensor_add(colt_t[pi][:, 5:6],
                                              colt_t[pi][:, 3:4],
                                              colt_t[pi][:, 4:5]),
                rmw=[r_colt[pi]])
        kb.emit("scalar",
                lambda e, pi=pi: e.activation(colt_t[pi][:, 6:7],
                                              colt_t[pi][:, 5:6], AF.Ln,
                                              bias=epsc[:, 0:1]),
                rmw=[r_colt[pi]])
        kb.emit("scalar",
                lambda e, pi=pi: e.activation(colt_t[pi][:, 6:7],
                                              colt_t[pi][:, 6:7], AF.Exp,
                                              scale=-0.5),
                rmw=[r_colt[pi]])
        kb.emit("vector",
                lambda e, pi=pi: e.scalar_tensor_tensor(
                    colt_t[pi][:, 7:8], colt_t[pi][:, 2:3], -1.0,
                    colt_t[pi][:, 6:7], ALU.mult, ALU.mult),
                rmw=[r_colt[pi]])
        kb.emit("vector",
                lambda e, pi=pi: e.tensor_scalar(xnb16[:, :], xt[pi][:, :],
                                                 colt_t[pi][:, 6:7], colt_t[pi][:, 7:8],
                                                 ALU.mult, ALU.add),
                reads=[r_xt[pi], r_colt[pi]], writes=[r_xn])
        if debug == 3 and i == 0:
            dma("gpsimd", xt_d.ap(), xt[0][:, :], reads=[r_xt[0]], ch="dbg2")
            dma("gpsimd", colt_d.ap(), colt_t[pi][:, :], reads=[r_colt[pi]], ch="dbg2")
            dma("gpsimd", xn_d.ap(), xnb16[:, :], reads=[r_xn], ch="dbg2")
        for g4 in range(4):
            pb = 6 + (g4 % 2)

            def tmm(e, g4=g4, pb=pb):
                last = None
                for k in range(4):
                    c = g4 * 4 + k
                    last = e.matmul(ps[pb][:, k * 128:(k + 1) * 128],
                                    xnb16[:, c * 128:(c + 1) * 128],
                                    identb[:, :], start=True, stop=True)
                return last
            kb.emit("tensor", tmm, reads=[r_xn, r_consts], writes=[r_ps[pb]])
            dst = AP(xnTblk[bi], g4 * 4 * TBA + sub * 128,
                     [[16 * TBA, 128], [TBA, 4], [1, 128]])
            src = AP(ps[pb], 0, [[512, 128], [128, 4], [1, 128]])
            kb.emit("vector", lambda e, dst=dst, src=src: e.tensor_copy(dst, src),
                    reads=[r_ps[pb]], rmw=[r_xnbuf[bi]])
        if sub == 1:
            dma("sync", AP(xnT_h, blk * TBA, [[16 * T, 128], [T, 16], [1, TBA]]),
                AP(xnTblk[bi], 0, [[16 * TBA, 128], [TBA, 16], [1, TBA]]),
                reads=[r_xnbuf[bi]], writes=[r_xnT_h[blk]], ch=f"xnst{bi}")

    prep_jobs = []
    for m, nat, dstT in (("u", din["wup"], wuT_h), ("g", din["wgate"], wgT_h),
                         ("d", wd_nat, wdT_h)):
        for ic in range(64):
            if m == "d":
                jr, jc = ic % 16, ic // 16
                src_ap = AP(nat, jr * 128 * IN + jc * 2048,
                            [[IN, 128], [1, 2048]])
                nat_reads = [r_wnat["d"]]
                dst_ap = AP(dstT, jc * 16 * DM + jr * 128,
                            [[64 * DM, 128], [DM, 16], [1, 128]])
            else:
                src_ap = AP(nat, ic * 128 * DM, [[DM, 128], [1, DM]])
                nat_reads = []
                dst_ap = AP(dstT, ic * 2048, [[64 * 2048, 128], [1, 2048]])

            def job(m=m, ic=ic, src_ap=src_ap, nat_reads=nat_reads,
                    dst_ap=dst_ap):
                pp = ic % 4
                dma("gpsimd", natb[pp][:, :], src_ap,
                    reads=nat_reads, writes=[r_natb[pp]], ch=f"natb{pp}")
                for g4 in range(4):
                    pb = 1 + (g4 % 2)

                    def tmms(e, pp=pp, g4=g4, pb=pb):
                        last = None
                        for k in range(4):
                            hc = g4 * 4 + k
                            last = e.matmul(ps[pb][:, k * 128:(k + 1) * 128],
                                            natb[pp][:, hc * 128:(hc + 1) * 128],
                                            identb[:, :], start=True, stop=True)
                        return last
                    kb.emit("tensor", tmms, reads=[r_natb[pp], r_consts],
                            writes=[r_ps[pb]])
                    kb.emit("vector",
                            lambda e, pp=pp, g4=g4, pb=pb: e.tensor_copy(
                                stgb[pp][:, g4 * 512:(g4 + 1) * 512],
                                ps[pb][:, 0:512]),
                            reads=[r_ps[pb]], rmw=[r_stgb[pp]])
                dma("sync", dst_ap, stgb[pp][:, :],
                    reads=[r_stgb[pp]], rmw=[r_wT[m]], ch=f"stg{pp}")
            prep_jobs.append(job)


    def emit_prologue(g, blk):
        bi = blk % 2
        xnb = xnTblk[bi]
        bi = blk % 2
        xnb = xnTblk[bi]
        dma("sync", AP(xnTblk[bi], 0, [[16 * TBA, 128], [TBA, 16], [1, TBA]]),
            AP(xnT_h, blk * TBA, [[16 * T, 128], [T, 16], [1, TBA]]),
            reads=[r_xnT_h[blk]], writes=[r_xnbuf[bi]], ch=f"xnld{bi}")

        pjc = [0]

        def projmm(mm_fn, ev_eng, ev_fn, wr, extra_reads=(), bi=bi):
            pb = 6 + (pjc[0] % 2)
            pjc[0] += 1
            kb.emit("tensor", lambda e, pb=pb, f=mm_fn: f(e, pb),
                    reads=[r_xnbuf[bi], r_wgrp] + list(extra_reads),
                    writes=[r_ps[pb]])
            kb.emit(ev_eng, lambda e, pb=pb, f=ev_fn: f(e, pb),
                    reads=[r_ps[pb], r_vecs, r_vbias], rmw=[wr])

        for hq in range(QHG):
            def mmq(e, pb, hq=hq, xnb=xnb):
                last = None
                for hc in range(16):
                    last = e.matmul(
                        ps[pb][:, 0:TBA],
                        wq_sb[:, hc * 1024 + hq * 128: hc * 1024 + hq * 128 + 128],
                        AP(xnb, hc * TBA, [[16 * TBA, 128], [1, TBA]]),
                        start=(hc == 0), stop=(hc == 15))
                return last

            def evq(e, pb, hq=hq, bi=bi, g=g):
                return e.activation(qb[bi][:, hq * TBA:(hq + 1) * TBA],
                                    ps[pb][:, 0:TBA], AF.Relu,
                                    bias=bq_s[:, g * 8 + hq:g * 8 + hq + 1],
                                    scale=RS128)
            projmm(mmq, "scalar", evq, r_qb[bi])
        for kv in range(KVG):
            def mmk(e, pb, kv=kv, xnb=xnb):
                last = None
                for hc in range(16):
                    last = e.matmul(
                        ps[pb][:, 0:TBA],
                        wk_sb[:, hc * 512 + kv * 128: hc * 512 + kv * 128 + 128],
                        AP(xnb, hc * TBA, [[16 * TBA, 128], [1, TBA]]),
                        start=(hc == 0), stop=(hc == 15))
                return last

            def evk(e, pb, kv=kv, bi=bi, g=g):
                return e.activation(kraw[bi][:, kv * TBA:(kv + 1) * TBA],
                                    ps[pb][:, 0:TBA], AF.Relu,
                                    bias=bk_s[:, g * 4 + kv:g * 4 + kv + 1])
            projmm(mmk, "scalar", evk, r_kraw[bi])
        for tt in range(2):
            def mmv(e, pb, tt=tt, xnb=xnb):
                last = None
                for hc in range(16):
                    lhs = AP(xnb, hc * TBA + tt * 128, [[16 * TBA, 128], [1, 128]])
                    last = e.matmul(ps[pb][:, 0:512], lhs,
                                    wv_sb[:, hc * 512:(hc + 1) * 512],
                                    start=(hc == 0), stop=(hc == 15))
                return last

            def evv(e, pb, tt=tt, bi=bi, g=g):
                return e.tensor_add(vb[bi][:, tt * 512:(tt + 1) * 512],
                                    ps[pb][:, 0:512],
                                    vbias[:, g * 512:(g + 1) * 512])
            projmm(mmv, "vector", evv, r_vb[bi])

        def mmz1(e, pb, xnb=xnb):
            last = None
            for hc in range(16):
                last = e.matmul(ps[pb][0:16, 0:TBA],
                                wg1_sb[:, hc * 16:(hc + 1) * 16],
                                AP(xnb, hc * TBA, [[16 * TBA, 128], [1, TBA]]),
                                start=(hc == 0), stop=(hc == 15))
            return last

        def evz1(e, pb):
            return e.tensor_copy(z1sb[:, :], ps[pb][0:16, 0:TBA])
        projmm(mmz1, "vector", evz1, r_z1)
        for kv in range(KVG):
            def mmz2(e, pb, kv=kv):
                return e.matmul(ps[pb][:, 0:TBA],
                                wg2_sb[:, kv * 128:(kv + 1) * 128],
                                z1sb[:, :], start=True, stop=True)

            def evz2(e, pb, kv=kv, g=g):
                return e.tensor_scalar_add(gt1[:, kv * TBA:(kv + 1) * TBA],
                                           ps[pb][:, 0:TBA],
                                           bg2_s[:, g * 4 + kv:g * 4 + kv + 1])
            pb = 6 + (pjc[0] % 2)
            pjc[0] += 1
            kb.emit("tensor", lambda e, pb=pb, f=mmz2: f(e, pb),
                    reads=[r_z1, r_wgrp], writes=[r_ps[pb]])
            kb.emit("vector", lambda e, pb=pb, f=evz2: f(e, pb),
                    reads=[r_ps[pb], r_vecs], rmw=[r_gt1])

        kb.emit("scalar", lambda e: e.activation(gt2[:, :], gt1[:, :], AF.Abs),
                reads=[r_gt1], writes=[r_gt2])
        kb.emit("scalar",
                lambda e: e.activation(gt2[:, :], gt2[:, :], AF.Exp, scale=-1.0),
                rmw=[r_gt2])
        kb.emit("scalar",
                lambda e: e.activation(gt2[:, :], gt2[:, :], AF.Ln, bias=1.0),
                rmw=[r_gt2])
        kb.emit("vector",
                lambda e: e.tensor_scalar_min(gt1[:, :], gt1[:, :], 0.0),
                rmw=[r_gt1])
        kb.emit("vector",
                lambda e: e.tensor_sub(gt1[:, :], gt1[:, :], gt2[:, :]),
                reads=[r_gt2], rmw=[r_gt1])
        for kv in range(KVG):
            kb.emit("vector",
                    lambda e, kv=kv, bi=bi: e.tensor_tensor_scan(
                        bbuf[:, kv * TBA:(kv + 1) * TBA],
                        scanmask[:, :], gt1[:, kv * TBA:(kv + 1) * TBA],
                        0.0, ALU.mult, ALU.add),
                    reads=[r_gt1, r_consts], rmw=[r_b])
        kb.emit("scalar",
                lambda e, bi=bi: e.activation(expb[bi][:, :], bbuf[:, :],
                                              AF.Exp, scale=1.0 / 16.0),
                reads=[r_b], writes=[r_expb[bi]])
        kb.emit("scalar",
                lambda e, bi=bi: e.activation(expmb[:, :], bbuf[:, :],
                                              AF.Exp, scale=-1.0 / 16.0),
                reads=[r_b], writes=[r_expmb])
        kb.emit("vector",
                lambda e, bi=bi: e.tensor_mul(kfm[bi][:, :], kraw[bi][:, :],
                                              expmb[:, :]),
                reads=[r_kraw[bi], r_expmb], writes=[r_kfm[bi]])

        def qexp(e, bi=bi):
            last = None
            for hq in range(QHG):
                kv = hq // 2
                last = e.tensor_mul(
                    qb[bi][:, hq * TBA:(hq + 1) * TBA],
                    qb[bi][:, hq * TBA:(hq + 1) * TBA],
                    expb[bi][:, kv * TBA:(kv + 1) * TBA])
            return last
        kb.emit("vector", qexp, reads=[r_expb[bi]], rmw=[r_qb[bi]])
        for half in range(2):
            pb = 6 + (half % 2)

            def mmt(e, pb=pb, half=half, bi=bi):
                last = None
                for k in range(4):
                    j = half * 4 + k
                    last = e.matmul(ps[pb][:, k * 128:(k + 1) * 128],
                                    kfm[bi][:, j * 128:(j + 1) * 128],
                                    identb[:, :], start=True, stop=True)
                return last
            kb.emit("tensor", mmt, reads=[r_kfm[bi], r_consts], writes=[r_ps[pb]])

            def evt(e, pb=pb, half=half, bi=bi):
                last = None
                for k in range(4):
                    j = half * 4 + k
                    kv, tt = j // 2, j % 2
                    last = e.tensor_copy(
                        kpm[bi][:, tt * 512 + kv * 128: tt * 512 + kv * 128 + 128],
                        ps[pb][:, k * 128:(k + 1) * 128])
                return last
            kb.emit("vector", evt, reads=[r_ps[pb]], rmw=[r_kpm[bi]])


    def emit_scan(g, blk):
        bi = blk % 2
        tj0 = [0]
        for cl in range(CPB):
            cpar = cl % 2
            tt, rr = cl // 2, (cl % 2) * 64

            def sdelta(e, bi=bi, tt=tt, rr=rr):
                last = None
                for kv in range(KVG):
                    c0 = tt * 512 + kv * 128
                    last = e.matmul(ps[0][:, kv * 128:(kv + 1) * 128],
                                    kpm[bi][rr:rr + 64, c0:c0 + 128],
                                    vb[bi][rr:rr + 64, c0:c0 + 128],
                                    start=False, stop=False,
                                    skip_group_check=True)
                return last
            kb.emit("tensor", sdelta, reads=[r_kpm[bi], r_vb[bi]], rmw=[r_ps[0]])

            def sdecay(e, cl=cl, bi=bi, cpar=cpar):
                last = None
                for kv in range(KVG):
                    cc = kv * TBA + (cl + 1) * CH - 1
                    last = e.tensor_scalar_mul(
                        ps[0][:, kv * 128:(kv + 1) * 128],
                        ps[0][:, kv * 128:(kv + 1) * 128],
                        expb[bi][:, cc:cc + 1])
                return e.tensor_copy(ssnap[1 - cpar][:, :], ps[0][:, :])
            kb.emit("vector", sdecay, reads=[r_expb[bi]], rmw=[r_ps[0]],
                    writes=[r_ssnap[1 - cpar]])

            def atmm(e, cl=cl, bi=bi, cpar=cpar, rr=rr):
                last = None
                for hq in range(QHG):
                    kv = hq // 2
                    last = e.matmul(
                        ps[5][rr:rr + 64, hq * CH:(hq + 1) * CH],
                        kfm[bi][:, kv * TBA + cl * CH: kv * TBA + (cl + 1) * CH],
                        qb[bi][:, hq * TBA + cl * CH: hq * TBA + (cl + 1) * CH],
                        start=True, stop=True,
                        tile_position=(0, rr))
                return last
            kb.emit("tensor", atmm, reads=[r_kfm[bi], r_qb[bi]],
                    writes=[r_ps[5]])

            def atev(e, cpar=cpar, rr=rr):
                last = None
                for hq in range(QHG):
                    last = e.tensor_mul(ats[rr:rr + 64, hq * CH:(hq + 1) * CH],
                                        ps[5][rr:rr + 64, hq * CH:(hq + 1) * CH],
                                        mask64[rr:rr + 64, :])
                return last
            kb.emit("vector", atev, reads=[r_ps[5], r_consts],
                    writes=[r_ats[cpar]])

            ob = (1 + cpar, 3 + cpar)

            def omm(e, bi=bi, cpar=cpar, tt=tt, rr=rr, ob=ob, cl=cl):
                last = None
                for hq in range(QHG):
                    kv = hq // 2
                    pb = ob[hq // 4]
                    cc = (hq % 4) * 128
                    c0 = tt * 512 + kv * 128
                    e.matmul(ps[pb][rr:rr + 64, cc:cc + 128],
                             ats[rr:rr + 64, hq * CH:(hq + 1) * CH],
                             vb[bi][rr:rr + 64, c0:c0 + 128],
                             start=True, stop=False, skip_group_check=True,
                             tile_position=(rr, rr))
                    last = e.matmul(ps[pb][rr:rr + 64, cc:cc + 128],
                                    qb[bi][:, hq * TBA + cl * CH: hq * TBA + (cl + 1) * CH],
                                    ssnap[cpar][:, kv * 128:(kv + 1) * 128],
                                    start=False, stop=True,
                                    skip_group_check=True,
                                    tile_position=(0, rr))
                return last
            kb.emit("tensor", omm,
                    reads=[r_ats[cpar], r_vb[bi], r_qb[bi], r_ssnap[cpar]],
                    writes=[r_ps[ob[0]], r_ps[ob[1]]])

            def msq(e, cl=cl, ob=ob, rr=rr):
                last = None
                m0 = (cl // 2) * QHG
                for hq in range(QHG):
                    pb = ob[hq // 4]
                    cc = (hq % 4) * 128
                    last = e.activation(
                        sqscr[rr:rr + 64, :], ps[pb][rr:rr + 64, cc:cc + 128],
                        AF.Square,
                        accum_out=ms_b[rr:rr + 64, m0 + hq: m0 + hq + 1])
                return last
            kb.emit("scalar", msq, reads=[r_ps[ob[0]], r_ps[ob[1]]],
                    rmw=[r_ms[cl]])

            m0 = (cl // 2) * QHG
            kb.emit("scalar",
                    lambda e, rr=rr, m0=m0: e.activation(
                        rstd_b[rr:rr + 64, m0:m0 + QHG],
                        ms_b[rr:rr + 64, m0:m0 + QHG], AF.Ln,
                        scale=1.0 / HD, bias=epsc[rr:rr + 64, 0:1]),
                    reads=[r_ms[cl]], writes=[r_rstd[cl]])
            kb.emit("scalar",
                    lambda e, rr=rr, m0=m0: e.activation(
                        rstd_b[rr:rr + 64, m0:m0 + QHG],
                        rstd_b[rr:rr + 64, m0:m0 + QHG], AF.Exp,
                        scale=-0.5),
                    rmw=[r_rstd[cl]])

            opar = (cl // 2) % 2

            def oev(e, cl=cl, ob=ob, opar=opar, rr=rr):
                last = None
                m0 = (cl // 2) * QHG
                for hq in range(QHG):
                    pb = ob[hq // 4]
                    cc = (hq % 4) * 128
                    last = e.tensor_scalar_mul(
                        osb[opar][rr:rr + 64, hq * 128:(hq + 1) * 128],
                        ps[pb][rr:rr + 64, cc:cc + 128],
                        rstd_b[rr:rr + 64, m0 + hq: m0 + hq + 1])
                return last
            kb.emit("vector", oev, reads=[r_ps[ob[0]], r_ps[ob[1]],
                                           r_rstd[cl]],
                    rmw=[r_osb[opar]])
            if cl % 2 == 1:
                for hh in range(2):
                    pb = 5

                    def otmm(e, pb=pb, hh=hh, opar=opar):
                        last = None
                        for k in range(4):
                            hq = hh * 4 + k
                            last = e.matmul(
                                ps[pb][:, k * 128:(k + 1) * 128],
                                osb[opar][:, hq * 128:(hq + 1) * 128],
                                identb[:, :], start=True, stop=True)
                        return last
                    kb.emit("tensor", otmm, reads=[r_osb[opar], r_consts],
                            writes=[r_ps[pb]])

                    def otev(e, pb=pb, hh=hh, opar=opar):
                        last = None
                        for k in range(4):
                            hq = hh * 4 + k
                            last = e.tensor_scalar_mul(
                                oTb[opar][:, hq * 128:(hq + 1) * 128],
                                ps[pb][:, k * 128:(k + 1) * 128],
                                gnw_s[:, 0:1])
                        return last
                    kb.emit("vector", otev, reads=[r_ps[pb], r_vecs],
                            rmw=[r_oTb[opar]])
                t0 = blk * TBA + (cl - 1) * CH
                dst = AP(oT_h, (t0 // 128) * 2048 + g * QHG * 128,
                         [[NH * T, 128], [1, QHG * 128]])
                src = AP(oTb[opar], 0, [[QHG * 128, 128], [1, QHG * 128]])
                dma("sync", dst, src, reads=[r_oTb[opar]], rmw=[r_oT_h],
                    ch=f"oT{opar}")


    for g in range(NG):
        if g > 0:
            emit_gweights(g)
        kb.emit("tensor",
                lambda e: e.matmul(ps[0][:, :], zer512[:, 0:128], zer512[:, :],
                                   start=True, stop=False, skip_group_check=True),
                reads=[r_consts], writes=[r_ps[0]])
        kb.emit("vector", lambda e: e.tensor_copy(ssnap[0][:, :], ps[0][:, :]),
                rmw=[r_ps[0]], writes=[r_ssnap[0]])

        prev = None
        for blk in range(NBA):
            emit_prologue(g, blk)
            if prev is not None:
                emit_scan(g, prev)
            prev = blk
        emit_scan(g, prev)

    kb.barrier()
    sb.release()
    sb.mark()
    wo_sb = sb.alloc([128, 16 * 2048], BF16, "wo_sb")
    natb = [sb.alloc([128, 2048], BF16, f"natb{i}") for i in range(4)]
    stgb = [sb.alloc([128, 2048], BF16, f"stgb{i}") for i in range(4)]
    oTt = [sb.alloc([128, NH * 128], BF16, f"oTt{i}") for i in range(2)]
    sst = [sb.alloc([128, 2048], F32, f"sst{i}") for i in range(2)]
    r_wo = R("wo")
    r_oTt = [R(), R()]
    r_sst = [R(), R()]
    r_s_h = [R(f"s_h{i}") for i in range(T // 128)]

    for q4 in range(4):
        dma("gpsimd",
            AP(wo_sb, q4 * 4 * 2048, [[16 * 2048, 128], [2048, 4], [1, 2048]]),
            AP(din["wo"], q4 * 4 * 128 * DM, [[DM, 128], [128 * DM, 4], [1, 2048]]),
            rmw=[r_wo], ch="wo")
    for tc in range(T // 128):
        for _ in range(12):
            if prep_jobs:
                prep_jobs.pop(0)()
        pi = tc % 2
        dma("sync", AP(oTt[pi], 0, [[NH * 128, 128], [1, NH * 128]]),
            AP(oT_h, tc * 2048, [[NH * T, 128], [1, NH * 128]]),
            reads=[r_oT_h], writes=[r_oTt[pi]], ch=f"oTt{pi}")
        for hb in range(4):
            pb = 6 + (hb % 2)

            def wom(e, pi=pi, hb=hb, pb=pb):
                last = None
                for fc in range(16):
                    last = e.matmul(
                        ps[pb][:, 0:512],
                        oTt[pi][:, fc * 128:(fc + 1) * 128],
                        wo_sb[:, fc * 2048 + hb * 512: fc * 2048 + (hb + 1) * 512],
                        start=(fc == 0), stop=(fc == 15))
                return last
            kb.emit("tensor", wom, reads=[r_oTt[pi], r_wo], writes=[r_ps[pb]])
            kb.emit("vector",
                    lambda e, pi=pi, hb=hb, pb=pb: e.tensor_copy(
                        sst[pi][:, hb * 512:(hb + 1) * 512], ps[pb][:, 0:512]),
                    reads=[r_ps[pb]], rmw=[r_sst[pi]])
        dma("sync", AP(s_h, tc * 128 * DM, [[DM, 128], [1, DM]]), sst[pi][:, :],
            reads=[r_sst[pi]], writes=[r_s_h[tc]], ch=f"sst{pi}")

    while prep_jobs:
        prep_jobs.pop(0)()

    kb.barrier()
    sb.release()
    sb.mark()
    x1 = sb.alloc([128, 16 * TBF], BF16, "x1")
    x2 = sb.alloc([128, 64 * TBF], BF16, "x2")
    wus = [sb.alloc([128, 16 * 256], BF16, f"wus{i}") for i in range(2)]
    wgs = [sb.alloc([128, 16 * 256], BF16, f"wgs{i}") for i in range(2)]
    wds = [sb.alloc([128, 32 * 256], BF16, f"wds{i}") for i in range(2)]
    sgt = [sb.alloc([128, TBF], BF16, f"sgt{i}") for i in range(2)]
    gst = [sb.alloc([128, 2048], BF16, f"gst{i}") for i in range(4)]
    x1b = sb.alloc([128, 16 * TBF], BF16, "x1b")
    sblk = sb.alloc([128, 2048], BF16, "sblk")
    xblk = [sb.alloc([128, 512], F32, f"xblk{i}") for i in range(2)]
    outst = [sb.alloc([128, 512], F32, f"outst{i}") for i in range(2)]
    dsc = sb.alloc([128, 512], F32, "dsc")

    r_x1 = R()
    rbuf = [{"c": [R() for _ in range(16)], "w": R()} for _ in range(2)]
    r_x2 = R()
    r_x2g = [R() for _ in range(64)]
    r_wus = [R(), R()]
    r_wgs = [R(), R()]
    r_wds = [R(), R()]
    r_sgt = [R(), R()]
    r_gst = [R(), R(), R(), R()]
    r_sblk = R()
    r_xblk = [R(), R()]
    r_outst = [R(), R()]
    r_dsc = R()

    def ffn_x1_load(blk):
        t0 = blk * TBF
        TB = blk % 2
        xa = (x1, x1b)[TB]
        ra = rbuf[TB]
        dma("sync", AP(xa, 0, [[16 * TBF, 128], [TBF, 16], [1, TBF]]),
            AP(xnT_h, t0, [[16 * T, 128], [T, 16], [1, TBF]]),
            reads=[r_xnT_h[2 * blk], r_xnT_h[2 * blk + 1]],
            writes=[ra["w"]] + ra["c"], ch=f"x1ld{TB}")

    def ffn_head(blk):
        TB = blk % 2
        xa, xo = (x1, x1b) if TB == 0 else (x1b, x1)
        ra, ro = rbuf[TB], rbuf[1 - TB]

        def sumul(e, xa=xa):
            last = None
            for c in range(16):
                a = xa[:, c * TBF:(c + 1) * TBF]
                last = e.tensor_scalar_mul(a, a, suug_s[:, c:c + 1])
            return last
        kb.emit("vector", sumul, reads=[r_vecs], rmw=ra["c"])
        for cb in range(16):
            pb = 6 + (cb % 2)
            kb.emit("tensor",
                    lambda e, cb=cb, pb=pb, xa=xa: e.matmul(
                        ps[pb][:, 0:TBF], h128b[:, :],
                        xa[:, cb * TBF:(cb + 1) * TBF], start=True, stop=True),
                    reads=[ra["c"][cb], r_consts], writes=[r_ps[pb]])
            kb.emit("vector",
                    lambda e, cb=cb, pb=pb, xa=xa: e.tensor_copy(
                        xa[:, cb * TBF:(cb + 1) * TBF], ps[pb][:, 0:TBF]),
                    reads=[r_ps[pb]], rmw=[ra["c"][cb]])
        bufs = [xa, xo]
        rr_ = [ra, ro]
        for s in range(4):
            h = 1 << s
            src_b, dst_b = bufs[s % 2], bufs[(s + 1) % 2]
            ng = 16 // (2 * h)
            dims = [[16 * TBF, 128], [2 * h * TBF, ng], [TBF, h], [1, TBF]]
            aA = AP(src_b, 0, dims)
            aB = AP(src_b, h * TBF, dims)
            oA = AP(dst_b, 0, dims)
            oB = AP(dst_b, h * TBF, dims)
            rd = ra["c"] if s == 0 else [rr_[s % 2]["w"]]
            wr = [rr_[(s + 1) % 2]["w"]] + (ra["c"] if s == 3 else [])

            def bf1(e, aA=aA, aB=aB, oA=oA, oB=oB):
                e.tensor_add(oA, aA, aB)
                return e.tensor_sub(oB, aA, aB)
            kb.emit("vector", bf1, reads=list(rd), writes=list(wr))

    def ffn_ug(blk):
        TB = blk % 2
        xa = (x1, x1b)[TB]
        ra = rbuf[TB]
        for sup in range(IN // 256):
            pp = sup % 2
            dma("sync", wus[pp][:, :],
                AP(wuT_h, sup * 4096, [[64 * 2048, 128], [1, 4096]]),
                reads=[r_wT["u"]], writes=[r_wus[pp]], ch=f"wus{pp}")
            dma("sync", wgs[pp][:, :],
                AP(wgT_h, sup * 4096, [[64 * 2048, 128], [1, 4096]]),
                reads=[r_wT["g"]], writes=[r_wgs[pp]], ch=f"wgs{pp}")
            for isl in range(2):
                ic = sup * 2 + isl
                pbu = pp * 4 + isl
                pbg = pp * 4 + 2 + isl

                def upm(e, isl=isl, pbu=pbu, pp=pp, xa=xa):
                    last = None
                    for hc in range(16):
                        last = e.matmul(
                            ps[pbu][:, 0:TBF],
                            wus[pp][:, isl * 2048 + hc * 128: isl * 2048 + hc * 128 + 128],
                            xa[:, hc * TBF:(hc + 1) * TBF],
                            start=(hc == 0), stop=(hc == 15))
                    return last
                kb.emit("tensor", upm, reads=[r_wus[pp], ra["w"]],
                        writes=[r_ps[pbu]])

                def gam(e, isl=isl, pbg=pbg, pp=pp, xa=xa):
                    last = None
                    for hc in range(16):
                        last = e.matmul(
                            ps[pbg][:, 0:TBF],
                            wgs[pp][:, isl * 2048 + hc * 128: isl * 2048 + hc * 128 + 128],
                            xa[:, hc * TBF:(hc + 1) * TBF],
                            start=(hc == 0), stop=(hc == 15))
                    return last
                kb.emit("tensor", gam, reads=[r_wgs[pp], ra["w"]],
                        writes=[r_ps[pbg]])
                kb.emit("scalar",
                        lambda e, ic=ic, pbg=pbg, isl=isl: e.activation(
                            sgt[isl][:, :], ps[pbg][:, 0:TBF], AF.Silu,
                            scale=sgate_s[:, ic:ic + 1]),
                        reads=[r_ps[pbg], r_vecs], writes=[r_sgt[isl]])
                dstx2 = AP(x2, ic, [[64 * TBF, 128], [64, TBF]])
                kb.emit("vector",
                        lambda e, ic=ic, pbu=pbu, isl=isl, dstx2=dstx2:
                        e.scalar_tensor_tensor(dstx2, ps[pbu][:, 0:TBF],
                                               sup_s[:, ic:ic + 1],
                                               sgt[isl][:, :],
                                               ALU.mult, ALU.mult),
                        reads=[r_ps[pbu], r_sgt[isl], r_vecs], rmw=[r_x2])

    def ffn_sandwich(blk):
        for hmat in (h128b, bh64b):
            for g4 in range(64):
                pb = g4 % 2

                def fmm2(e, g4=g4, pb=pb, hmat=hmat):
                    last = None
                    for k in range(4):
                        j = g4 * 4 + k
                        last = e.matmul(ps[pb][:, k * 128:(k + 1) * 128],
                                        x2[:, j * 128:(j + 1) * 128],
                                        hmat[:, :], start=True, stop=True)
                    return last
                kb.emit("tensor", fmm2, reads=[r_x2, r_x2g[g4], r_consts],
                        writes=[r_ps[pb]])
                kb.emit("vector",
                        lambda e, g4=g4, pb=pb: e.tensor_copy(
                            x2[:, g4 * 512:(g4 + 1) * 512], ps[pb][:, 0:512]),
                        reads=[r_ps[pb]], rmw=[r_x2g[g4]])

    def ffn_down(blk):
        for hb in range(8):
            for piece in range(2):
                wpp = (hb * 2 + piece) % 2
                dma("sync",
                    AP(wds[wpp], 0, [[32 * 256, 128], [256, 32], [1, 256]]),
                    AP(wdT_h, piece * 32 * DM + hb * 256,
                       [[64 * DM, 128], [DM, 32], [1, 256]]),
                    reads=[r_wT["d"]], writes=[r_wds[wpp]], ch=f"wds{wpp}")
            for tc4 in range(TBF // 128):
                pb = 2 + ((hb * 4 + tc4) % 4)

                def dwm(e, hb=hb, tc4=tc4, pb=pb):
                    last = None
                    for piece in range(2):
                        wpp = (hb * 2 + piece) % 2
                        for icl in range(32):
                            ic = piece * 32 + icl
                            lhs = AP(x2, tc4 * 128 * 64 + ic,
                                     [[64 * TBF, 128], [64, 128]])
                            last = e.matmul(
                                ps[pb][:, 0:256], lhs,
                                AP(wds[wpp], icl * 256, [[32 * 256, 128], [1, 256]]),
                                start=(ic == 0), stop=(ic == 63))
                    return last
                kb.emit("tensor", dwm,
                        reads=[r_x2, r_wds[0], r_wds[1]]
                        + r_x2g[tc4 * 16:(tc4 + 1) * 16],
                        writes=[r_ps[pb]])
                kb.emit("vector",
                        lambda e, hb=hb, pb=pb: e.scalar_tensor_tensor(
                            dsc[:, 0:256], ps[pb][:, 0:256], 1.0,
                            wsdb[:, hb * 256:(hb + 1) * 256], ALU.mult, ALU.mult),
                        reads=[r_ps[pb], r_wsdb], writes=[r_dsc])
                kb.emit("scalar",
                        lambda e, hb=hb, tc4=tc4: e.activation(
                            gst[tc4][:, hb * 256:(hb + 1) * 256], dsc[:, 0:256],
                            AF.Silu),
                        reads=[r_dsc], rmw=[r_gst[tc4]])
        for tc4 in range(TBF // 128):
            tc = blk * (TBF // 128) + tc4
            dma("gpsimd", sblk[:, :], AP(s_h, tc * 128 * DM, [[DM, 128], [1, DM]]),
                reads=[r_s_h[tc]], writes=[r_sblk], ch="sblk")
            if fulldbg:
                dma("sync", AP(g_d, tc * 128 * DM, [[DM, 128], [1, DM]]),
                    gst[tc4][:, :], reads=[r_gst[tc4]], ch="dbg")
            for h4 in range(4):
                po = h4 % 2
                dma("gpsimd", xblk[po][:, :],
                    AP(din["x"], tc * 128 * DM + h4 * 512, [[DM, 128], [1, 512]]),
                    writes=[r_xblk[po]], ch=f"xblk{po}")
                kb.emit("vector",
                        lambda e, tc4=tc4, h4=h4, po=po: e.tensor_mul(
                            outst[po][:, :], gst[tc4][:, h4 * 512:(h4 + 1) * 512],
                            sblk[:, h4 * 512:(h4 + 1) * 512]),
                        reads=[r_gst[tc4], r_sblk], writes=[r_outst[po]])
                kb.emit("vector",
                        lambda e, h4=h4, po=po: e.tensor_add(
                            outst[po][:, :], outst[po][:, :], xblk[po][:, :]),
                        reads=[r_xblk[po]], rmw=[r_outst[po]])
                dma("gpsimd", AP(y, tc * 128 * DM + h4 * 512, [[DM, 128], [1, 512]]),
                    outst[po][:, :], reads=[r_outst[po]], ch=f"y{po}")

    for blk in range(NBF):
        if blk == 0:
            ffn_x1_load(0)
            ffn_head(0)
        if blk + 1 < NBF:
            ffn_x1_load(blk + 1)
        ffn_ug(blk)
        ffn_sandwich(blk)
        if blk + 1 < NBF:
            ffn_head(blk + 1)
        ffn_down(blk)

    with nc.Block() as block:
        kb.emit_all(nc, block)
    return nc


_NC_CACHE = None


def kernel(**inputs):
    global _NC_CACHE
    if _NC_CACHE is None:
        _NC_CACHE = build_program()
    nc = _NC_CACHE
    x = np.ascontiguousarray(inputs["L0_attn_hidden_states"], dtype=np.float32)
    base = {
        "wq": np.ascontiguousarray(inputs["L0_attn_Wq"], dtype=np.float32),
        "bq": np.ascontiguousarray(inputs["L0_attn_bq"], dtype=np.float32).reshape(1, -1),
        "wk": np.ascontiguousarray(inputs["L0_attn_Wk"], dtype=np.float32),
        "bk": np.ascontiguousarray(inputs["L0_attn_bk"], dtype=np.float32).reshape(1, -1),
        "wv": np.ascontiguousarray(inputs["L0_attn_Wv"], dtype=np.float32),
        "bv": np.ascontiguousarray(inputs["L0_attn_bv"], dtype=np.float32).reshape(1, -1),
        "wg1": np.ascontiguousarray(inputs["L0_attn_Wg1"], dtype=np.float32),
        "wg2": np.ascontiguousarray(inputs["L0_attn_Wg2"], dtype=np.float32),
        "bg2": np.ascontiguousarray(inputs["L0_attn_bg2"], dtype=np.float32).reshape(1, -1),
        "gnw": np.ascontiguousarray(inputs["L0_attn_g_norm_w"], dtype=np.float32).reshape(1, -1),
        "wo": np.ascontiguousarray(inputs["L0_attn_Wo"], dtype=np.float32),
        "su_ug": np.ascontiguousarray(inputs["L0_ffn_SU_ug"], dtype=np.float32).reshape(1, -1),
        "su_dp": np.ascontiguousarray(inputs["L0_ffn_SU_dp"], dtype=np.float32).reshape(1, -1),
        "wsug": np.ascontiguousarray(inputs["L0_ffn_Wscale_ug"], dtype=np.float32).reshape(1, -1),
        "wsdp": np.ascontiguousarray(inputs["L0_ffn_Wscale_dp"], dtype=np.float32).reshape(1, -1),
        "wup": np.ascontiguousarray(inputs["L0_ffn_W_up"], dtype=np.float32),
        "wgate": np.ascontiguousarray(inputs["L0_ffn_W_gate"], dtype=np.float32),
        "wdown": np.ascontiguousarray(inputs["L0_ffn_W_down"], dtype=np.float32),
    }
    in_maps = []
    for c in range(8):
        m = dict(base)
        m["x"] = np.ascontiguousarray(x[c])
        in_maps.append(m)
    res = run_bass_kernel_spmd(nc, in_maps, core_ids=list(range(8)))
    return np.stack([np.asarray(r["y"], dtype=np.float32) for r in res.results], axis=0)


# version 24
# speedup vs baseline: 1.0850x; 1.0850x over previous
import numpy as np
import concourse.bass as bass
import concourse.mybir as mybir
from concourse.bass_utils import run_bass_kernel_spmd

F32 = mybir.dt.float32
BF16 = mybir.dt.bfloat16
AF = mybir.ActivationFunctionType
ALU = mybir.AluOpType
AX = mybir.AxisListType
AP = bass.AP

T = 2048
DM = 2048
HKV = 8
NH = 16
HD = 128
CH = 64
IN = 8192
NG = 2
KVG = HKV // NG
QHG = NH // NG
TBA = 256
NBA = T // TBA
CPB = TBA // CH
TBF = 512
NBF = T // TBF
RS128 = float(HD) ** -0.5
EPS = 1e-5

ENGS = ("tensor", "vector", "scalar", "gpsimd", "sync")


def _hadamard(n):
    i = np.arange(n)
    a = i[:, None] & i[None, :]
    pc = np.zeros((n, n), dtype=np.int64)
    for b in range(14):
        pc += (a >> b) & 1
    return np.where(pc % 2 == 0, 1.0, -1.0).astype(np.float32)


H128_NP = _hadamard(128)
BH16_NP = np.kron(np.eye(8, dtype=np.float32), _hadamard(16))
BH64_NP = np.kron(np.eye(2, dtype=np.float32), _hadamard(64))
IDENT_NP = np.eye(128, dtype=np.float32)
IDF_NP = np.eye(64, dtype=np.float32)
MASK64_NP = np.tile((np.arange(64)[None, :] >= np.arange(64)[:, None]).astype(np.float32), (2, 1))
ONES_NP = np.ones((1, 128), dtype=np.float32)


class Step:
    __slots__ = ("eng", "fn", "dma", "cval", "gid", "deps", "bg", "ch")


class Res:
    __slots__ = ("writer", "readers", "name")

    def __init__(self, name=""):
        self.writer = None
        self.readers = []
        self.name = name


class KB:
    def __init__(self):
        self.steps = []
        self.cnt = {e: 0 for e in ENGS}
        self.dcnt = {}
        self.frontier_deps = []
        self.frontier_pending = set()

    def emit(self, eng, fn, reads=(), writes=(), rmw=(), deps=(), dma=False,
             bg=False, ch=None):
        d = []
        for r in reads:
            if r.writer is not None:
                d.append(r.writer)
        for w in tuple(writes) + tuple(rmw):
            if w.writer is not None:
                d.append(w.writer)
            d.extend(w.readers)
        d.extend(deps)
        if eng in self.frontier_pending:
            d.extend(self.frontier_deps)
            self.frontier_pending.discard(eng)
        st = Step()
        st.eng = eng
        st.fn = fn
        st.dma = dma
        st.bg = bg
        st.gid = len(self.steps)
        if dma:
            st.ch = ch if ch is not None else f"_q_{eng}"
            self.dcnt[st.ch] = self.dcnt.get(st.ch, 0) + 1
            st.cval = 16 * self.dcnt[st.ch]
        else:
            st.ch = None
            self.cnt[eng] += 1
            st.cval = self.cnt[eng]
        for r in reads:
            r.readers.append(st)
        for w in tuple(writes) + tuple(rmw):
            w.writer = st
            w.readers = []
        st.deps = []
        seen = set()
        for s in d:
            if s is None or s.gid in seen or s is st:
                continue
            seen.add(s.gid)
            assert s.gid < st.gid
            st.deps.append(s)
        self.steps.append(st)
        return st

    def barrier(self):
        last = {}
        for st in self.steps:
            if st.bg:
                continue
            last[(st.eng, st.dma)] = st
        self.frontier_deps = list(last.values())
        self.frontier_pending = set(ENGS)

    def emit_all(self, nc, block):
        sems = {}
        for e in ENGS:
            if self.cnt[e]:
                sems[("c", e)] = nc.semaphore(f"sc_{e}").__enter__()
        for chn in self.dcnt:
            sems[("d", chn)] = nc.semaphore(f"sd_{chn}").__enter__()
        for h in sems.values():
            nc.gpsimd.sem_clear(h)
        nc.all_engine_barrier()
        decos = {"tensor": block.tensor, "vector": block.vector,
                 "scalar": block.scalar, "gpsimd": block.gpsimd,
                 "sync": block.sync}
        total_dma = {chn: 16 * n for chn, n in self.dcnt.items()}

        def mk(ename):
            def body(e):
                water = {}
                for st in self.steps:
                    if st.eng != ename:
                        continue
                    for dep in st.deps:
                        if (dep.eng == ename == "tensor"
                                and not dep.dma and not st.dma):
                            continue
                        key = ("d", dep.ch) if dep.dma else ("c", dep.eng)
                        if water.get(key, 0) >= dep.cval:
                            continue
                        water[key] = dep.cval
                        e.wait_ge(sems[key], dep.cval)
                    last = st.fn(e)
                    if st.dma:
                        last.then_inc(sems[("d", st.ch)], 16)
                    else:
                        last.then_inc(sems[("c", ename)], 1)
                if ename == "sync":
                    for chn, tot in total_dma.items():
                        e.wait_ge(sems[("d", chn)], tot)
            return body

        for ename in ENGS:
            decos[ename](mk(ename))


class Sbuf:
    BASE = 16544
    TOP = 229340

    def __init__(self, nc):
        self.nc = nc
        self.off = self.BASE
        self.marks = []
        self.peak = self.BASE
        self.n = 0

    def alloc(self, shape, dtype, name=None):
        esz = 4 if dtype == F32 else 2
        nb = shape[1] * esz
        off = (self.off + 31) // 32 * 32
        assert off + nb <= self.TOP, f"SBUF overflow: {name} ({nb}B at {off})"
        self.off = off + nb
        self.peak = max(self.peak, self.off)
        self.n += 1
        self.offsets = getattr(self, "offsets", {})
        self.offsets[name or f"sb{self.n}"] = off
        return self.nc.alloc_sbuf_tensor_at(name or f"sb{self.n}", list(shape),
                                            dtype, offset=off)

    def mark(self):
        self.marks.append(self.off)

    def release(self):
        self.off = self.marks.pop()


def build_program(debug=False):
    nc = bass.Bass("TRN2", target_bir_lowering=False)
    kb = KB()

    din = {}
    for name, shape in [
        ("x", [T, DM]), ("wq", [DM, DM]), ("bq", [1, DM]), ("wk", [DM, 1024]),
        ("bk", [1, 1024]), ("wv", [DM, 1024]), ("bv", [1, 1024]),
        ("wg1", [DM, 16]), ("wg2", [16, 1024]), ("bg2", [1, 1024]),
        ("gnw", [1, HD]), ("wo", [DM, DM]), ("su_ug", [1, DM]), ("su_dp", [1, IN]),
        ("wsug", [1, 2 * IN]), ("wsdp", [1, DM]), ("wup", [IN, DM]),
        ("wgate", [IN, DM]), ("wdown", [DM, IN]),
    ]:
        din[name] = nc.dram_tensor(name, shape, F32, kind="ExternalInput")
    y = nc.dram_tensor("y", [T, DM], F32, kind="ExternalOutput")

    c_h128 = nc.inline_tensor(H128_NP, name="c_h128")
    c_bh16 = nc.inline_tensor(BH16_NP, name="c_bh16")
    c_bh64 = nc.inline_tensor(BH64_NP, name="c_bh64")
    c_ident = nc.inline_tensor(IDENT_NP, name="c_ident")
    c_idf = nc.inline_tensor(IDF_NP, name="c_idf")
    c_mask = nc.inline_tensor(MASK64_NP, name="c_mask")
    c_ones = nc.inline_tensor(ONES_NP, name="c_ones")

    dkind = "ExternalOutput" if debug else "Internal"
    fulldbg = (debug == True)
    xnT_h = nc.dram_tensor("xnT_h", [128, 16 * T], BF16, kind=dkind)
    oT_h = nc.dram_tensor("oT_h", [128, NH * T], BF16, kind=dkind)
    s_h = nc.dram_tensor("s_h", [T, DM], F32, kind=dkind)
    if fulldbg:
        x1_d = nc.dram_tensor("x1_d", [128, 16 * T], BF16, kind="ExternalOutput")
        g_d = nc.dram_tensor("g_d", [T, DM], BF16, kind="ExternalOutput")
    if debug == 3:
        xt_d = nc.dram_tensor("xt_d", [128, DM], F32, kind="ExternalOutput")
        colt_d = nc.dram_tensor("colt_d", [128, 8], F32, kind="ExternalOutput")
        xn_d = nc.dram_tensor("xn_d", [128, DM], BF16, kind="ExternalOutput")
    wu_nat = nc.dram_tensor("wu_nat", [IN, DM], BF16, kind="Internal")
    wg_nat = nc.dram_tensor("wg_nat", [IN, DM], BF16, kind="Internal")
    wd_nat = nc.dram_tensor("wd_nat", [DM, IN], BF16, kind="Internal")
    wuT_h = nc.dram_tensor("wuT_h", [128, 64 * 2048], BF16, kind="Internal")
    wgT_h = nc.dram_tensor("wgT_h", [128, 64 * 2048], BF16, kind="Internal")
    wdT_h = nc.dram_tensor("wdT_h", [128, 64 * DM], BF16, kind="Internal")

    sb = Sbuf(nc)
    h128b = sb.alloc([128, 128], BF16, "h128b")
    bh16b = sb.alloc([128, 128], BF16, "bh16b")
    bh64b = sb.alloc([128, 128], BF16, "bh64b")
    identb = sb.alloc([128, 128], BF16, "identb")
    identf = sb.alloc([64, 64], F32, "identf")
    mask64 = sb.alloc([128, 64], F32, "mask64")
    onesf = sb.alloc([1, 128], F32, "onesf")
    zer512 = sb.alloc([128, 512], BF16, "zer512")
    scanmask = sb.alloc([128, TBA], F32, "scanmask")
    bq_s = sb.alloc([128, 16], F32, "bq_s")
    bk_s = sb.alloc([128, 8], F32, "bk_s")
    bg2_s = sb.alloc([128, 8], F32, "bg2_s")
    gnw_s = sb.alloc([128, 1], F32, "gnw_s")
    suug_s = sb.alloc([128, 16], F32, "suug_s")
    sup_s = sb.alloc([128, 64], F32, "sup_s")
    sgate_s = sb.alloc([128, 64], F32, "sgate_s")
    wsdb = sb.alloc([128, 2048], F32, "wsdb")
    ms_b = sb.alloc([128, QHG * CPB // 2], F32, "ms_b")
    rstd_b = sb.alloc([128, QHG * CPB // 2], F32, "rstd_b")
    colt_t = [sb.alloc([128, 8], F32, f"colt{i}") for i in range(2)]
    epsc = sb.alloc([128, 1], F32, "epsc")
    sqscr = sb.alloc([128, 128], BF16, "sqscr")

    R = Res
    r_consts = R("consts")
    r_vecs = R("vecs")
    r_vbias = R("vbias")
    r_wsdb = R("wsdb")

    ps = [nc.alloc_psum_tensor(f"ps{i}", [128, 512], F32) for i in range(8)]
    r_ps = [R(f"ps{i}") for i in range(8)]

    def dma(q, dst, src, reads=(), writes=(), rmw=(), deps=(), slow=False,
            bg=False, ch=None):
        if slow:
            fn = lambda e: e.dma_start(dst, src, allow_slow_non_contiguous=True)
        else:
            fn = lambda e: e.dma_start(dst, src)
        return kb.emit(q, fn, reads=reads, writes=writes, rmw=rmw, deps=deps,
                       dma=True, bg=bg, ch=ch)

    for dst, src in [(h128b, c_h128), (bh16b, c_bh16), (bh64b, c_bh64),
                     (identb, c_ident)]:
        dma("gpsimd", dst[:, :], src.ap(), rmw=[r_consts], ch="consts")
    dma("gpsimd", identf[:, :], c_idf.ap(), rmw=[r_consts], ch="consts")
    dma("gpsimd", mask64[:, :], c_mask.ap(), rmw=[r_consts], ch="consts")
    dma("gpsimd", onesf[:, :], c_ones.ap(), rmw=[r_consts], ch="consts")
    kb.emit("vector", lambda e: e.memset(zer512[:, :], 0.0), rmw=[r_consts])
    kb.emit("vector", lambda e: e.memset(epsc[:, :], EPS), rmw=[r_consts])
    kb.emit("vector", lambda e: e.memset(scanmask[:, :], 1.0), rmw=[r_consts])
    kb.emit("vector",
            lambda e: e.memset(AP(scanmask, 0, [[TBA, 128], [CH, CPB]]), 0.0),
            rmw=[r_consts])

    sb.mark()
    vrow = sb.alloc([64, 128], F32, "vrow")
    bvrow = sb.alloc([1, 1024], F32, "bvrow")
    wsdrow = sb.alloc([1, 2048], F32, "wsdrow")

    sb.release()
    sb.mark()
    xt = [sb.alloc([128, DM], F32, f"xt{i}") for i in range(2)]
    xsq = sb.alloc([128, DM], F32, "xsq")
    xnb16 = sb.alloc([128, DM], BF16, "xnb16")
    xnTblk = [sb.alloc([128, 16 * TBA], BF16, f"xnTblk{i}") for i in range(2)]

    r_xt = [R(), R()]
    r_xsq = R()
    r_xn = R()
    r_xnbuf = [R(), R()]
    r_xnT_h = [R(f"xnTh{b}") for b in range(NBA)]
    r_colt = [R(), R()]
    r_wtp = [R(), R()]
    r_natb = [R() for _ in range(4)]
    r_stgb = [R() for _ in range(4)]
    r_wT = {m: R(f"wT_{m}") for m in ("u", "g", "d")}

    vbias = sb.alloc([128, 1024], F32, "vbias")
    wq_sb = sb.alloc([128, 16 * 1024], BF16, "wq_sb")
    wk_sb = sb.alloc([128, 16 * 512], BF16, "wk_sb")
    wv_sb = sb.alloc([128, 16 * 512], BF16, "wv_sb")
    wg1_sb = sb.alloc([128, 16 * 16], BF16, "wg1_sb")
    wg2_sb = sb.alloc([16, 512], BF16, "wg2_sb")
    qb = [sb.alloc([128, QHG * TBA], BF16, f"qb{i}") for i in range(2)]
    kraw = [sb.alloc([128, KVG * TBA], BF16, f"kraw{i}") for i in range(2)]
    kfm = [sb.alloc([128, KVG * TBA], BF16, f"kfm{i}") for i in range(2)]
    kpm = [sb.alloc([128, KVG * TBA], BF16, f"kpm{i}") for i in range(2)]
    vb = [sb.alloc([128, KVG * TBA], BF16, f"vb{i}") for i in range(2)]
    gt1 = sb.alloc([128, KVG * TBA], F32, "gt1")
    gt2 = sb.alloc([128, KVG * TBA], F32, "gt2")
    z1sb = sb.alloc([16, TBA], BF16, "z1sb")
    bbuf = sb.alloc([128, KVG * TBA], F32, "bbuf")
    expb = [sb.alloc([128, KVG * TBA], F32, f"expb{i}") for i in range(2)]
    expmb = sb.alloc([128, KVG * TBA], F32, "expmb")
    qs = [sb.alloc([128, QHG * CH], BF16, f"qs{i}") for i in range(2)]
    ssnap = [sb.alloc([128, KVG * 128], BF16, f"ssnap{i}") for i in range(2)]
    ats = sb.alloc([128, QHG * CH], BF16, "ats")
    osb = [sb.alloc([128, QHG * 128], BF16, f"osb{i}") for i in range(2)]
    oTb = [sb.alloc([128, QHG * 128], BF16, f"oTb{i}") for i in range(2)]

    r_wgrp = R("wgrp")
    r_qb = [R(), R()]
    r_kraw = [R(), R()]
    r_kfm = [R(), R()]
    r_kpm = [R(), R()]
    r_vb = [R(), R()]
    r_gt1 = R()
    r_gt2 = R()
    r_z1 = R()
    r_b = R()
    r_expb = [R(), R()]
    r_expmb = R()
    r_qs = [R(), R()]
    r_ssnap = [R(), R()]
    r_ats = [R(), R()]
    r_osb = [R(), R()]
    r_oTb = [R(), R()]
    r_ms = [R() for _ in range(4)]
    r_rstd = [R() for _ in range(4)]
    r_oT_h = R("oT_h")

    def vecload(dst_ap, src_t, src_off, n, scale=None, mult_ps=False):
        rows = n // 128
        dma("gpsimd", vrow[0:rows, :], AP(src_t, src_off, [[128, rows], [1, 128]]),
            reads=[r_consts], rmw=[r_vecs], ch="vec")
        kb.emit("tensor",
                lambda e, rows=rows: e.matmul(ps[6][:, 0:rows], vrow[0:rows, :],
                                              identf[0:rows, 0:rows],
                                              start=True, stop=True),
                reads=[r_vecs, r_consts], writes=[r_ps[6]])
        if mult_ps:
            fn = lambda e, rows=rows: e.tensor_mul(dst_ap, dst_ap, ps[6][:, 0:rows])
        elif scale is not None:
            fn = lambda e, rows=rows, s=float(scale): e.tensor_scalar_mul(
                dst_ap, ps[6][:, 0:rows], s)
        else:
            fn = lambda e, rows=rows: e.tensor_copy(dst_ap, ps[6][:, 0:rows])
        kb.emit("vector", fn, reads=[r_ps[6]], rmw=[r_vecs])

    c2048 = 1.0 / np.sqrt(2048.0)
    vecload(bq_s[:, :], din["bq"], 0, DM, scale=RS128)
    vecload(bk_s[:, :], din["bk"], 0, 1024)
    vecload(bg2_s[:, :], din["bg2"], 0, 1024)
    vecload(gnw_s[:, :], din["gnw"], 0, HD)
    vecload(suug_s[:, :], din["su_ug"], 0, DM)
    vecload(sup_s[:, :], din["wsug"], 0, IN, scale=c2048)
    vecload(sup_s[:, :], din["su_dp"], 0, IN, mult_ps=True)
    vecload(sgate_s[:, :], din["wsug"], IN, IN, scale=c2048)

    dma("gpsimd", bvrow[:, :], din["bv"].ap(), rmw=[r_vbias], ch="bvrow")
    dma("gpsimd", wsdrow[:, :], din["wsdp"].ap(), rmw=[r_wsdb], ch="wsdrow")
    c8192 = 1.0 / np.sqrt(8192.0)
    for i in range(2):
        kb.emit("tensor",
                lambda e, i=i: e.matmul(ps[7][:, 0:512], onesf[:, :],
                                        bvrow[:, i * 512:(i + 1) * 512],
                                        start=True, stop=True),
                reads=[r_vbias], writes=[r_ps[7]])
        kb.emit("vector",
                lambda e, i=i: e.tensor_copy(vbias[:, i * 512:(i + 1) * 512],
                                             ps[7][:, 0:512]),
                reads=[r_ps[7]], rmw=[r_vbias])
    for i in range(4):
        kb.emit("tensor",
                lambda e, i=i: e.matmul(ps[7][:, 0:512], onesf[:, :],
                                        wsdrow[:, i * 512:(i + 1) * 512],
                                        start=True, stop=True),
                reads=[r_wsdb], writes=[r_ps[7]])
        kb.emit("vector",
                lambda e, i=i: e.tensor_scalar_mul(wsdb[:, i * 512:(i + 1) * 512],
                                                   ps[7][:, 0:512], c8192),
                reads=[r_ps[7]], rmw=[r_wsdb])

    def emit_gweights(g):
        for q4 in range(4):
            dma("gpsimd",
                AP(wq_sb, q4 * 4 * 1024, [[16 * 1024, 128], [1024, 4], [1, 1024]]),
                AP(din["wq"], q4 * 4 * 128 * DM + g * 1024,
                   [[DM, 128], [128 * DM, 4], [1, 1024]]),
                rmw=[r_wgrp], ch="gw")
            dma("gpsimd",
                AP(wk_sb, q4 * 4 * 512, [[16 * 512, 128], [512, 4], [1, 512]]),
                AP(din["wk"], q4 * 4 * 128 * 1024 + g * 512,
                   [[1024, 128], [128 * 1024, 4], [1, 512]]),
                rmw=[r_wgrp], ch="gw")
            dma("gpsimd",
                AP(wv_sb, q4 * 4 * 512, [[16 * 512, 128], [512, 4], [1, 512]]),
                AP(din["wv"], q4 * 4 * 128 * 1024 + g * 512,
                   [[1024, 128], [128 * 1024, 4], [1, 512]]),
                rmw=[r_wgrp], ch="gw")
        dma("gpsimd", AP(wg1_sb, 0, [[256, 128], [16, 16], [1, 16]]),
            AP(din["wg1"], 0, [[16, 128], [128 * 16, 16], [1, 16]]),
            rmw=[r_wgrp], ch="gw")
        dma("gpsimd", wg2_sb[:, :],
            AP(din["wg2"], g * 512, [[1024, 16], [1, 512]]),
            rmw=[r_wgrp], ch="gw")

    emit_gweights(0)

    r_wnat = {m: R(f"wnat_{m}") for m in ("u", "g", "d")}
    for m, src, dst in (("d", din["wdown"], wd_nat),):
        rows, cols = src.shape
        nsp = 8
        rr = rows // nsp
        for r8 in range(nsp):
            dma("gpsimd", AP(dst, r8 * rr * cols, [[cols, rr], [1, cols]]),
                AP(src, r8 * rr * cols, [[cols, rr], [1, cols]]),
                rmw=[r_wnat[m]], bg=True, ch="wnat_" + m)

    kb.barrier()

    for i in range(T // 128):
        pi = i % 2
        blk = i // 2
        sub = i % 2
        bi = blk % 2
        dma("sync", xt[pi][:, :], AP(din["x"], i * 128 * DM, [[DM, 128], [1, DM]]),
            writes=[r_xt[pi]], ch=f"xt{pi}")
        kb.emit("vector",
                lambda e, pi=pi: e.reduce_sum(colt_t[pi][:, 0:1], xt[pi][:, :], axis=AX.X),
                reads=[r_xt[pi]], rmw=[r_colt[pi]])
        kb.emit("vector",
                lambda e, pi=pi: e.scalar_tensor_tensor(
                    xsq[:, :], xt[pi][:, :], 1.0, xt[pi][:, :], ALU.mult, ALU.mult,
                    accum_out=colt_t[pi][:, 1:2]),
                reads=[r_xt[pi]], writes=[r_xsq], rmw=[r_colt[pi]])

        def stats1(e, pi=pi):
            e.tensor_scalar_mul(colt_t[pi][:, 2:3], colt_t[pi][:, 0:1], 1.0 / DM)
            return e.tensor_scalar_mul(colt_t[pi][:, 3:4], colt_t[pi][:, 1:2], 1.0 / DM)
        kb.emit("vector", stats1, rmw=[r_colt[pi]])
        kb.emit("vector",
                lambda e, pi=pi: e.scalar_tensor_tensor(
                    colt_t[pi][:, 4:5], colt_t[pi][:, 2:3], -1.0,
                    colt_t[pi][:, 2:3], ALU.mult, ALU.mult),
                rmw=[r_colt[pi]])
        kb.emit("vector",
                lambda e, pi=pi: e.tensor_add(colt_t[pi][:, 5:6],
                                              colt_t[pi][:, 3:4],
                                              colt_t[pi][:, 4:5]),
                rmw=[r_colt[pi]])
        kb.emit("scalar",
                lambda e, pi=pi: e.activation(colt_t[pi][:, 6:7],
                                              colt_t[pi][:, 5:6], AF.Ln,
                                              bias=epsc[:, 0:1]),
                rmw=[r_colt[pi]])
        kb.emit("scalar",
                lambda e, pi=pi: e.activation(colt_t[pi][:, 6:7],
                                              colt_t[pi][:, 6:7], AF.Exp,
                                              scale=-0.5),
                rmw=[r_colt[pi]])
        kb.emit("vector",
                lambda e, pi=pi: e.scalar_tensor_tensor(
                    colt_t[pi][:, 7:8], colt_t[pi][:, 2:3], -1.0,
                    colt_t[pi][:, 6:7], ALU.mult, ALU.mult),
                rmw=[r_colt[pi]])
        kb.emit("vector",
                lambda e, pi=pi: e.tensor_scalar(xnb16[:, :], xt[pi][:, :],
                                                 colt_t[pi][:, 6:7], colt_t[pi][:, 7:8],
                                                 ALU.mult, ALU.add),
                reads=[r_xt[pi], r_colt[pi]], writes=[r_xn])
        if debug == 3 and i == 0:
            dma("gpsimd", xt_d.ap(), xt[0][:, :], reads=[r_xt[0]], ch="dbg2")
            dma("gpsimd", colt_d.ap(), colt_t[pi][:, :], reads=[r_colt[pi]], ch="dbg2")
            dma("gpsimd", xn_d.ap(), xnb16[:, :], reads=[r_xn], ch="dbg2")
        for g4 in range(4):
            pb = 6 + (g4 % 2)

            def tmm(e, g4=g4, pb=pb):
                last = None
                for k in range(4):
                    c = g4 * 4 + k
                    last = e.matmul(ps[pb][:, k * 128:(k + 1) * 128],
                                    xnb16[:, c * 128:(c + 1) * 128],
                                    identb[:, :], start=True, stop=True)
                return last
            kb.emit("tensor", tmm, reads=[r_xn, r_consts], writes=[r_ps[pb]])
            dst = AP(xnTblk[bi], g4 * 4 * TBA + sub * 128,
                     [[16 * TBA, 128], [TBA, 4], [1, 128]])
            src = AP(ps[pb], 0, [[512, 128], [128, 4], [1, 128]])
            kb.emit("vector", lambda e, dst=dst, src=src: e.tensor_copy(dst, src),
                    reads=[r_ps[pb]], rmw=[r_xnbuf[bi]])
        if sub == 1:
            dma("sync", AP(xnT_h, blk * TBA, [[16 * T, 128], [T, 16], [1, TBA]]),
                AP(xnTblk[bi], 0, [[16 * TBA, 128], [TBA, 16], [1, TBA]]),
                reads=[r_xnbuf[bi]], writes=[r_xnT_h[blk]], ch=f"xnst{bi}")

    prep_jobs = []
    for m, nat, dstT in (("u", din["wup"], wuT_h), ("g", din["wgate"], wgT_h)):
        for ic in range(64):
            def job(m=m, nat=nat, dstT=dstT, ic=ic):
                pp = ic % 4
                dma("gpsimd", natb[pp][:, :],
                    AP(nat, ic * 128 * DM, [[DM, 128], [1, DM]]),
                    writes=[r_natb[pp]], ch=f"natb{pp}")
                for g4 in range(4):
                    pb = 1 + (g4 % 2)

                    def tmms(e, pp=pp, g4=g4, pb=pb):
                        last = None
                        for k in range(4):
                            hc = g4 * 4 + k
                            last = e.matmul(ps[pb][:, k * 128:(k + 1) * 128],
                                            natb[pp][:, hc * 128:(hc + 1) * 128],
                                            identb[:, :], start=True, stop=True)
                        return last
                    kb.emit("tensor", tmms, reads=[r_natb[pp], r_consts],
                            writes=[r_ps[pb]])
                    kb.emit("vector",
                            lambda e, pp=pp, g4=g4, pb=pb: e.tensor_copy(
                                stgb[pp][:, g4 * 512:(g4 + 1) * 512],
                                ps[pb][:, 0:512]),
                            reads=[r_ps[pb]], rmw=[r_stgb[pp]])
                dma("sync", AP(dstT, ic * 2048, [[64 * 2048, 128], [1, 2048]]),
                    stgb[pp][:, :], reads=[r_stgb[pp]], rmw=[r_wT[m]],
                    ch=f"stg{pp}")
            prep_jobs.append(job)
    trip_jobs = []
    for ic in range(64):
        pi = ic % 2
        tsrc = AP(wd_nat, ic * 128, [[IN, DM], [1, 128]])
        tdst = AP(wdT_h, ic * 512, [[64 * DM, 128], [64 * 512, 4], [1, 512]])

        def job(pi=pi, tsrc=tsrc, tdst=tdst):
            kb.emit("scalar",
                    lambda e: e.dma_start(wtp[pi][:, :], tsrc, transpose=True),
                    reads=[r_wnat["d"]], writes=[r_wtp[pi]], dma=True,
                    ch=f"wtp{pi}")
            dma("scalar", tdst,
                AP(wtp[pi], 0, [[2048, 128], [512, 4], [1, 512]]),
                reads=[r_wtp[pi]], rmw=[r_wT["d"]], ch=f"tripst{pi}")
        trip_jobs.append(job)


    def emit_prologue(g, blk):
        bi = blk % 2
        xnb = xnTblk[bi]
        bi = blk % 2
        xnb = xnTblk[bi]
        dma("sync", AP(xnTblk[bi], 0, [[16 * TBA, 128], [TBA, 16], [1, TBA]]),
            AP(xnT_h, blk * TBA, [[16 * T, 128], [T, 16], [1, TBA]]),
            reads=[r_xnT_h[blk]], writes=[r_xnbuf[bi]], ch=f"xnld{bi}")

        pjc = [0]

        def projmm(mm_fn, ev_eng, ev_fn, wr, extra_reads=(), bi=bi):
            pb = 6 + (pjc[0] % 2)
            pjc[0] += 1
            kb.emit("tensor", lambda e, pb=pb, f=mm_fn: f(e, pb),
                    reads=[r_xnbuf[bi], r_wgrp] + list(extra_reads),
                    writes=[r_ps[pb]])
            kb.emit(ev_eng, lambda e, pb=pb, f=ev_fn: f(e, pb),
                    reads=[r_ps[pb], r_vecs, r_vbias], rmw=[wr])

        for hq in range(QHG):
            def mmq(e, pb, hq=hq, xnb=xnb):
                last = None
                for hc in range(16):
                    last = e.matmul(
                        ps[pb][:, 0:TBA],
                        wq_sb[:, hc * 1024 + hq * 128: hc * 1024 + hq * 128 + 128],
                        AP(xnb, hc * TBA, [[16 * TBA, 128], [1, TBA]]),
                        start=(hc == 0), stop=(hc == 15))
                return last

            def evq(e, pb, hq=hq, bi=bi, g=g):
                return e.activation(qb[bi][:, hq * TBA:(hq + 1) * TBA],
                                    ps[pb][:, 0:TBA], AF.Relu,
                                    bias=bq_s[:, g * 8 + hq:g * 8 + hq + 1],
                                    scale=RS128)
            projmm(mmq, "scalar", evq, r_qb[bi])
        for kv in range(KVG):
            def mmk(e, pb, kv=kv, xnb=xnb):
                last = None
                for hc in range(16):
                    last = e.matmul(
                        ps[pb][:, 0:TBA],
                        wk_sb[:, hc * 512 + kv * 128: hc * 512 + kv * 128 + 128],
                        AP(xnb, hc * TBA, [[16 * TBA, 128], [1, TBA]]),
                        start=(hc == 0), stop=(hc == 15))
                return last

            def evk(e, pb, kv=kv, bi=bi, g=g):
                return e.activation(kraw[bi][:, kv * TBA:(kv + 1) * TBA],
                                    ps[pb][:, 0:TBA], AF.Relu,
                                    bias=bk_s[:, g * 4 + kv:g * 4 + kv + 1])
            projmm(mmk, "scalar", evk, r_kraw[bi])
        for tt in range(2):
            def mmv(e, pb, tt=tt, xnb=xnb):
                last = None
                for hc in range(16):
                    lhs = AP(xnb, hc * TBA + tt * 128, [[16 * TBA, 128], [1, 128]])
                    last = e.matmul(ps[pb][:, 0:512], lhs,
                                    wv_sb[:, hc * 512:(hc + 1) * 512],
                                    start=(hc == 0), stop=(hc == 15))
                return last

            def evv(e, pb, tt=tt, bi=bi, g=g):
                return e.tensor_add(vb[bi][:, tt * 512:(tt + 1) * 512],
                                    ps[pb][:, 0:512],
                                    vbias[:, g * 512:(g + 1) * 512])
            projmm(mmv, "vector", evv, r_vb[bi])

        def mmz1(e, pb, xnb=xnb):
            last = None
            for hc in range(16):
                last = e.matmul(ps[pb][0:16, 0:TBA],
                                wg1_sb[:, hc * 16:(hc + 1) * 16],
                                AP(xnb, hc * TBA, [[16 * TBA, 128], [1, TBA]]),
                                start=(hc == 0), stop=(hc == 15))
            return last

        def evz1(e, pb):
            return e.tensor_copy(z1sb[:, :], ps[pb][0:16, 0:TBA])
        projmm(mmz1, "vector", evz1, r_z1)
        for kv in range(KVG):
            def mmz2(e, pb, kv=kv):
                return e.matmul(ps[pb][:, 0:TBA],
                                wg2_sb[:, kv * 128:(kv + 1) * 128],
                                z1sb[:, :], start=True, stop=True)

            def evz2(e, pb, kv=kv, g=g):
                return e.tensor_scalar_add(gt1[:, kv * TBA:(kv + 1) * TBA],
                                           ps[pb][:, 0:TBA],
                                           bg2_s[:, g * 4 + kv:g * 4 + kv + 1])
            pb = 6 + (pjc[0] % 2)
            pjc[0] += 1
            kb.emit("tensor", lambda e, pb=pb, f=mmz2: f(e, pb),
                    reads=[r_z1, r_wgrp], writes=[r_ps[pb]])
            kb.emit("vector", lambda e, pb=pb, f=evz2: f(e, pb),
                    reads=[r_ps[pb], r_vecs], rmw=[r_gt1])

        kb.emit("scalar", lambda e: e.activation(gt2[:, :], gt1[:, :], AF.Abs),
                reads=[r_gt1], writes=[r_gt2])
        kb.emit("scalar",
                lambda e: e.activation(gt2[:, :], gt2[:, :], AF.Exp, scale=-1.0),
                rmw=[r_gt2])
        kb.emit("scalar",
                lambda e: e.activation(gt2[:, :], gt2[:, :], AF.Ln, bias=1.0),
                rmw=[r_gt2])
        kb.emit("vector",
                lambda e: e.tensor_scalar_min(gt1[:, :], gt1[:, :], 0.0),
                rmw=[r_gt1])
        kb.emit("vector",
                lambda e: e.tensor_sub(gt1[:, :], gt1[:, :], gt2[:, :]),
                reads=[r_gt2], rmw=[r_gt1])
        for kv in range(KVG):
            kb.emit("vector",
                    lambda e, kv=kv, bi=bi: e.tensor_tensor_scan(
                        bbuf[:, kv * TBA:(kv + 1) * TBA],
                        scanmask[:, :], gt1[:, kv * TBA:(kv + 1) * TBA],
                        0.0, ALU.mult, ALU.add),
                    reads=[r_gt1, r_consts], rmw=[r_b])
        kb.emit("scalar",
                lambda e, bi=bi: e.activation(expb[bi][:, :], bbuf[:, :],
                                              AF.Exp, scale=1.0 / 16.0),
                reads=[r_b], writes=[r_expb[bi]])
        kb.emit("scalar",
                lambda e, bi=bi: e.activation(expmb[:, :], bbuf[:, :],
                                              AF.Exp, scale=-1.0 / 16.0),
                reads=[r_b], writes=[r_expmb])
        kb.emit("vector",
                lambda e, bi=bi: e.tensor_mul(kfm[bi][:, :], kraw[bi][:, :],
                                              expmb[:, :]),
                reads=[r_kraw[bi], r_expmb], writes=[r_kfm[bi]])

        def qexp(e, bi=bi):
            last = None
            for hq in range(QHG):
                kv = hq // 2
                last = e.tensor_mul(
                    qb[bi][:, hq * TBA:(hq + 1) * TBA],
                    qb[bi][:, hq * TBA:(hq + 1) * TBA],
                    expb[bi][:, kv * TBA:(kv + 1) * TBA])
            return last
        kb.emit("vector", qexp, reads=[r_expb[bi]], rmw=[r_qb[bi]])
        for half in range(2):
            pb = 6 + (half % 2)

            def mmt(e, pb=pb, half=half, bi=bi):
                last = None
                for k in range(4):
                    j = half * 4 + k
                    last = e.matmul(ps[pb][:, k * 128:(k + 1) * 128],
                                    kfm[bi][:, j * 128:(j + 1) * 128],
                                    identb[:, :], start=True, stop=True)
                return last
            kb.emit("tensor", mmt, reads=[r_kfm[bi], r_consts], writes=[r_ps[pb]])

            def evt(e, pb=pb, half=half, bi=bi):
                last = None
                for k in range(4):
                    j = half * 4 + k
                    kv, tt = j // 2, j % 2
                    last = e.tensor_copy(
                        kpm[bi][:, tt * 512 + kv * 128: tt * 512 + kv * 128 + 128],
                        ps[pb][:, k * 128:(k + 1) * 128])
                return last
            kb.emit("vector", evt, reads=[r_ps[pb]], rmw=[r_kpm[bi]])


    def emit_scan(g, blk):
        bi = blk % 2
        tj0 = [0]
        for cl in range(CPB):
            cpar = cl % 2
            tt, rr = cl // 2, (cl % 2) * 64

            def sdelta(e, bi=bi, tt=tt, rr=rr):
                last = None
                for kv in range(KVG):
                    c0 = tt * 512 + kv * 128
                    last = e.matmul(ps[0][:, kv * 128:(kv + 1) * 128],
                                    kpm[bi][rr:rr + 64, c0:c0 + 128],
                                    vb[bi][rr:rr + 64, c0:c0 + 128],
                                    start=False, stop=False,
                                    skip_group_check=True)
                return last
            kb.emit("tensor", sdelta, reads=[r_kpm[bi], r_vb[bi]], rmw=[r_ps[0]])

            def sdecay(e, cl=cl, bi=bi, cpar=cpar):
                last = None
                for kv in range(KVG):
                    cc = kv * TBA + (cl + 1) * CH - 1
                    last = e.tensor_scalar_mul(
                        ps[0][:, kv * 128:(kv + 1) * 128],
                        ps[0][:, kv * 128:(kv + 1) * 128],
                        expb[bi][:, cc:cc + 1])
                return e.tensor_copy(ssnap[1 - cpar][:, :], ps[0][:, :])
            kb.emit("vector", sdecay, reads=[r_expb[bi]], rmw=[r_ps[0]],
                    writes=[r_ssnap[1 - cpar]])

            def atmm(e, cl=cl, bi=bi, cpar=cpar, rr=rr):
                last = None
                for hq in range(QHG):
                    kv = hq // 2
                    last = e.matmul(
                        ps[5][rr:rr + 64, hq * CH:(hq + 1) * CH],
                        kfm[bi][:, kv * TBA + cl * CH: kv * TBA + (cl + 1) * CH],
                        qb[bi][:, hq * TBA + cl * CH: hq * TBA + (cl + 1) * CH],
                        start=True, stop=True,
                        tile_position=(0, rr))
                return last
            kb.emit("tensor", atmm, reads=[r_kfm[bi], r_qb[bi]],
                    writes=[r_ps[5]])

            def atev(e, cpar=cpar, rr=rr):
                last = None
                for hq in range(QHG):
                    last = e.tensor_mul(ats[rr:rr + 64, hq * CH:(hq + 1) * CH],
                                        ps[5][rr:rr + 64, hq * CH:(hq + 1) * CH],
                                        mask64[rr:rr + 64, :])
                return last
            kb.emit("vector", atev, reads=[r_ps[5], r_consts],
                    writes=[r_ats[cpar]])

            ob = (1 + cpar, 3 + cpar)

            def omm(e, bi=bi, cpar=cpar, tt=tt, rr=rr, ob=ob, cl=cl):
                last = None
                for hq in range(QHG):
                    kv = hq // 2
                    pb = ob[hq // 4]
                    cc = (hq % 4) * 128
                    c0 = tt * 512 + kv * 128
                    e.matmul(ps[pb][rr:rr + 64, cc:cc + 128],
                             ats[rr:rr + 64, hq * CH:(hq + 1) * CH],
                             vb[bi][rr:rr + 64, c0:c0 + 128],
                             start=True, stop=False, skip_group_check=True,
                             tile_position=(rr, rr))
                    last = e.matmul(ps[pb][rr:rr + 64, cc:cc + 128],
                                    qb[bi][:, hq * TBA + cl * CH: hq * TBA + (cl + 1) * CH],
                                    ssnap[cpar][:, kv * 128:(kv + 1) * 128],
                                    start=False, stop=True,
                                    skip_group_check=True,
                                    tile_position=(0, rr))
                return last
            kb.emit("tensor", omm,
                    reads=[r_ats[cpar], r_vb[bi], r_qb[bi], r_ssnap[cpar]],
                    writes=[r_ps[ob[0]], r_ps[ob[1]]])

            def msq(e, cl=cl, ob=ob, rr=rr):
                last = None
                m0 = (cl // 2) * QHG
                for hq in range(QHG):
                    pb = ob[hq // 4]
                    cc = (hq % 4) * 128
                    last = e.activation(
                        sqscr[rr:rr + 64, :], ps[pb][rr:rr + 64, cc:cc + 128],
                        AF.Square,
                        accum_out=ms_b[rr:rr + 64, m0 + hq: m0 + hq + 1])
                return last
            kb.emit("scalar", msq, reads=[r_ps[ob[0]], r_ps[ob[1]]],
                    rmw=[r_ms[cl]])

            m0 = (cl // 2) * QHG
            kb.emit("scalar",
                    lambda e, rr=rr, m0=m0: e.activation(
                        rstd_b[rr:rr + 64, m0:m0 + QHG],
                        ms_b[rr:rr + 64, m0:m0 + QHG], AF.Ln,
                        scale=1.0 / HD, bias=epsc[rr:rr + 64, 0:1]),
                    reads=[r_ms[cl]], writes=[r_rstd[cl]])
            kb.emit("scalar",
                    lambda e, rr=rr, m0=m0: e.activation(
                        rstd_b[rr:rr + 64, m0:m0 + QHG],
                        rstd_b[rr:rr + 64, m0:m0 + QHG], AF.Exp,
                        scale=-0.5),
                    rmw=[r_rstd[cl]])

            opar = (cl // 2) % 2

            def oev(e, cl=cl, ob=ob, opar=opar, rr=rr):
                last = None
                m0 = (cl // 2) * QHG
                for hq in range(QHG):
                    pb = ob[hq // 4]
                    cc = (hq % 4) * 128
                    last = e.tensor_scalar_mul(
                        osb[opar][rr:rr + 64, hq * 128:(hq + 1) * 128],
                        ps[pb][rr:rr + 64, cc:cc + 128],
                        rstd_b[rr:rr + 64, m0 + hq: m0 + hq + 1])
                return last
            kb.emit("vector", oev, reads=[r_ps[ob[0]], r_ps[ob[1]],
                                           r_rstd[cl]],
                    rmw=[r_osb[opar]])
            if cl % 2 == 1:
                for hh in range(2):
                    pb = 5

                    def otmm(e, pb=pb, hh=hh, opar=opar):
                        last = None
                        for k in range(4):
                            hq = hh * 4 + k
                            last = e.matmul(
                                ps[pb][:, k * 128:(k + 1) * 128],
                                osb[opar][:, hq * 128:(hq + 1) * 128],
                                identb[:, :], start=True, stop=True)
                        return last
                    kb.emit("tensor", otmm, reads=[r_osb[opar], r_consts],
                            writes=[r_ps[pb]])

                    def otev(e, pb=pb, hh=hh, opar=opar):
                        last = None
                        for k in range(4):
                            hq = hh * 4 + k
                            last = e.tensor_scalar_mul(
                                oTb[opar][:, hq * 128:(hq + 1) * 128],
                                ps[pb][:, k * 128:(k + 1) * 128],
                                gnw_s[:, 0:1])
                        return last
                    kb.emit("vector", otev, reads=[r_ps[pb], r_vecs],
                            rmw=[r_oTb[opar]])
                t0 = blk * TBA + (cl - 1) * CH
                dst = AP(oT_h, (t0 // 128) * 2048 + g * QHG * 128,
                         [[NH * T, 128], [1, QHG * 128]])
                src = AP(oTb[opar], 0, [[QHG * 128, 128], [1, QHG * 128]])
                dma("sync", dst, src, reads=[r_oTb[opar]], rmw=[r_oT_h],
                    ch=f"oT{opar}")


    for g in range(NG):
        if g > 0:
            emit_gweights(g)
        kb.emit("tensor",
                lambda e: e.matmul(ps[0][:, :], zer512[:, 0:128], zer512[:, :],
                                   start=True, stop=False, skip_group_check=True),
                reads=[r_consts], writes=[r_ps[0]])
        kb.emit("vector", lambda e: e.tensor_copy(ssnap[0][:, :], ps[0][:, :]),
                rmw=[r_ps[0]], writes=[r_ssnap[0]])

        prev = None
        for blk in range(NBA):
            emit_prologue(g, blk)
            if prev is not None:
                emit_scan(g, prev)
            prev = blk
        emit_scan(g, prev)

    kb.barrier()
    sb.release()
    sb.mark()
    wo_sb = sb.alloc([128, 16 * 2048], BF16, "wo_sb")
    wtp = [sb.alloc([128, 2048], BF16, f"wtp{i}") for i in range(2)]
    natb = [sb.alloc([128, 2048], BF16, f"natb{i}") for i in range(4)]
    stgb = [sb.alloc([128, 2048], BF16, f"stgb{i}") for i in range(4)]
    oTt = [sb.alloc([128, NH * 128], BF16, f"oTt{i}") for i in range(2)]
    sst = [sb.alloc([128, 2048], F32, f"sst{i}") for i in range(2)]
    r_wo = R("wo")
    r_oTt = [R(), R()]
    r_sst = [R(), R()]
    r_s_h = [R(f"s_h{i}") for i in range(T // 128)]

    for q4 in range(4):
        dma("gpsimd",
            AP(wo_sb, q4 * 4 * 2048, [[16 * 2048, 128], [2048, 4], [1, 2048]]),
            AP(din["wo"], q4 * 4 * 128 * DM, [[DM, 128], [128 * DM, 4], [1, 2048]]),
            rmw=[r_wo], ch="wo")
    for tc in range(T // 128):
        for _ in range(4):
            if trip_jobs:
                trip_jobs.pop(0)()
        for _ in range(8):
            if prep_jobs:
                prep_jobs.pop(0)()
        pi = tc % 2
        dma("sync", AP(oTt[pi], 0, [[NH * 128, 128], [1, NH * 128]]),
            AP(oT_h, tc * 2048, [[NH * T, 128], [1, NH * 128]]),
            reads=[r_oT_h], writes=[r_oTt[pi]], ch=f"oTt{pi}")
        for hb in range(4):
            pb = 6 + (hb % 2)

            def wom(e, pi=pi, hb=hb, pb=pb):
                last = None
                for fc in range(16):
                    last = e.matmul(
                        ps[pb][:, 0:512],
                        oTt[pi][:, fc * 128:(fc + 1) * 128],
                        wo_sb[:, fc * 2048 + hb * 512: fc * 2048 + (hb + 1) * 512],
                        start=(fc == 0), stop=(fc == 15))
                return last
            kb.emit("tensor", wom, reads=[r_oTt[pi], r_wo], writes=[r_ps[pb]])
            kb.emit("vector",
                    lambda e, pi=pi, hb=hb, pb=pb: e.tensor_copy(
                        sst[pi][:, hb * 512:(hb + 1) * 512], ps[pb][:, 0:512]),
                    reads=[r_ps[pb]], rmw=[r_sst[pi]])
        dma("sync", AP(s_h, tc * 128 * DM, [[DM, 128], [1, DM]]), sst[pi][:, :],
            reads=[r_sst[pi]], writes=[r_s_h[tc]], ch=f"sst{pi}")

    while trip_jobs:
        trip_jobs.pop(0)()
    while prep_jobs:
        prep_jobs.pop(0)()

    kb.barrier()
    sb.release()
    sb.mark()
    x1 = sb.alloc([128, 16 * TBF], BF16, "x1")
    x2 = sb.alloc([128, 64 * TBF], BF16, "x2")
    wus = [sb.alloc([128, 16 * 256], BF16, f"wus{i}") for i in range(2)]
    wgs = [sb.alloc([128, 16 * 256], BF16, f"wgs{i}") for i in range(2)]
    wds = [sb.alloc([128, 32 * 256], BF16, f"wds{i}") for i in range(2)]
    sgt = [sb.alloc([128, TBF], BF16, f"sgt{i}") for i in range(2)]
    gst = [sb.alloc([128, 2048], BF16, f"gst{i}") for i in range(4)]
    x1b = sb.alloc([128, 16 * TBF], BF16, "x1b")
    sblk = sb.alloc([128, 2048], BF16, "sblk")
    xblk = [sb.alloc([128, 512], F32, f"xblk{i}") for i in range(2)]
    outst = [sb.alloc([128, 512], F32, f"outst{i}") for i in range(2)]
    dsc = sb.alloc([128, 512], F32, "dsc")

    r_x1 = R()
    rbuf = [{"c": [R() for _ in range(16)], "w": R()} for _ in range(2)]
    r_x2 = R()
    r_x2g = [R() for _ in range(64)]
    r_wus = [R(), R()]
    r_wgs = [R(), R()]
    r_wds = [R(), R()]
    r_sgt = [R(), R()]
    r_gst = [R(), R(), R(), R()]
    r_sblk = R()
    r_xblk = [R(), R()]
    r_outst = [R(), R()]
    r_dsc = R()

    def ffn_head(blk):
        t0 = blk * TBF
        TB = blk % 2
        xa, xo = (x1, x1b) if TB == 0 else (x1b, x1)
        ra, ro = rbuf[TB], rbuf[1 - TB]
        dma("sync", AP(xa, 0, [[16 * TBF, 128], [TBF, 16], [1, TBF]]),
            AP(xnT_h, t0, [[16 * T, 128], [T, 16], [1, TBF]]),
            reads=[r_xnT_h[2 * blk], r_xnT_h[2 * blk + 1]],
            writes=[ra["w"]] + ra["c"], ch=f"x1ld{TB}")

        def sumul(e, xa=xa):
            last = None
            for c in range(16):
                a = xa[:, c * TBF:(c + 1) * TBF]
                last = e.tensor_scalar_mul(a, a, suug_s[:, c:c + 1])
            return last
        kb.emit("vector", sumul, reads=[r_vecs], rmw=ra["c"])
        for cb in range(16):
            pb = 6 + (cb % 2)
            kb.emit("tensor",
                    lambda e, cb=cb, pb=pb, xa=xa: e.matmul(
                        ps[pb][:, 0:TBF], h128b[:, :],
                        xa[:, cb * TBF:(cb + 1) * TBF], start=True, stop=True),
                    reads=[ra["c"][cb], r_consts], writes=[r_ps[pb]])
            kb.emit("vector",
                    lambda e, cb=cb, pb=pb, xa=xa: e.tensor_copy(
                        xa[:, cb * TBF:(cb + 1) * TBF], ps[pb][:, 0:TBF]),
                    reads=[r_ps[pb]], rmw=[ra["c"][cb]])
        bufs = [xa, xo]
        rr_ = [ra, ro]
        for s in range(4):
            h = 1 << s
            src_b, dst_b = bufs[s % 2], bufs[(s + 1) % 2]
            ng = 16 // (2 * h)
            dims = [[16 * TBF, 128], [2 * h * TBF, ng], [TBF, h], [1, TBF]]
            aA = AP(src_b, 0, dims)
            aB = AP(src_b, h * TBF, dims)
            oA = AP(dst_b, 0, dims)
            oB = AP(dst_b, h * TBF, dims)
            rd = ra["c"] if s == 0 else [rr_[s % 2]["w"]]
            wr = [rr_[(s + 1) % 2]["w"]] + (ra["c"] if s == 3 else [])

            def bf1(e, aA=aA, aB=aB, oA=oA, oB=oB):
                e.tensor_add(oA, aA, aB)
                return e.tensor_sub(oB, aA, aB)
            kb.emit("vector", bf1, reads=list(rd), writes=list(wr))

    def ffn_ug(blk):
        TB = blk % 2
        xa = (x1, x1b)[TB]
        ra = rbuf[TB]
        for sup in range(IN // 256):
            pp = sup % 2
            dma("sync", wus[pp][:, :],
                AP(wuT_h, sup * 4096, [[64 * 2048, 128], [1, 4096]]),
                reads=[r_wT["u"]], writes=[r_wus[pp]], ch=f"wus{pp}")
            dma("sync", wgs[pp][:, :],
                AP(wgT_h, sup * 4096, [[64 * 2048, 128], [1, 4096]]),
                reads=[r_wT["g"]], writes=[r_wgs[pp]], ch=f"wgs{pp}")
            for isl in range(2):
                ic = sup * 2 + isl
                pbu = pp * 4 + isl
                pbg = pp * 4 + 2 + isl

                def upm(e, isl=isl, pbu=pbu, pp=pp, xa=xa):
                    last = None
                    for hc in range(16):
                        last = e.matmul(
                            ps[pbu][:, 0:TBF],
                            wus[pp][:, isl * 2048 + hc * 128: isl * 2048 + hc * 128 + 128],
                            xa[:, hc * TBF:(hc + 1) * TBF],
                            start=(hc == 0), stop=(hc == 15))
                    return last
                kb.emit("tensor", upm, reads=[r_wus[pp], ra["w"]],
                        writes=[r_ps[pbu]])

                def gam(e, isl=isl, pbg=pbg, pp=pp, xa=xa):
                    last = None
                    for hc in range(16):
                        last = e.matmul(
                            ps[pbg][:, 0:TBF],
                            wgs[pp][:, isl * 2048 + hc * 128: isl * 2048 + hc * 128 + 128],
                            xa[:, hc * TBF:(hc + 1) * TBF],
                            start=(hc == 0), stop=(hc == 15))
                    return last
                kb.emit("tensor", gam, reads=[r_wgs[pp], ra["w"]],
                        writes=[r_ps[pbg]])
                kb.emit("scalar",
                        lambda e, ic=ic, pbg=pbg, isl=isl: e.activation(
                            sgt[isl][:, :], ps[pbg][:, 0:TBF], AF.Silu,
                            scale=sgate_s[:, ic:ic + 1]),
                        reads=[r_ps[pbg], r_vecs], writes=[r_sgt[isl]])
                dstx2 = AP(x2, ic, [[64 * TBF, 128], [64, TBF]])
                kb.emit("vector",
                        lambda e, ic=ic, pbu=pbu, isl=isl, dstx2=dstx2:
                        e.scalar_tensor_tensor(dstx2, ps[pbu][:, 0:TBF],
                                               sup_s[:, ic:ic + 1],
                                               sgt[isl][:, :],
                                               ALU.mult, ALU.mult),
                        reads=[r_ps[pbu], r_sgt[isl], r_vecs], rmw=[r_x2])

    def ffn_sandwich(blk):
        for hmat in (h128b, bh64b):
            for g4 in range(64):
                pb = g4 % 2

                def fmm2(e, g4=g4, pb=pb, hmat=hmat):
                    last = None
                    for k in range(4):
                        j = g4 * 4 + k
                        last = e.matmul(ps[pb][:, k * 128:(k + 1) * 128],
                                        x2[:, j * 128:(j + 1) * 128],
                                        hmat[:, :], start=True, stop=True)
                    return last
                kb.emit("tensor", fmm2, reads=[r_x2, r_x2g[g4], r_consts],
                        writes=[r_ps[pb]])
                kb.emit("vector",
                        lambda e, g4=g4, pb=pb: e.tensor_copy(
                            x2[:, g4 * 512:(g4 + 1) * 512], ps[pb][:, 0:512]),
                        reads=[r_ps[pb]], rmw=[r_x2g[g4]])

    def ffn_down(blk):
        for ob in range(4):
            for pc in range(4):
                wpp = (ob * 4 + pc) % 2
                dma("sync", wds[wpp][:, :],
                    AP(wdT_h, ob * 64 * 512 + pc * 16 * 512,
                       [[64 * DM, 128], [1, 16 * 512]]),
                    reads=[r_wT["d"]], writes=[r_wds[wpp]], ch=f"wds{wpp}")
                for tc4 in range(TBF // 128):
                    pb = (ob % 2) * 4 + tc4

                    def dwm(e, pc=pc, wpp=wpp, tc4=tc4, pb=pb):
                        last = None
                        for icl in range(16):
                            ic = pc * 16 + icl
                            lhs = AP(x2, tc4 * 128 * 64 + ic,
                                     [[64 * TBF, 128], [64, 128]])
                            last = e.matmul(
                                ps[pb][:, 0:512], lhs,
                                AP(wds[wpp], icl * 512,
                                   [[16 * 512, 128], [1, 512]]),
                                start=(ic == 0), stop=(ic == 63))
                        return last
                    rds = [r_x2, r_wds[wpp]] + r_x2g[tc4 * 16:(tc4 + 1) * 16]
                    if pc == 0:
                        kb.emit("tensor", dwm, reads=rds, writes=[r_ps[pb]])
                    else:
                        kb.emit("tensor", dwm, reads=rds, rmw=[r_ps[pb]])
            for tc4 in range(TBF // 128):
                pb = (ob % 2) * 4 + tc4
                kb.emit("vector",
                        lambda e, ob=ob, pb=pb: e.scalar_tensor_tensor(
                            dsc[:, 0:512], ps[pb][:, 0:512], 1.0,
                            wsdb[:, ob * 512:(ob + 1) * 512], ALU.mult,
                            ALU.mult),
                        reads=[r_ps[pb], r_wsdb], writes=[r_dsc])
                kb.emit("scalar",
                        lambda e, ob=ob, tc4=tc4: e.activation(
                            gst[tc4][:, ob * 512:(ob + 1) * 512], dsc[:, 0:512],
                            AF.Silu),
                        reads=[r_dsc], rmw=[r_gst[tc4]])
        for tc4 in range(TBF // 128):
            tc = blk * (TBF // 128) + tc4
            dma("gpsimd", sblk[:, :], AP(s_h, tc * 128 * DM, [[DM, 128], [1, DM]]),
                reads=[r_s_h[tc]], writes=[r_sblk], ch="sblk")
            if fulldbg:
                dma("sync", AP(g_d, tc * 128 * DM, [[DM, 128], [1, DM]]),
                    gst[tc4][:, :], reads=[r_gst[tc4]], ch="dbg")
            for h4 in range(4):
                po = h4 % 2
                dma("gpsimd", xblk[po][:, :],
                    AP(din["x"], tc * 128 * DM + h4 * 512, [[DM, 128], [1, 512]]),
                    writes=[r_xblk[po]], ch=f"xblk{po}")
                kb.emit("vector",
                        lambda e, tc4=tc4, h4=h4, po=po: e.tensor_mul(
                            outst[po][:, :], gst[tc4][:, h4 * 512:(h4 + 1) * 512],
                            sblk[:, h4 * 512:(h4 + 1) * 512]),
                        reads=[r_gst[tc4], r_sblk], writes=[r_outst[po]])
                kb.emit("vector",
                        lambda e, h4=h4, po=po: e.tensor_add(
                            outst[po][:, :], outst[po][:, :], xblk[po][:, :]),
                        reads=[r_xblk[po]], rmw=[r_outst[po]])
                dma("gpsimd", AP(y, tc * 128 * DM + h4 * 512, [[DM, 128], [1, 512]]),
                    outst[po][:, :], reads=[r_outst[po]], ch=f"y{po}")

    for blk in range(NBF):
        if blk == 0:
            ffn_head(0)
        ffn_ug(blk)
        ffn_sandwich(blk)
        if blk + 1 < NBF:
            ffn_head(blk + 1)
        ffn_down(blk)

    with nc.Block() as block:
        kb.emit_all(nc, block)
    return nc


_NC_CACHE = None


def kernel(**inputs):
    global _NC_CACHE
    if _NC_CACHE is None:
        _NC_CACHE = build_program()
    nc = _NC_CACHE
    x = np.ascontiguousarray(inputs["L0_attn_hidden_states"], dtype=np.float32)
    base = {
        "wq": np.ascontiguousarray(inputs["L0_attn_Wq"], dtype=np.float32),
        "bq": np.ascontiguousarray(inputs["L0_attn_bq"], dtype=np.float32).reshape(1, -1),
        "wk": np.ascontiguousarray(inputs["L0_attn_Wk"], dtype=np.float32),
        "bk": np.ascontiguousarray(inputs["L0_attn_bk"], dtype=np.float32).reshape(1, -1),
        "wv": np.ascontiguousarray(inputs["L0_attn_Wv"], dtype=np.float32),
        "bv": np.ascontiguousarray(inputs["L0_attn_bv"], dtype=np.float32).reshape(1, -1),
        "wg1": np.ascontiguousarray(inputs["L0_attn_Wg1"], dtype=np.float32),
        "wg2": np.ascontiguousarray(inputs["L0_attn_Wg2"], dtype=np.float32),
        "bg2": np.ascontiguousarray(inputs["L0_attn_bg2"], dtype=np.float32).reshape(1, -1),
        "gnw": np.ascontiguousarray(inputs["L0_attn_g_norm_w"], dtype=np.float32).reshape(1, -1),
        "wo": np.ascontiguousarray(inputs["L0_attn_Wo"], dtype=np.float32),
        "su_ug": np.ascontiguousarray(inputs["L0_ffn_SU_ug"], dtype=np.float32).reshape(1, -1),
        "su_dp": np.ascontiguousarray(inputs["L0_ffn_SU_dp"], dtype=np.float32).reshape(1, -1),
        "wsug": np.ascontiguousarray(inputs["L0_ffn_Wscale_ug"], dtype=np.float32).reshape(1, -1),
        "wsdp": np.ascontiguousarray(inputs["L0_ffn_Wscale_dp"], dtype=np.float32).reshape(1, -1),
        "wup": np.ascontiguousarray(inputs["L0_ffn_W_up"], dtype=np.float32),
        "wgate": np.ascontiguousarray(inputs["L0_ffn_W_gate"], dtype=np.float32),
        "wdown": np.ascontiguousarray(inputs["L0_ffn_W_down"], dtype=np.float32),
    }
    in_maps = []
    for c in range(8):
        m = dict(base)
        m["x"] = np.ascontiguousarray(x[c])
        in_maps.append(m)
    res = run_bass_kernel_spmd(nc, in_maps, core_ids=list(range(8)))
    return np.stack([np.asarray(r["y"], dtype=np.float32) for r in res.results], axis=0)


# version 25
# speedup vs baseline: 1.1200x; 1.0322x over previous
import numpy as np
import concourse.bass as bass
import concourse.mybir as mybir
from concourse.bass_utils import run_bass_kernel_spmd

F32 = mybir.dt.float32
BF16 = mybir.dt.bfloat16
AF = mybir.ActivationFunctionType
ALU = mybir.AluOpType
AX = mybir.AxisListType
AP = bass.AP

T = 2048
DM = 2048
HKV = 8
NH = 16
HD = 128
CH = 64
IN = 8192
NG = 2
KVG = HKV // NG
QHG = NH // NG
TBA = 256
NBA = T // TBA
CPB = TBA // CH
TBF = 512
NBF = T // TBF
RS128 = float(HD) ** -0.5
EPS = 1e-5

ENGS = ("tensor", "vector", "scalar", "gpsimd", "sync")


def _hadamard(n):
    i = np.arange(n)
    a = i[:, None] & i[None, :]
    pc = np.zeros((n, n), dtype=np.int64)
    for b in range(14):
        pc += (a >> b) & 1
    return np.where(pc % 2 == 0, 1.0, -1.0).astype(np.float32)


H128_NP = _hadamard(128)
BH16_NP = np.kron(np.eye(8, dtype=np.float32), _hadamard(16))
BH64_NP = np.kron(np.eye(2, dtype=np.float32), _hadamard(64))
IDENT_NP = np.eye(128, dtype=np.float32)
IDF_NP = np.eye(64, dtype=np.float32)
MASK64_NP = np.tile((np.arange(64)[None, :] >= np.arange(64)[:, None]).astype(np.float32), (2, 1))
ONES_NP = np.ones((1, 128), dtype=np.float32)


class Step:
    __slots__ = ("eng", "fn", "dma", "cval", "gid", "deps", "bg", "ch")


class Res:
    __slots__ = ("writer", "readers", "name")

    def __init__(self, name=""):
        self.writer = None
        self.readers = []
        self.name = name


class KB:
    def __init__(self):
        self.steps = []
        self.cnt = {e: 0 for e in ENGS}
        self.dcnt = {}
        self.frontier_deps = []
        self.frontier_pending = set()

    def emit(self, eng, fn, reads=(), writes=(), rmw=(), deps=(), dma=False,
             bg=False, ch=None):
        d = []
        for r in reads:
            if r.writer is not None:
                d.append(r.writer)
        for w in tuple(writes) + tuple(rmw):
            if w.writer is not None:
                d.append(w.writer)
            d.extend(w.readers)
        d.extend(deps)
        if eng in self.frontier_pending:
            d.extend(self.frontier_deps)
            self.frontier_pending.discard(eng)
        st = Step()
        st.eng = eng
        st.fn = fn
        st.dma = dma
        st.bg = bg
        st.gid = len(self.steps)
        if dma:
            st.ch = ch if ch is not None else f"_q_{eng}"
            self.dcnt[st.ch] = self.dcnt.get(st.ch, 0) + 1
            st.cval = 16 * self.dcnt[st.ch]
        else:
            st.ch = None
            self.cnt[eng] += 1
            st.cval = self.cnt[eng]
        for r in reads:
            r.readers.append(st)
        for w in tuple(writes) + tuple(rmw):
            w.writer = st
            w.readers = []
        st.deps = []
        seen = set()
        for s in d:
            if s is None or s.gid in seen or s is st:
                continue
            seen.add(s.gid)
            assert s.gid < st.gid
            st.deps.append(s)
        self.steps.append(st)
        return st

    def barrier(self):
        last = {}
        for st in self.steps:
            if st.bg:
                continue
            last[(st.eng, st.dma)] = st
        self.frontier_deps = list(last.values())
        self.frontier_pending = set(ENGS)

    def emit_all(self, nc, block):
        sems = {}
        for e in ENGS:
            if self.cnt[e]:
                sems[("c", e)] = nc.semaphore(f"sc_{e}").__enter__()
        for chn in self.dcnt:
            sems[("d", chn)] = nc.semaphore(f"sd_{chn}").__enter__()
        for h in sems.values():
            nc.gpsimd.sem_clear(h)
        nc.all_engine_barrier()
        decos = {"tensor": block.tensor, "vector": block.vector,
                 "scalar": block.scalar, "gpsimd": block.gpsimd,
                 "sync": block.sync}
        total_dma = {chn: 16 * n for chn, n in self.dcnt.items()}

        def mk(ename):
            def body(e):
                water = {}
                for st in self.steps:
                    if st.eng != ename:
                        continue
                    for dep in st.deps:
                        if (dep.eng == ename == "tensor"
                                and not dep.dma and not st.dma):
                            continue
                        key = ("d", dep.ch) if dep.dma else ("c", dep.eng)
                        if water.get(key, 0) >= dep.cval:
                            continue
                        water[key] = dep.cval
                        e.wait_ge(sems[key], dep.cval)
                    last = st.fn(e)
                    if st.dma:
                        last.then_inc(sems[("d", st.ch)], 16)
                    else:
                        last.then_inc(sems[("c", ename)], 1)
                if ename == "sync":
                    for chn, tot in total_dma.items():
                        e.wait_ge(sems[("d", chn)], tot)
            return body

        for ename in ENGS:
            decos[ename](mk(ename))


class Sbuf:
    BASE = 16544
    TOP = 229340

    def __init__(self, nc):
        self.nc = nc
        self.off = self.BASE
        self.marks = []
        self.peak = self.BASE
        self.n = 0

    def alloc(self, shape, dtype, name=None):
        esz = 4 if dtype == F32 else 2
        nb = shape[1] * esz
        off = (self.off + 31) // 32 * 32
        assert off + nb <= self.TOP, f"SBUF overflow: {name} ({nb}B at {off})"
        self.off = off + nb
        self.peak = max(self.peak, self.off)
        self.n += 1
        self.offsets = getattr(self, "offsets", {})
        self.offsets[name or f"sb{self.n}"] = off
        return self.nc.alloc_sbuf_tensor_at(name or f"sb{self.n}", list(shape),
                                            dtype, offset=off)

    def mark(self):
        self.marks.append(self.off)

    def release(self):
        self.off = self.marks.pop()


def build_program(debug=False):
    nc = bass.Bass("TRN2", target_bir_lowering=False)
    kb = KB()

    din = {}
    for name, shape in [
        ("x", [T, DM]), ("wq", [DM, DM]), ("bq", [1, DM]), ("wk", [DM, 1024]),
        ("bk", [1, 1024]), ("wv", [DM, 1024]), ("bv", [1, 1024]),
        ("wg1", [DM, 16]), ("wg2", [16, 1024]), ("bg2", [1, 1024]),
        ("gnw", [1, HD]), ("wo", [DM, DM]), ("su_ug", [1, DM]), ("su_dp", [1, IN]),
        ("wsug", [1, 2 * IN]), ("wsdp", [1, DM]), ("wup", [IN, DM]),
        ("wgate", [IN, DM]), ("wdown", [DM, IN]),
    ]:
        din[name] = nc.dram_tensor(name, shape, F32, kind="ExternalInput")
    y = nc.dram_tensor("y", [T, DM], F32, kind="ExternalOutput")

    c_h128 = nc.inline_tensor(H128_NP, name="c_h128")
    c_bh16 = nc.inline_tensor(BH16_NP, name="c_bh16")
    c_bh64 = nc.inline_tensor(BH64_NP, name="c_bh64")
    c_ident = nc.inline_tensor(IDENT_NP, name="c_ident")
    c_idf = nc.inline_tensor(IDF_NP, name="c_idf")
    c_mask = nc.inline_tensor(MASK64_NP, name="c_mask")
    c_ones = nc.inline_tensor(ONES_NP, name="c_ones")

    dkind = "ExternalOutput" if debug else "Internal"
    fulldbg = (debug == True)
    xnT_h = nc.dram_tensor("xnT_h", [128, 16 * T], BF16, kind=dkind)
    oT_h = nc.dram_tensor("oT_h", [128, NH * T], BF16, kind=dkind)
    s_h = nc.dram_tensor("s_h", [T, DM], F32, kind=dkind)
    if fulldbg:
        x1_d = nc.dram_tensor("x1_d", [128, 16 * T], BF16, kind="ExternalOutput")
        g_d = nc.dram_tensor("g_d", [T, DM], BF16, kind="ExternalOutput")
    if debug == 3:
        xt_d = nc.dram_tensor("xt_d", [128, DM], F32, kind="ExternalOutput")
        colt_d = nc.dram_tensor("colt_d", [128, 8], F32, kind="ExternalOutput")
        xn_d = nc.dram_tensor("xn_d", [128, DM], BF16, kind="ExternalOutput")
    wu_nat = nc.dram_tensor("wu_nat", [IN, DM], BF16, kind="Internal")
    wg_nat = nc.dram_tensor("wg_nat", [IN, DM], BF16, kind="Internal")
    wd_nat = nc.dram_tensor("wd_nat", [DM, IN], BF16, kind="Internal")
    wuT_h = nc.dram_tensor("wuT_h", [128, 64 * 2048], BF16, kind="Internal")
    wgT_h = nc.dram_tensor("wgT_h", [128, 64 * 2048], BF16, kind="Internal")
    wdT_h = nc.dram_tensor("wdT_h", [128, 64 * DM], BF16, kind="Internal")

    sb = Sbuf(nc)
    h128b = sb.alloc([128, 128], BF16, "h128b")
    bh16b = sb.alloc([128, 128], BF16, "bh16b")
    bh64b = sb.alloc([128, 128], BF16, "bh64b")
    identb = sb.alloc([128, 128], BF16, "identb")
    identf = sb.alloc([64, 64], F32, "identf")
    mask64 = sb.alloc([128, 64], F32, "mask64")
    onesf = sb.alloc([1, 128], F32, "onesf")
    zer512 = sb.alloc([128, 512], BF16, "zer512")
    scanmask = sb.alloc([128, TBA], F32, "scanmask")
    bq_s = sb.alloc([128, 16], F32, "bq_s")
    bk_s = sb.alloc([128, 8], F32, "bk_s")
    bg2_s = sb.alloc([128, 8], F32, "bg2_s")
    gnw_s = sb.alloc([128, 1], F32, "gnw_s")
    suug_s = sb.alloc([128, 16], F32, "suug_s")
    sup_s = sb.alloc([128, 64], F32, "sup_s")
    sgate_s = sb.alloc([128, 64], F32, "sgate_s")
    wsdb = sb.alloc([128, 2048], F32, "wsdb")
    ms_b = sb.alloc([128, QHG * CPB // 2], F32, "ms_b")
    rstd_b = sb.alloc([128, QHG * CPB // 2], F32, "rstd_b")
    colt_t = [sb.alloc([128, 8], F32, f"colt{i}") for i in range(2)]
    epsc = sb.alloc([128, 1], F32, "epsc")
    sqscr = sb.alloc([128, 128], BF16, "sqscr")

    R = Res
    r_consts = R("consts")
    r_vecs = R("vecs")
    r_vbias = R("vbias")
    r_wsdb = R("wsdb")

    ps = [nc.alloc_psum_tensor(f"ps{i}", [128, 512], F32) for i in range(8)]
    r_ps = [R(f"ps{i}") for i in range(8)]

    def dma(q, dst, src, reads=(), writes=(), rmw=(), deps=(), slow=False,
            bg=False, ch=None):
        if slow:
            fn = lambda e: e.dma_start(dst, src, allow_slow_non_contiguous=True)
        else:
            fn = lambda e: e.dma_start(dst, src)
        return kb.emit(q, fn, reads=reads, writes=writes, rmw=rmw, deps=deps,
                       dma=True, bg=bg, ch=ch)

    for dst, src in [(h128b, c_h128), (bh16b, c_bh16), (bh64b, c_bh64),
                     (identb, c_ident)]:
        dma("gpsimd", dst[:, :], src.ap(), rmw=[r_consts], ch="consts")
    dma("gpsimd", identf[:, :], c_idf.ap(), rmw=[r_consts], ch="consts")
    dma("gpsimd", mask64[:, :], c_mask.ap(), rmw=[r_consts], ch="consts")
    dma("gpsimd", onesf[:, :], c_ones.ap(), rmw=[r_consts], ch="consts")
    kb.emit("vector", lambda e: e.memset(zer512[:, :], 0.0), rmw=[r_consts])
    kb.emit("vector", lambda e: e.memset(epsc[:, :], EPS), rmw=[r_consts])
    kb.emit("vector", lambda e: e.memset(scanmask[:, :], 1.0), rmw=[r_consts])
    kb.emit("vector",
            lambda e: e.memset(AP(scanmask, 0, [[TBA, 128], [CH, CPB]]), 0.0),
            rmw=[r_consts])

    sb.mark()
    vrow = sb.alloc([64, 128], F32, "vrow")
    bvrow = sb.alloc([1, 1024], F32, "bvrow")
    wsdrow = sb.alloc([1, 2048], F32, "wsdrow")

    sb.release()
    sb.mark()
    xt = [sb.alloc([128, DM], F32, f"xt{i}") for i in range(2)]
    xsq = sb.alloc([128, DM], F32, "xsq")
    xnb16 = sb.alloc([128, DM], BF16, "xnb16")
    xnTblk = [sb.alloc([128, 16 * TBA], BF16, f"xnTblk{i}") for i in range(2)]

    r_xt = [R(), R()]
    r_xsq = R()
    r_xn = R()
    r_xnbuf = [R(), R()]
    r_xnT_h = [R(f"xnTh{b}") for b in range(NBA)]
    r_colt = [R(), R()]
    r_wtp = [R(), R()]
    r_natb = [R() for _ in range(4)]
    r_stgb = [R() for _ in range(4)]
    r_wT = {m: R(f"wT_{m}") for m in ("u", "g", "d")}

    vbias = sb.alloc([128, 1024], F32, "vbias")
    wq_sb = sb.alloc([128, 16 * 1024], BF16, "wq_sb")
    wk_sb = sb.alloc([128, 16 * 512], BF16, "wk_sb")
    wv_sb = sb.alloc([128, 16 * 512], BF16, "wv_sb")
    wg1_sb = sb.alloc([128, 16 * 16], BF16, "wg1_sb")
    wg2_sb = sb.alloc([16, 512], BF16, "wg2_sb")
    qb = [sb.alloc([128, QHG * TBA], BF16, f"qb{i}") for i in range(2)]
    kraw = [sb.alloc([128, KVG * TBA], BF16, f"kraw{i}") for i in range(2)]
    kfm = [sb.alloc([128, KVG * TBA], BF16, f"kfm{i}") for i in range(2)]
    kpm = [sb.alloc([128, KVG * TBA], BF16, f"kpm{i}") for i in range(2)]
    vb = [sb.alloc([128, KVG * TBA], BF16, f"vb{i}") for i in range(2)]
    gt1 = sb.alloc([128, KVG * TBA], F32, "gt1")
    gt2 = sb.alloc([128, KVG * TBA], F32, "gt2")
    z1sb = sb.alloc([16, TBA], BF16, "z1sb")
    bbuf = sb.alloc([128, KVG * TBA], F32, "bbuf")
    expb = [sb.alloc([128, KVG * TBA], F32, f"expb{i}") for i in range(2)]
    expmb = sb.alloc([128, KVG * TBA], F32, "expmb")
    qs = [sb.alloc([128, QHG * CH], BF16, f"qs{i}") for i in range(2)]
    ssnap = [sb.alloc([128, KVG * 128], BF16, f"ssnap{i}") for i in range(2)]
    ats = sb.alloc([128, QHG * CH], BF16, "ats")
    osb = [sb.alloc([128, QHG * 128], BF16, f"osb{i}") for i in range(2)]
    oTb = [sb.alloc([128, QHG * 128], BF16, f"oTb{i}") for i in range(2)]

    r_wgrp = R("wgrp")
    r_qb = [R(), R()]
    r_kraw = [R(), R()]
    r_kfm = [R(), R()]
    r_kpm = [R(), R()]
    r_vb = [R(), R()]
    r_gt1 = R()
    r_gt2 = R()
    r_z1 = R()
    r_b = R()
    r_expb = [R(), R()]
    r_expmb = R()
    r_qs = [R(), R()]
    r_ssnap = [R(), R()]
    r_ats = [R(), R()]
    r_osb = [R(), R()]
    r_oTb = [R(), R()]
    r_ms = [R() for _ in range(4)]
    r_rstd = [R() for _ in range(4)]
    r_oT_h = R("oT_h")

    def vecload(dst_ap, src_t, src_off, n, scale=None, mult_ps=False):
        rows = n // 128
        dma("gpsimd", vrow[0:rows, :], AP(src_t, src_off, [[128, rows], [1, 128]]),
            reads=[r_consts], rmw=[r_vecs], ch="vec")
        kb.emit("tensor",
                lambda e, rows=rows: e.matmul(ps[6][:, 0:rows], vrow[0:rows, :],
                                              identf[0:rows, 0:rows],
                                              start=True, stop=True),
                reads=[r_vecs, r_consts], writes=[r_ps[6]])
        if mult_ps:
            fn = lambda e, rows=rows: e.tensor_mul(dst_ap, dst_ap, ps[6][:, 0:rows])
        elif scale is not None:
            fn = lambda e, rows=rows, s=float(scale): e.tensor_scalar_mul(
                dst_ap, ps[6][:, 0:rows], s)
        else:
            fn = lambda e, rows=rows: e.tensor_copy(dst_ap, ps[6][:, 0:rows])
        kb.emit("vector", fn, reads=[r_ps[6]], rmw=[r_vecs])

    c2048 = 1.0 / np.sqrt(2048.0)
    vecload(bq_s[:, :], din["bq"], 0, DM, scale=RS128)
    vecload(bk_s[:, :], din["bk"], 0, 1024)
    vecload(bg2_s[:, :], din["bg2"], 0, 1024)
    vecload(gnw_s[:, :], din["gnw"], 0, HD)
    vecload(suug_s[:, :], din["su_ug"], 0, DM)
    vecload(sup_s[:, :], din["wsug"], 0, IN, scale=c2048)
    vecload(sup_s[:, :], din["su_dp"], 0, IN, mult_ps=True)
    vecload(sgate_s[:, :], din["wsug"], IN, IN, scale=c2048)

    dma("gpsimd", bvrow[:, :], din["bv"].ap(), rmw=[r_vbias], ch="bvrow")
    dma("gpsimd", wsdrow[:, :], din["wsdp"].ap(), rmw=[r_wsdb], ch="wsdrow")
    c8192 = 1.0 / np.sqrt(8192.0)
    for i in range(2):
        kb.emit("tensor",
                lambda e, i=i: e.matmul(ps[7][:, 0:512], onesf[:, :],
                                        bvrow[:, i * 512:(i + 1) * 512],
                                        start=True, stop=True),
                reads=[r_vbias], writes=[r_ps[7]])
        kb.emit("vector",
                lambda e, i=i: e.tensor_copy(vbias[:, i * 512:(i + 1) * 512],
                                             ps[7][:, 0:512]),
                reads=[r_ps[7]], rmw=[r_vbias])
    for i in range(4):
        kb.emit("tensor",
                lambda e, i=i: e.matmul(ps[7][:, 0:512], onesf[:, :],
                                        wsdrow[:, i * 512:(i + 1) * 512],
                                        start=True, stop=True),
                reads=[r_wsdb], writes=[r_ps[7]])
        kb.emit("vector",
                lambda e, i=i: e.tensor_scalar_mul(wsdb[:, i * 512:(i + 1) * 512],
                                                   ps[7][:, 0:512], c8192),
                reads=[r_ps[7]], rmw=[r_wsdb])

    def emit_gweights(g):
        for q4 in range(4):
            dma("gpsimd",
                AP(wq_sb, q4 * 4 * 1024, [[16 * 1024, 128], [1024, 4], [1, 1024]]),
                AP(din["wq"], q4 * 4 * 128 * DM + g * 1024,
                   [[DM, 128], [128 * DM, 4], [1, 1024]]),
                rmw=[r_wgrp], ch="gw")
            dma("gpsimd",
                AP(wk_sb, q4 * 4 * 512, [[16 * 512, 128], [512, 4], [1, 512]]),
                AP(din["wk"], q4 * 4 * 128 * 1024 + g * 512,
                   [[1024, 128], [128 * 1024, 4], [1, 512]]),
                rmw=[r_wgrp], ch="gw")
            dma("gpsimd",
                AP(wv_sb, q4 * 4 * 512, [[16 * 512, 128], [512, 4], [1, 512]]),
                AP(din["wv"], q4 * 4 * 128 * 1024 + g * 512,
                   [[1024, 128], [128 * 1024, 4], [1, 512]]),
                rmw=[r_wgrp], ch="gw")
        dma("gpsimd", AP(wg1_sb, 0, [[256, 128], [16, 16], [1, 16]]),
            AP(din["wg1"], 0, [[16, 128], [128 * 16, 16], [1, 16]]),
            rmw=[r_wgrp], ch="gw")
        dma("gpsimd", wg2_sb[:, :],
            AP(din["wg2"], g * 512, [[1024, 16], [1, 512]]),
            rmw=[r_wgrp], ch="gw")

    emit_gweights(0)

    r_wnat = {m: R(f"wnat_{m}") for m in ("u", "g", "d")}
    for m, src, dst in (("d", din["wdown"], wd_nat),):
        rows, cols = src.shape
        nsp = 8
        rr = rows // nsp
        for r8 in range(nsp):
            dma("gpsimd", AP(dst, r8 * rr * cols, [[cols, rr], [1, cols]]),
                AP(src, r8 * rr * cols, [[cols, rr], [1, cols]]),
                rmw=[r_wnat[m]], bg=True, ch="wnat_" + m)

    kb.barrier()

    for i in range(T // 128):
        pi = i % 2
        blk = i // 2
        sub = i % 2
        bi = blk % 2
        dma("sync", xt[pi][:, :], AP(din["x"], i * 128 * DM, [[DM, 128], [1, DM]]),
            writes=[r_xt[pi]], ch=f"xt{pi}")
        kb.emit("vector",
                lambda e, pi=pi: e.reduce_sum(colt_t[pi][:, 0:1], xt[pi][:, :], axis=AX.X),
                reads=[r_xt[pi]], rmw=[r_colt[pi]])
        kb.emit("vector",
                lambda e, pi=pi: e.scalar_tensor_tensor(
                    xsq[:, :], xt[pi][:, :], 1.0, xt[pi][:, :], ALU.mult, ALU.mult,
                    accum_out=colt_t[pi][:, 1:2]),
                reads=[r_xt[pi]], writes=[r_xsq], rmw=[r_colt[pi]])

        def stats1(e, pi=pi):
            e.tensor_scalar_mul(colt_t[pi][:, 2:3], colt_t[pi][:, 0:1], 1.0 / DM)
            return e.tensor_scalar_mul(colt_t[pi][:, 3:4], colt_t[pi][:, 1:2], 1.0 / DM)
        kb.emit("vector", stats1, rmw=[r_colt[pi]])
        kb.emit("vector",
                lambda e, pi=pi: e.scalar_tensor_tensor(
                    colt_t[pi][:, 4:5], colt_t[pi][:, 2:3], -1.0,
                    colt_t[pi][:, 2:3], ALU.mult, ALU.mult),
                rmw=[r_colt[pi]])
        kb.emit("vector",
                lambda e, pi=pi: e.tensor_add(colt_t[pi][:, 5:6],
                                              colt_t[pi][:, 3:4],
                                              colt_t[pi][:, 4:5]),
                rmw=[r_colt[pi]])
        kb.emit("scalar",
                lambda e, pi=pi: e.activation(colt_t[pi][:, 6:7],
                                              colt_t[pi][:, 5:6], AF.Ln,
                                              bias=epsc[:, 0:1]),
                rmw=[r_colt[pi]])
        kb.emit("scalar",
                lambda e, pi=pi: e.activation(colt_t[pi][:, 6:7],
                                              colt_t[pi][:, 6:7], AF.Exp,
                                              scale=-0.5),
                rmw=[r_colt[pi]])
        kb.emit("vector",
                lambda e, pi=pi: e.scalar_tensor_tensor(
                    colt_t[pi][:, 7:8], colt_t[pi][:, 2:3], -1.0,
                    colt_t[pi][:, 6:7], ALU.mult, ALU.mult),
                rmw=[r_colt[pi]])
        kb.emit("vector",
                lambda e, pi=pi: e.tensor_scalar(xnb16[:, :], xt[pi][:, :],
                                                 colt_t[pi][:, 6:7], colt_t[pi][:, 7:8],
                                                 ALU.mult, ALU.add),
                reads=[r_xt[pi], r_colt[pi]], writes=[r_xn])
        if debug == 3 and i == 0:
            dma("gpsimd", xt_d.ap(), xt[0][:, :], reads=[r_xt[0]], ch="dbg2")
            dma("gpsimd", colt_d.ap(), colt_t[pi][:, :], reads=[r_colt[pi]], ch="dbg2")
            dma("gpsimd", xn_d.ap(), xnb16[:, :], reads=[r_xn], ch="dbg2")
        for g4 in range(4):
            pb = 6 + (g4 % 2)

            def tmm(e, g4=g4, pb=pb):
                last = None
                for k in range(4):
                    c = g4 * 4 + k
                    last = e.matmul(ps[pb][:, k * 128:(k + 1) * 128],
                                    xnb16[:, c * 128:(c + 1) * 128],
                                    identb[:, :], start=True, stop=True)
                return last
            kb.emit("tensor", tmm, reads=[r_xn, r_consts], writes=[r_ps[pb]])
            dst = AP(xnTblk[bi], g4 * 4 * TBA + sub * 128,
                     [[16 * TBA, 128], [TBA, 4], [1, 128]])
            src = AP(ps[pb], 0, [[512, 128], [128, 4], [1, 128]])
            kb.emit("vector", lambda e, dst=dst, src=src: e.tensor_copy(dst, src),
                    reads=[r_ps[pb]], rmw=[r_xnbuf[bi]])
        if sub == 1:
            dma("sync", AP(xnT_h, blk * TBA, [[16 * T, 128], [T, 16], [1, TBA]]),
                AP(xnTblk[bi], 0, [[16 * TBA, 128], [TBA, 16], [1, TBA]]),
                reads=[r_xnbuf[bi]], writes=[r_xnT_h[blk]], ch=f"xnst{bi}")

    prep_jobs = []
    for m, nat, dstT in (("u", din["wup"], wuT_h), ("g", din["wgate"], wgT_h)):
        for ic in range(64):
            def job(m=m, nat=nat, dstT=dstT, ic=ic):
                pp = ic % 4
                dma("gpsimd", natb[pp][:, :],
                    AP(nat, ic * 128 * DM, [[DM, 128], [1, DM]]),
                    writes=[r_natb[pp]], ch=f"natb{pp}")
                for g4 in range(4):
                    pb = 1 + (g4 % 2)

                    def tmms(e, pp=pp, g4=g4, pb=pb):
                        last = None
                        for k in range(4):
                            hc = g4 * 4 + k
                            last = e.matmul(ps[pb][:, k * 128:(k + 1) * 128],
                                            natb[pp][:, hc * 128:(hc + 1) * 128],
                                            identb[:, :], start=True, stop=True)
                        return last
                    kb.emit("tensor", tmms, reads=[r_natb[pp], r_consts],
                            writes=[r_ps[pb]])
                    kb.emit("vector",
                            lambda e, pp=pp, g4=g4, pb=pb: e.tensor_copy(
                                stgb[pp][:, g4 * 512:(g4 + 1) * 512],
                                ps[pb][:, 0:512]),
                            reads=[r_ps[pb]], rmw=[r_stgb[pp]])
                dma("sync", AP(dstT, ic * 2048, [[64 * 2048, 128], [1, 2048]]),
                    stgb[pp][:, :], reads=[r_stgb[pp]], rmw=[r_wT[m]],
                    ch=f"stg{pp}")
            prep_jobs.append(job)
    trip_jobs = []
    for ic in range(64):
        pi = ic % 2
        tsrc = AP(wd_nat, ic * 128, [[IN, DM], [1, 128]])
        tdst = AP(wdT_h, ic * 512, [[64 * DM, 128], [64 * 512, 4], [1, 512]])

        def job(pi=pi, tsrc=tsrc, tdst=tdst):
            kb.emit("scalar",
                    lambda e: e.dma_start(wtp[pi][:, :], tsrc, transpose=True),
                    reads=[r_wnat["d"]], writes=[r_wtp[pi]], dma=True,
                    ch=f"wtp{pi}")
            dma("scalar", tdst,
                AP(wtp[pi], 0, [[2048, 128], [512, 4], [1, 512]]),
                reads=[r_wtp[pi]], rmw=[r_wT["d"]], ch=f"tripst{pi}")
        trip_jobs.append(job)


    def emit_prologue(g, blk):
        bi = blk % 2
        xnb = xnTblk[bi]
        bi = blk % 2
        xnb = xnTblk[bi]
        dma("sync", AP(xnTblk[bi], 0, [[16 * TBA, 128], [TBA, 16], [1, TBA]]),
            AP(xnT_h, blk * TBA, [[16 * T, 128], [T, 16], [1, TBA]]),
            reads=[r_xnT_h[blk]], writes=[r_xnbuf[bi]], ch=f"xnld{bi}")

        pjc = [0]

        def projmm(mm_fn, ev_eng, ev_fn, wr, extra_reads=(), bi=bi):
            pb = 6 + (pjc[0] % 2)
            pjc[0] += 1
            kb.emit("tensor", lambda e, pb=pb, f=mm_fn: f(e, pb),
                    reads=[r_xnbuf[bi], r_wgrp] + list(extra_reads),
                    writes=[r_ps[pb]])
            kb.emit(ev_eng, lambda e, pb=pb, f=ev_fn: f(e, pb),
                    reads=[r_ps[pb], r_vecs, r_vbias], rmw=[wr])

        for hq in range(QHG):
            def mmq(e, pb, hq=hq, xnb=xnb):
                last = None
                for hc in range(16):
                    last = e.matmul(
                        ps[pb][:, 0:TBA],
                        wq_sb[:, hc * 1024 + hq * 128: hc * 1024 + hq * 128 + 128],
                        AP(xnb, hc * TBA, [[16 * TBA, 128], [1, TBA]]),
                        start=(hc == 0), stop=(hc == 15))
                return last

            def evq(e, pb, hq=hq, bi=bi, g=g):
                return e.activation(qb[bi][:, hq * TBA:(hq + 1) * TBA],
                                    ps[pb][:, 0:TBA], AF.Relu,
                                    bias=bq_s[:, g * 8 + hq:g * 8 + hq + 1],
                                    scale=RS128)
            projmm(mmq, "scalar", evq, r_qb[bi])
        for kv in range(KVG):
            def mmk(e, pb, kv=kv, xnb=xnb):
                last = None
                for hc in range(16):
                    last = e.matmul(
                        ps[pb][:, 0:TBA],
                        wk_sb[:, hc * 512 + kv * 128: hc * 512 + kv * 128 + 128],
                        AP(xnb, hc * TBA, [[16 * TBA, 128], [1, TBA]]),
                        start=(hc == 0), stop=(hc == 15))
                return last

            def evk(e, pb, kv=kv, bi=bi, g=g):
                return e.activation(kraw[bi][:, kv * TBA:(kv + 1) * TBA],
                                    ps[pb][:, 0:TBA], AF.Relu,
                                    bias=bk_s[:, g * 4 + kv:g * 4 + kv + 1])
            projmm(mmk, "scalar", evk, r_kraw[bi])
        for tt in range(2):
            def mmv(e, pb, tt=tt, xnb=xnb):
                last = None
                for hc in range(16):
                    lhs = AP(xnb, hc * TBA + tt * 128, [[16 * TBA, 128], [1, 128]])
                    last = e.matmul(ps[pb][:, 0:512], lhs,
                                    wv_sb[:, hc * 512:(hc + 1) * 512],
                                    start=(hc == 0), stop=(hc == 15))
                return last

            def evv(e, pb, tt=tt, bi=bi, g=g):
                return e.tensor_add(vb[bi][:, tt * 512:(tt + 1) * 512],
                                    ps[pb][:, 0:512],
                                    vbias[:, g * 512:(g + 1) * 512])
            projmm(mmv, "vector", evv, r_vb[bi])

        def mmz1(e, pb, xnb=xnb):
            last = None
            for hc in range(16):
                last = e.matmul(ps[pb][0:16, 0:TBA],
                                wg1_sb[:, hc * 16:(hc + 1) * 16],
                                AP(xnb, hc * TBA, [[16 * TBA, 128], [1, TBA]]),
                                start=(hc == 0), stop=(hc == 15))
            return last

        def evz1(e, pb):
            return e.tensor_copy(z1sb[:, :], ps[pb][0:16, 0:TBA])
        projmm(mmz1, "vector", evz1, r_z1)
        for kv in range(KVG):
            def mmz2(e, pb, kv=kv):
                return e.matmul(ps[pb][:, 0:TBA],
                                wg2_sb[:, kv * 128:(kv + 1) * 128],
                                z1sb[:, :], start=True, stop=True)

            def evz2(e, pb, kv=kv, g=g):
                return e.tensor_scalar_add(gt1[:, kv * TBA:(kv + 1) * TBA],
                                           ps[pb][:, 0:TBA],
                                           bg2_s[:, g * 4 + kv:g * 4 + kv + 1])
            pb = 6 + (pjc[0] % 2)
            pjc[0] += 1
            kb.emit("tensor", lambda e, pb=pb, f=mmz2: f(e, pb),
                    reads=[r_z1, r_wgrp], writes=[r_ps[pb]])
            kb.emit("vector", lambda e, pb=pb, f=evz2: f(e, pb),
                    reads=[r_ps[pb], r_vecs], rmw=[r_gt1])

        kb.emit("scalar", lambda e: e.activation(gt2[:, :], gt1[:, :], AF.Abs),
                reads=[r_gt1], writes=[r_gt2])
        kb.emit("scalar",
                lambda e: e.activation(gt2[:, :], gt2[:, :], AF.Exp, scale=-1.0),
                rmw=[r_gt2])
        kb.emit("scalar",
                lambda e: e.activation(gt2[:, :], gt2[:, :], AF.Ln, bias=1.0),
                rmw=[r_gt2])
        kb.emit("vector",
                lambda e: e.tensor_scalar_min(gt1[:, :], gt1[:, :], 0.0),
                rmw=[r_gt1])
        kb.emit("vector",
                lambda e: e.tensor_sub(gt1[:, :], gt1[:, :], gt2[:, :]),
                reads=[r_gt2], rmw=[r_gt1])
        for kv in range(KVG):
            kb.emit("vector",
                    lambda e, kv=kv, bi=bi: e.tensor_tensor_scan(
                        bbuf[:, kv * TBA:(kv + 1) * TBA],
                        scanmask[:, :], gt1[:, kv * TBA:(kv + 1) * TBA],
                        0.0, ALU.mult, ALU.add),
                    reads=[r_gt1, r_consts], rmw=[r_b])
        kb.emit("scalar",
                lambda e, bi=bi: e.activation(expb[bi][:, :], bbuf[:, :],
                                              AF.Exp, scale=1.0 / 16.0),
                reads=[r_b], writes=[r_expb[bi]])
        kb.emit("scalar",
                lambda e, bi=bi: e.activation(expmb[:, :], bbuf[:, :],
                                              AF.Exp, scale=-1.0 / 16.0),
                reads=[r_b], writes=[r_expmb])
        kb.emit("vector",
                lambda e, bi=bi: e.tensor_mul(kfm[bi][:, :], kraw[bi][:, :],
                                              expmb[:, :]),
                reads=[r_kraw[bi], r_expmb], writes=[r_kfm[bi]])

        def qexp(e, bi=bi):
            last = None
            for hq in range(QHG):
                kv = hq // 2
                last = e.tensor_mul(
                    qb[bi][:, hq * TBA:(hq + 1) * TBA],
                    qb[bi][:, hq * TBA:(hq + 1) * TBA],
                    expb[bi][:, kv * TBA:(kv + 1) * TBA])
            return last
        kb.emit("vector", qexp, reads=[r_expb[bi]], rmw=[r_qb[bi]])
        for half in range(2):
            pb = 6 + (half % 2)

            def mmt(e, pb=pb, half=half, bi=bi):
                last = None
                for k in range(4):
                    j = half * 4 + k
                    last = e.matmul(ps[pb][:, k * 128:(k + 1) * 128],
                                    kfm[bi][:, j * 128:(j + 1) * 128],
                                    identb[:, :], start=True, stop=True)
                return last
            kb.emit("tensor", mmt, reads=[r_kfm[bi], r_consts], writes=[r_ps[pb]])

            def evt(e, pb=pb, half=half, bi=bi):
                last = None
                for k in range(4):
                    j = half * 4 + k
                    kv, tt = j // 2, j % 2
                    last = e.tensor_copy(
                        kpm[bi][:, tt * 512 + kv * 128: tt * 512 + kv * 128 + 128],
                        ps[pb][:, k * 128:(k + 1) * 128])
                return last
            kb.emit("vector", evt, reads=[r_ps[pb]], rmw=[r_kpm[bi]])


    def emit_scan(g, blk):
        bi = blk % 2
        tj0 = [0]
        for cl in range(CPB):
            cpar = cl % 2
            tt, rr = cl // 2, (cl % 2) * 64

            def sdelta(e, bi=bi, tt=tt, rr=rr):
                last = None
                for kv in range(KVG):
                    c0 = tt * 512 + kv * 128
                    last = e.matmul(ps[0][:, kv * 128:(kv + 1) * 128],
                                    kpm[bi][rr:rr + 64, c0:c0 + 128],
                                    vb[bi][rr:rr + 64, c0:c0 + 128],
                                    start=False, stop=False,
                                    skip_group_check=True)
                return last
            kb.emit("tensor", sdelta, reads=[r_kpm[bi], r_vb[bi]], rmw=[r_ps[0]])

            def sdecay(e, cl=cl, bi=bi, cpar=cpar):
                last = None
                for kv in range(KVG):
                    cc = kv * TBA + (cl + 1) * CH - 1
                    last = e.tensor_scalar_mul(
                        ps[0][:, kv * 128:(kv + 1) * 128],
                        ps[0][:, kv * 128:(kv + 1) * 128],
                        expb[bi][:, cc:cc + 1])
                return e.tensor_copy(ssnap[1 - cpar][:, :], ps[0][:, :])
            kb.emit("vector", sdecay, reads=[r_expb[bi]], rmw=[r_ps[0]],
                    writes=[r_ssnap[1 - cpar]])

            def atmm(e, cl=cl, bi=bi, cpar=cpar, rr=rr):
                last = None
                for hq in range(QHG):
                    kv = hq // 2
                    last = e.matmul(
                        ps[5][rr:rr + 64, hq * CH:(hq + 1) * CH],
                        kfm[bi][:, kv * TBA + cl * CH: kv * TBA + (cl + 1) * CH],
                        qb[bi][:, hq * TBA + cl * CH: hq * TBA + (cl + 1) * CH],
                        start=True, stop=True,
                        tile_position=(0, rr))
                return last
            kb.emit("tensor", atmm, reads=[r_kfm[bi], r_qb[bi]],
                    writes=[r_ps[5]])

            def atev(e, cpar=cpar, rr=rr):
                last = None
                for hq in range(QHG):
                    last = e.tensor_mul(ats[rr:rr + 64, hq * CH:(hq + 1) * CH],
                                        ps[5][rr:rr + 64, hq * CH:(hq + 1) * CH],
                                        mask64[rr:rr + 64, :])
                return last
            kb.emit("vector", atev, reads=[r_ps[5], r_consts],
                    writes=[r_ats[cpar]])

            ob = (1 + cpar, 3 + cpar)

            def omm(e, bi=bi, cpar=cpar, tt=tt, rr=rr, ob=ob, cl=cl):
                last = None
                for hq in range(QHG):
                    kv = hq // 2
                    pb = ob[hq // 4]
                    cc = (hq % 4) * 128
                    c0 = tt * 512 + kv * 128
                    e.matmul(ps[pb][rr:rr + 64, cc:cc + 128],
                             ats[rr:rr + 64, hq * CH:(hq + 1) * CH],
                             vb[bi][rr:rr + 64, c0:c0 + 128],
                             start=True, stop=False, skip_group_check=True,
                             tile_position=(rr, rr))
                    last = e.matmul(ps[pb][rr:rr + 64, cc:cc + 128],
                                    qb[bi][:, hq * TBA + cl * CH: hq * TBA + (cl + 1) * CH],
                                    ssnap[cpar][:, kv * 128:(kv + 1) * 128],
                                    start=False, stop=True,
                                    skip_group_check=True,
                                    tile_position=(0, rr))
                return last
            kb.emit("tensor", omm,
                    reads=[r_ats[cpar], r_vb[bi], r_qb[bi], r_ssnap[cpar]],
                    writes=[r_ps[ob[0]], r_ps[ob[1]]])

            def msq(e, cl=cl, ob=ob, rr=rr):
                last = None
                m0 = (cl // 2) * QHG
                for hq in range(QHG):
                    pb = ob[hq // 4]
                    cc = (hq % 4) * 128
                    last = e.activation(
                        sqscr[rr:rr + 64, :], ps[pb][rr:rr + 64, cc:cc + 128],
                        AF.Square,
                        accum_out=ms_b[rr:rr + 64, m0 + hq: m0 + hq + 1])
                return last
            kb.emit("scalar", msq, reads=[r_ps[ob[0]], r_ps[ob[1]]],
                    rmw=[r_ms[cl]])

            m0 = (cl // 2) * QHG
            kb.emit("scalar",
                    lambda e, rr=rr, m0=m0: e.activation(
                        rstd_b[rr:rr + 64, m0:m0 + QHG],
                        ms_b[rr:rr + 64, m0:m0 + QHG], AF.Ln,
                        scale=1.0 / HD, bias=epsc[rr:rr + 64, 0:1]),
                    reads=[r_ms[cl]], writes=[r_rstd[cl]])
            kb.emit("scalar",
                    lambda e, rr=rr, m0=m0: e.activation(
                        rstd_b[rr:rr + 64, m0:m0 + QHG],
                        rstd_b[rr:rr + 64, m0:m0 + QHG], AF.Exp,
                        scale=-0.5),
                    rmw=[r_rstd[cl]])

            opar = (cl // 2) % 2

            def oev(e, cl=cl, ob=ob, opar=opar, rr=rr):
                last = None
                m0 = (cl // 2) * QHG
                for hq in range(QHG):
                    pb = ob[hq // 4]
                    cc = (hq % 4) * 128
                    last = e.tensor_scalar_mul(
                        osb[opar][rr:rr + 64, hq * 128:(hq + 1) * 128],
                        ps[pb][rr:rr + 64, cc:cc + 128],
                        rstd_b[rr:rr + 64, m0 + hq: m0 + hq + 1])
                return last
            kb.emit("vector", oev, reads=[r_ps[ob[0]], r_ps[ob[1]],
                                           r_rstd[cl]],
                    rmw=[r_osb[opar]])
            if cl % 2 == 1:
                for hh in range(2):
                    pb = 5

                    def otmm(e, pb=pb, hh=hh, opar=opar):
                        last = None
                        for k in range(4):
                            hq = hh * 4 + k
                            last = e.matmul(
                                ps[pb][:, k * 128:(k + 1) * 128],
                                osb[opar][:, hq * 128:(hq + 1) * 128],
                                identb[:, :], start=True, stop=True)
                        return last
                    kb.emit("tensor", otmm, reads=[r_osb[opar], r_consts],
                            writes=[r_ps[pb]])

                    def otev(e, pb=pb, hh=hh, opar=opar):
                        last = None
                        for k in range(4):
                            hq = hh * 4 + k
                            last = e.tensor_scalar_mul(
                                oTb[opar][:, hq * 128:(hq + 1) * 128],
                                ps[pb][:, k * 128:(k + 1) * 128],
                                gnw_s[:, 0:1])
                        return last
                    kb.emit("vector", otev, reads=[r_ps[pb], r_vecs],
                            rmw=[r_oTb[opar]])
                t0 = blk * TBA + (cl - 1) * CH
                dst = AP(oT_h, (t0 // 128) * 2048 + g * QHG * 128,
                         [[NH * T, 128], [1, QHG * 128]])
                src = AP(oTb[opar], 0, [[QHG * 128, 128], [1, QHG * 128]])
                dma("sync", dst, src, reads=[r_oTb[opar]], rmw=[r_oT_h],
                    ch=f"oT{opar}")


    for g in range(NG):
        if g > 0:
            emit_gweights(g)
        kb.emit("tensor",
                lambda e: e.matmul(ps[0][:, :], zer512[:, 0:128], zer512[:, :],
                                   start=True, stop=False, skip_group_check=True),
                reads=[r_consts], writes=[r_ps[0]])
        kb.emit("vector", lambda e: e.tensor_copy(ssnap[0][:, :], ps[0][:, :]),
                rmw=[r_ps[0]], writes=[r_ssnap[0]])

        prev = None
        for blk in range(NBA):
            emit_prologue(g, blk)
            if prev is not None:
                emit_scan(g, prev)
            prev = blk
        emit_scan(g, prev)

    kb.barrier()
    sb.release()
    sb.mark()
    wo_sb = sb.alloc([128, 16 * 2048], BF16, "wo_sb")
    wtp = [sb.alloc([128, 2048], BF16, f"wtp{i}") for i in range(2)]
    natb = [sb.alloc([128, 2048], BF16, f"natb{i}") for i in range(4)]
    stgb = [sb.alloc([128, 2048], BF16, f"stgb{i}") for i in range(4)]
    oTt = [sb.alloc([128, NH * 128], BF16, f"oTt{i}") for i in range(2)]
    sst = [sb.alloc([128, 2048], F32, f"sst{i}") for i in range(2)]
    r_wo = R("wo")
    r_oTt = [R(), R()]
    r_sst = [R(), R()]
    r_s_h = [R(f"s_h{i}") for i in range(T // 128)]

    for q4 in range(4):
        dma("gpsimd",
            AP(wo_sb, q4 * 4 * 2048, [[16 * 2048, 128], [2048, 4], [1, 2048]]),
            AP(din["wo"], q4 * 4 * 128 * DM, [[DM, 128], [128 * DM, 4], [1, 2048]]),
            rmw=[r_wo], ch="wo")
    for tc in range(T // 128):
        for _ in range(4):
            if trip_jobs:
                trip_jobs.pop(0)()
        for _ in range(8):
            if prep_jobs:
                prep_jobs.pop(0)()
        pi = tc % 2
        dma("gpsimd", AP(oTt[pi], 0, [[NH * 128, 128], [1, NH * 128]]),
            AP(oT_h, tc * 2048, [[NH * T, 128], [1, NH * 128]]),
            reads=[r_oT_h], writes=[r_oTt[pi]], ch=f"oTt{pi}")
        for hb in range(4):
            pb = 6 + (hb % 2)

            def wom(e, pi=pi, hb=hb, pb=pb):
                last = None
                for fc in range(16):
                    last = e.matmul(
                        ps[pb][:, 0:512],
                        oTt[pi][:, fc * 128:(fc + 1) * 128],
                        wo_sb[:, fc * 2048 + hb * 512: fc * 2048 + (hb + 1) * 512],
                        start=(fc == 0), stop=(fc == 15))
                return last
            kb.emit("tensor", wom, reads=[r_oTt[pi], r_wo], writes=[r_ps[pb]])
            kb.emit("vector",
                    lambda e, pi=pi, hb=hb, pb=pb: e.tensor_copy(
                        sst[pi][:, hb * 512:(hb + 1) * 512], ps[pb][:, 0:512]),
                    reads=[r_ps[pb]], rmw=[r_sst[pi]])
        dma("gpsimd", AP(s_h, tc * 128 * DM, [[DM, 128], [1, DM]]), sst[pi][:, :],
            reads=[r_sst[pi]], writes=[r_s_h[tc]], ch=f"sst{pi}")

    while trip_jobs:
        trip_jobs.pop(0)()
    while prep_jobs:
        prep_jobs.pop(0)()

    kb.barrier()
    sb.release()
    sb.mark()
    x1 = sb.alloc([128, 16 * TBF], BF16, "x1")
    x2 = sb.alloc([128, 64 * TBF], BF16, "x2")
    wus = [sb.alloc([128, 16 * 256], BF16, f"wus{i}") for i in range(2)]
    wgs = [sb.alloc([128, 16 * 256], BF16, f"wgs{i}") for i in range(2)]
    wds = [sb.alloc([128, 32 * 256], BF16, f"wds{i}") for i in range(2)]
    sgt = [sb.alloc([128, TBF], BF16, f"sgt{i}") for i in range(2)]
    gst = [sb.alloc([128, 2048], BF16, f"gst{i}") for i in range(4)]
    x1b = sb.alloc([128, 16 * TBF], BF16, "x1b")
    sblk = sb.alloc([128, 2048], BF16, "sblk")
    xblk = [sb.alloc([128, 512], F32, f"xblk{i}") for i in range(2)]
    outst = [sb.alloc([128, 512], F32, f"outst{i}") for i in range(2)]
    dsc = sb.alloc([128, 512], F32, "dsc")

    r_x1 = R()
    rbuf = [{"c": [R() for _ in range(16)], "w": R()} for _ in range(2)]
    r_x2 = R()
    r_x2g = [R() for _ in range(64)]
    r_wus = [R(), R()]
    r_wgs = [R(), R()]
    r_wds = [R(), R()]
    r_sgt = [R(), R()]
    r_gst = [R(), R(), R(), R()]
    r_sblk = R()
    r_xblk = [R(), R()]
    r_outst = [R(), R()]
    r_dsc = R()

    def ffn_head(blk):
        t0 = blk * TBF
        TB = blk % 2
        xa, xo = (x1, x1b) if TB == 0 else (x1b, x1)
        ra, ro = rbuf[TB], rbuf[1 - TB]
        dma("sync", AP(xa, 0, [[16 * TBF, 128], [TBF, 16], [1, TBF]]),
            AP(xnT_h, t0, [[16 * T, 128], [T, 16], [1, TBF]]),
            reads=[r_xnT_h[2 * blk], r_xnT_h[2 * blk + 1]],
            writes=[ra["w"]] + ra["c"], ch=f"x1ld{TB}")

        def sumul(e, xa=xa):
            last = None
            for c in range(16):
                a = xa[:, c * TBF:(c + 1) * TBF]
                last = e.tensor_scalar_mul(a, a, suug_s[:, c:c + 1])
            return last
        kb.emit("vector", sumul, reads=[r_vecs], rmw=ra["c"])
        for cb in range(16):
            pb = 6 + (cb % 2)
            kb.emit("tensor",
                    lambda e, cb=cb, pb=pb, xa=xa: e.matmul(
                        ps[pb][:, 0:TBF], h128b[:, :],
                        xa[:, cb * TBF:(cb + 1) * TBF], start=True, stop=True),
                    reads=[ra["c"][cb], r_consts], writes=[r_ps[pb]])
            kb.emit("vector",
                    lambda e, cb=cb, pb=pb, xa=xa: e.tensor_copy(
                        xa[:, cb * TBF:(cb + 1) * TBF], ps[pb][:, 0:TBF]),
                    reads=[r_ps[pb]], rmw=[ra["c"][cb]])
        bufs = [xa, xo]
        rr_ = [ra, ro]
        for s in range(4):
            h = 1 << s
            src_b, dst_b = bufs[s % 2], bufs[(s + 1) % 2]
            ng = 16 // (2 * h)
            dims = [[16 * TBF, 128], [2 * h * TBF, ng], [TBF, h], [1, TBF]]
            aA = AP(src_b, 0, dims)
            aB = AP(src_b, h * TBF, dims)
            oA = AP(dst_b, 0, dims)
            oB = AP(dst_b, h * TBF, dims)
            rd = ra["c"] if s == 0 else [rr_[s % 2]["w"]]
            wr = [rr_[(s + 1) % 2]["w"]] + (ra["c"] if s == 3 else [])

            def bf1(e, aA=aA, aB=aB, oA=oA, oB=oB):
                e.tensor_add(oA, aA, aB)
                return e.tensor_sub(oB, aA, aB)
            kb.emit("vector", bf1, reads=list(rd), writes=list(wr))

    def ffn_ug(blk):
        TB = blk % 2
        xa = (x1, x1b)[TB]
        ra = rbuf[TB]
        for sup in range(IN // 256):
            pp = sup % 2
            dma("gpsimd", wus[pp][:, :],
                AP(wuT_h, sup * 4096, [[64 * 2048, 128], [1, 4096]]),
                reads=[r_wT["u"]], writes=[r_wus[pp]], ch=f"wus{pp}")
            dma("gpsimd", wgs[pp][:, :],
                AP(wgT_h, sup * 4096, [[64 * 2048, 128], [1, 4096]]),
                reads=[r_wT["g"]], writes=[r_wgs[pp]], ch=f"wgs{pp}")
            for isl in range(2):
                ic = sup * 2 + isl
                pbu = pp * 4 + isl
                pbg = pp * 4 + 2 + isl

                def upm(e, isl=isl, pbu=pbu, pp=pp, xa=xa):
                    last = None
                    for hc in range(16):
                        last = e.matmul(
                            ps[pbu][:, 0:TBF],
                            wus[pp][:, isl * 2048 + hc * 128: isl * 2048 + hc * 128 + 128],
                            xa[:, hc * TBF:(hc + 1) * TBF],
                            start=(hc == 0), stop=(hc == 15))
                    return last
                kb.emit("tensor", upm, reads=[r_wus[pp], ra["w"]],
                        writes=[r_ps[pbu]])

                def gam(e, isl=isl, pbg=pbg, pp=pp, xa=xa):
                    last = None
                    for hc in range(16):
                        last = e.matmul(
                            ps[pbg][:, 0:TBF],
                            wgs[pp][:, isl * 2048 + hc * 128: isl * 2048 + hc * 128 + 128],
                            xa[:, hc * TBF:(hc + 1) * TBF],
                            start=(hc == 0), stop=(hc == 15))
                    return last
                kb.emit("tensor", gam, reads=[r_wgs[pp], ra["w"]],
                        writes=[r_ps[pbg]])
                kb.emit("scalar",
                        lambda e, ic=ic, pbg=pbg, isl=isl: e.activation(
                            sgt[isl][:, :], ps[pbg][:, 0:TBF], AF.Silu,
                            scale=sgate_s[:, ic:ic + 1]),
                        reads=[r_ps[pbg], r_vecs], writes=[r_sgt[isl]])
                dstx2 = AP(x2, ic, [[64 * TBF, 128], [64, TBF]])
                kb.emit("vector",
                        lambda e, ic=ic, pbu=pbu, isl=isl, dstx2=dstx2:
                        e.scalar_tensor_tensor(dstx2, ps[pbu][:, 0:TBF],
                                               sup_s[:, ic:ic + 1],
                                               sgt[isl][:, :],
                                               ALU.mult, ALU.mult),
                        reads=[r_ps[pbu], r_sgt[isl], r_vecs], rmw=[r_x2])

    def ffn_sandwich(blk):
        for hmat in (h128b, bh64b):
            for g4 in range(64):
                pb = g4 % 2

                def fmm2(e, g4=g4, pb=pb, hmat=hmat):
                    last = None
                    for k in range(4):
                        j = g4 * 4 + k
                        last = e.matmul(ps[pb][:, k * 128:(k + 1) * 128],
                                        x2[:, j * 128:(j + 1) * 128],
                                        hmat[:, :], start=True, stop=True)
                    return last
                kb.emit("tensor", fmm2, reads=[r_x2, r_x2g[g4], r_consts],
                        writes=[r_ps[pb]])
                kb.emit("vector",
                        lambda e, g4=g4, pb=pb: e.tensor_copy(
                            x2[:, g4 * 512:(g4 + 1) * 512], ps[pb][:, 0:512]),
                        reads=[r_ps[pb]], rmw=[r_x2g[g4]])

    def ffn_down(blk):
        for ob in range(4):
            for pc in range(4):
                wpp = (ob * 4 + pc) % 2
                dma("sync", wds[wpp][:, :],
                    AP(wdT_h, ob * 64 * 512 + pc * 16 * 512,
                       [[64 * DM, 128], [1, 16 * 512]]),
                    reads=[r_wT["d"]], writes=[r_wds[wpp]], ch=f"wds{wpp}")
                for tc4 in range(TBF // 128):
                    pb = (ob % 2) * 4 + tc4

                    def dwm(e, pc=pc, wpp=wpp, tc4=tc4, pb=pb):
                        last = None
                        for icl in range(16):
                            ic = pc * 16 + icl
                            lhs = AP(x2, tc4 * 128 * 64 + ic,
                                     [[64 * TBF, 128], [64, 128]])
                            last = e.matmul(
                                ps[pb][:, 0:512], lhs,
                                AP(wds[wpp], icl * 512,
                                   [[16 * 512, 128], [1, 512]]),
                                start=(ic == 0), stop=(ic == 63))
                        return last
                    rds = [r_x2, r_wds[wpp]] + r_x2g[tc4 * 16:(tc4 + 1) * 16]
                    if pc == 0:
                        kb.emit("tensor", dwm, reads=rds, writes=[r_ps[pb]])
                    else:
                        kb.emit("tensor", dwm, reads=rds, rmw=[r_ps[pb]])
            for tc4 in range(TBF // 128):
                pb = (ob % 2) * 4 + tc4
                kb.emit("vector",
                        lambda e, ob=ob, pb=pb: e.scalar_tensor_tensor(
                            dsc[:, 0:512], ps[pb][:, 0:512], 1.0,
                            wsdb[:, ob * 512:(ob + 1) * 512], ALU.mult,
                            ALU.mult),
                        reads=[r_ps[pb], r_wsdb], writes=[r_dsc])
                kb.emit("scalar",
                        lambda e, ob=ob, tc4=tc4: e.activation(
                            gst[tc4][:, ob * 512:(ob + 1) * 512], dsc[:, 0:512],
                            AF.Silu),
                        reads=[r_dsc], rmw=[r_gst[tc4]])
        for tc4 in range(TBF // 128):
            tc = blk * (TBF // 128) + tc4
            dma("gpsimd", sblk[:, :], AP(s_h, tc * 128 * DM, [[DM, 128], [1, DM]]),
                reads=[r_s_h[tc]], writes=[r_sblk], ch="sblk")
            if fulldbg:
                dma("sync", AP(g_d, tc * 128 * DM, [[DM, 128], [1, DM]]),
                    gst[tc4][:, :], reads=[r_gst[tc4]], ch="dbg")
            for h4 in range(4):
                po = h4 % 2
                dma("gpsimd", xblk[po][:, :],
                    AP(din["x"], tc * 128 * DM + h4 * 512, [[DM, 128], [1, 512]]),
                    writes=[r_xblk[po]], ch=f"xblk{po}")
                kb.emit("vector",
                        lambda e, tc4=tc4, h4=h4, po=po: e.tensor_mul(
                            outst[po][:, :], gst[tc4][:, h4 * 512:(h4 + 1) * 512],
                            sblk[:, h4 * 512:(h4 + 1) * 512]),
                        reads=[r_gst[tc4], r_sblk], writes=[r_outst[po]])
                kb.emit("vector",
                        lambda e, h4=h4, po=po: e.tensor_add(
                            outst[po][:, :], outst[po][:, :], xblk[po][:, :]),
                        reads=[r_xblk[po]], rmw=[r_outst[po]])
                dma("gpsimd", AP(y, tc * 128 * DM + h4 * 512, [[DM, 128], [1, 512]]),
                    outst[po][:, :], reads=[r_outst[po]], ch=f"y{po}")

    for blk in range(NBF):
        if blk == 0:
            ffn_head(0)
        ffn_ug(blk)
        ffn_sandwich(blk)
        if blk + 1 < NBF:
            ffn_head(blk + 1)
        ffn_down(blk)

    with nc.Block() as block:
        kb.emit_all(nc, block)
    return nc


_NC_CACHE = None


def kernel(**inputs):
    global _NC_CACHE
    if _NC_CACHE is None:
        _NC_CACHE = build_program()
    nc = _NC_CACHE
    x = np.ascontiguousarray(inputs["L0_attn_hidden_states"], dtype=np.float32)
    base = {
        "wq": np.ascontiguousarray(inputs["L0_attn_Wq"], dtype=np.float32),
        "bq": np.ascontiguousarray(inputs["L0_attn_bq"], dtype=np.float32).reshape(1, -1),
        "wk": np.ascontiguousarray(inputs["L0_attn_Wk"], dtype=np.float32),
        "bk": np.ascontiguousarray(inputs["L0_attn_bk"], dtype=np.float32).reshape(1, -1),
        "wv": np.ascontiguousarray(inputs["L0_attn_Wv"], dtype=np.float32),
        "bv": np.ascontiguousarray(inputs["L0_attn_bv"], dtype=np.float32).reshape(1, -1),
        "wg1": np.ascontiguousarray(inputs["L0_attn_Wg1"], dtype=np.float32),
        "wg2": np.ascontiguousarray(inputs["L0_attn_Wg2"], dtype=np.float32),
        "bg2": np.ascontiguousarray(inputs["L0_attn_bg2"], dtype=np.float32).reshape(1, -1),
        "gnw": np.ascontiguousarray(inputs["L0_attn_g_norm_w"], dtype=np.float32).reshape(1, -1),
        "wo": np.ascontiguousarray(inputs["L0_attn_Wo"], dtype=np.float32),
        "su_ug": np.ascontiguousarray(inputs["L0_ffn_SU_ug"], dtype=np.float32).reshape(1, -1),
        "su_dp": np.ascontiguousarray(inputs["L0_ffn_SU_dp"], dtype=np.float32).reshape(1, -1),
        "wsug": np.ascontiguousarray(inputs["L0_ffn_Wscale_ug"], dtype=np.float32).reshape(1, -1),
        "wsdp": np.ascontiguousarray(inputs["L0_ffn_Wscale_dp"], dtype=np.float32).reshape(1, -1),
        "wup": np.ascontiguousarray(inputs["L0_ffn_W_up"], dtype=np.float32),
        "wgate": np.ascontiguousarray(inputs["L0_ffn_W_gate"], dtype=np.float32),
        "wdown": np.ascontiguousarray(inputs["L0_ffn_W_down"], dtype=np.float32),
    }
    in_maps = []
    for c in range(8):
        m = dict(base)
        m["x"] = np.ascontiguousarray(x[c])
        in_maps.append(m)
    res = run_bass_kernel_spmd(nc, in_maps, core_ids=list(range(8)))
    return np.stack([np.asarray(r["y"], dtype=np.float32) for r in res.results], axis=0)


# version 26
# speedup vs baseline: 1.1695x; 1.0442x over previous
import numpy as np
import concourse.bass as bass
import concourse.mybir as mybir
from concourse.bass_utils import run_bass_kernel_spmd

F32 = mybir.dt.float32
BF16 = mybir.dt.bfloat16
AF = mybir.ActivationFunctionType
ALU = mybir.AluOpType
AX = mybir.AxisListType
AP = bass.AP

T = 2048
DM = 2048
HKV = 8
NH = 16
HD = 128
CH = 64
IN = 8192
NG = 2
KVG = HKV // NG
QHG = NH // NG
TBA = 256
NBA = T // TBA
CPB = TBA // CH
TBF = 512
NBF = T // TBF
RS128 = float(HD) ** -0.5
EPS = 1e-5

ENGS = ("tensor", "vector", "scalar", "gpsimd", "sync")


def _hadamard(n):
    i = np.arange(n)
    a = i[:, None] & i[None, :]
    pc = np.zeros((n, n), dtype=np.int64)
    for b in range(14):
        pc += (a >> b) & 1
    return np.where(pc % 2 == 0, 1.0, -1.0).astype(np.float32)


H128_NP = _hadamard(128)
BH16_NP = np.kron(np.eye(8, dtype=np.float32), _hadamard(16))
BH64_NP = np.kron(np.eye(2, dtype=np.float32), _hadamard(64))
IDENT_NP = np.eye(128, dtype=np.float32)
IDF_NP = np.eye(64, dtype=np.float32)
MASK64_NP = np.tile((np.arange(64)[None, :] >= np.arange(64)[:, None]).astype(np.float32), (2, 1))
ONES_NP = np.ones((1, 128), dtype=np.float32)


class Step:
    __slots__ = ("eng", "fn", "dma", "cval", "gid", "deps", "bg", "ch")


class Res:
    __slots__ = ("writer", "readers", "name")

    def __init__(self, name=""):
        self.writer = None
        self.readers = []
        self.name = name


class KB:
    def __init__(self):
        self.steps = []
        self.cnt = {e: 0 for e in ENGS}
        self.dcnt = {}
        self.frontier_deps = []
        self.frontier_pending = set()

    def emit(self, eng, fn, reads=(), writes=(), rmw=(), deps=(), dma=False,
             bg=False, ch=None):
        d = []
        for r in reads:
            if r.writer is not None:
                d.append(r.writer)
        for w in tuple(writes) + tuple(rmw):
            if w.writer is not None:
                d.append(w.writer)
            d.extend(w.readers)
        d.extend(deps)
        if eng in self.frontier_pending:
            d.extend(self.frontier_deps)
            self.frontier_pending.discard(eng)
        st = Step()
        st.eng = eng
        st.fn = fn
        st.dma = dma
        st.bg = bg
        st.gid = len(self.steps)
        if dma:
            st.ch = ch if ch is not None else f"_q_{eng}"
            self.dcnt[st.ch] = self.dcnt.get(st.ch, 0) + 1
            st.cval = 16 * self.dcnt[st.ch]
        else:
            st.ch = None
            self.cnt[eng] += 1
            st.cval = self.cnt[eng]
        for r in reads:
            r.readers.append(st)
        for w in tuple(writes) + tuple(rmw):
            w.writer = st
            w.readers = []
        st.deps = []
        seen = set()
        for s in d:
            if s is None or s.gid in seen or s is st:
                continue
            seen.add(s.gid)
            assert s.gid < st.gid
            st.deps.append(s)
        self.steps.append(st)
        return st

    def barrier(self):
        last = {}
        for st in self.steps:
            if st.bg:
                continue
            last[(st.eng, st.dma)] = st
        self.frontier_deps = list(last.values())
        self.frontier_pending = set(ENGS)

    def emit_all(self, nc, block):
        sems = {}
        for e in ENGS:
            if self.cnt[e]:
                sems[("c", e)] = nc.semaphore(f"sc_{e}").__enter__()
        for chn in self.dcnt:
            sems[("d", chn)] = nc.semaphore(f"sd_{chn}").__enter__()
        for h in sems.values():
            nc.gpsimd.sem_clear(h)
        nc.all_engine_barrier()
        decos = {"tensor": block.tensor, "vector": block.vector,
                 "scalar": block.scalar, "gpsimd": block.gpsimd,
                 "sync": block.sync}
        total_dma = {chn: 16 * n for chn, n in self.dcnt.items()}

        def mk(ename):
            def body(e):
                water = {}
                for st in self.steps:
                    if st.eng != ename:
                        continue
                    for dep in st.deps:
                        if (dep.eng == ename == "tensor"
                                and not dep.dma and not st.dma):
                            continue
                        key = ("d", dep.ch) if dep.dma else ("c", dep.eng)
                        if water.get(key, 0) >= dep.cval:
                            continue
                        water[key] = dep.cval
                        e.wait_ge(sems[key], dep.cval)
                    last = st.fn(e)
                    if st.dma:
                        last.then_inc(sems[("d", st.ch)], 16)
                    else:
                        last.then_inc(sems[("c", ename)], 1)
                if ename == "sync":
                    for chn, tot in total_dma.items():
                        e.wait_ge(sems[("d", chn)], tot)
            return body

        for ename in ENGS:
            decos[ename](mk(ename))


class Sbuf:
    BASE = 16544
    TOP = 229340

    def __init__(self, nc):
        self.nc = nc
        self.off = self.BASE
        self.marks = []
        self.peak = self.BASE
        self.n = 0

    def alloc(self, shape, dtype, name=None):
        esz = 4 if dtype == F32 else 2
        nb = shape[1] * esz
        off = (self.off + 31) // 32 * 32
        assert off + nb <= self.TOP, f"SBUF overflow: {name} ({nb}B at {off})"
        self.off = off + nb
        self.peak = max(self.peak, self.off)
        self.n += 1
        self.offsets = getattr(self, "offsets", {})
        self.offsets[name or f"sb{self.n}"] = off
        return self.nc.alloc_sbuf_tensor_at(name or f"sb{self.n}", list(shape),
                                            dtype, offset=off)

    def mark(self):
        self.marks.append(self.off)

    def release(self):
        self.off = self.marks.pop()


def build_program(debug=False):
    nc = bass.Bass("TRN2", target_bir_lowering=False)
    kb = KB()

    din = {}
    for name, shape in [
        ("x", [T, DM]), ("wq", [DM, DM]), ("bq", [1, DM]), ("wk", [DM, 1024]),
        ("bk", [1, 1024]), ("wv", [DM, 1024]), ("bv", [1, 1024]),
        ("wg1", [DM, 16]), ("wg2", [16, 1024]), ("bg2", [1, 1024]),
        ("gnw", [1, HD]), ("wo", [DM, DM]), ("su_ug", [1, DM]), ("su_dp", [1, IN]),
        ("wsug", [1, 2 * IN]), ("wsdp", [1, DM]), ("wup", [IN, DM]),
        ("wgate", [IN, DM]), ("wdown", [DM, IN]),
    ]:
        din[name] = nc.dram_tensor(name, shape, F32, kind="ExternalInput")
    y = nc.dram_tensor("y", [T, DM], F32, kind="ExternalOutput")

    c_h128 = nc.inline_tensor(H128_NP, name="c_h128")
    c_bh16 = nc.inline_tensor(BH16_NP, name="c_bh16")
    c_bh64 = nc.inline_tensor(BH64_NP, name="c_bh64")
    c_ident = nc.inline_tensor(IDENT_NP, name="c_ident")
    c_idf = nc.inline_tensor(IDF_NP, name="c_idf")
    c_mask = nc.inline_tensor(MASK64_NP, name="c_mask")
    c_ones = nc.inline_tensor(ONES_NP, name="c_ones")

    dkind = "ExternalOutput" if debug else "Internal"
    fulldbg = (debug == True)
    xnT_h = nc.dram_tensor("xnT_h", [128, 16 * T], BF16, kind=dkind)
    oT_h = nc.dram_tensor("oT_h", [128, NH * T], BF16, kind=dkind)
    s_h = nc.dram_tensor("s_h", [T, DM], F32, kind=dkind)
    if fulldbg:
        x1_d = nc.dram_tensor("x1_d", [128, 16 * T], BF16, kind="ExternalOutput")
        g_d = nc.dram_tensor("g_d", [T, DM], BF16, kind="ExternalOutput")
    if debug == 3:
        xt_d = nc.dram_tensor("xt_d", [128, DM], F32, kind="ExternalOutput")
        colt_d = nc.dram_tensor("colt_d", [128, 8], F32, kind="ExternalOutput")
        xn_d = nc.dram_tensor("xn_d", [128, DM], BF16, kind="ExternalOutput")
    wu_nat = nc.dram_tensor("wu_nat", [IN, DM], BF16, kind="Internal")
    wg_nat = nc.dram_tensor("wg_nat", [IN, DM], BF16, kind="Internal")
    wd_nat = nc.dram_tensor("wd_nat", [DM, IN], BF16, kind="Internal")
    wuT_h = nc.dram_tensor("wuT_h", [128, 64 * 2048], BF16, kind="Internal")
    wgT_h = nc.dram_tensor("wgT_h", [128, 64 * 2048], BF16, kind="Internal")
    wdT_h = nc.dram_tensor("wdT_h", [128, 64 * DM], BF16, kind="Internal")

    sb = Sbuf(nc)
    h128b = sb.alloc([128, 128], BF16, "h128b")
    bh16b = sb.alloc([128, 128], BF16, "bh16b")
    bh64b = sb.alloc([128, 128], BF16, "bh64b")
    identb = sb.alloc([128, 128], BF16, "identb")
    identf = sb.alloc([64, 64], F32, "identf")
    mask64 = sb.alloc([128, 64], F32, "mask64")
    onesf = sb.alloc([1, 128], F32, "onesf")
    zer512 = sb.alloc([128, 512], BF16, "zer512")
    scanmask = sb.alloc([128, TBA], F32, "scanmask")
    bq_s = sb.alloc([128, 16], F32, "bq_s")
    bk_s = sb.alloc([128, 8], F32, "bk_s")
    bg2_s = sb.alloc([128, 8], F32, "bg2_s")
    gnw_s = sb.alloc([128, 1], F32, "gnw_s")
    suug_s = sb.alloc([128, 16], F32, "suug_s")
    sup_s = sb.alloc([128, 64], F32, "sup_s")
    sgate_s = sb.alloc([128, 64], F32, "sgate_s")
    wsdb = sb.alloc([128, 2048], F32, "wsdb")
    ms_b = sb.alloc([128, QHG * CPB // 2], F32, "ms_b")
    rstd_b = sb.alloc([128, QHG * CPB // 2], F32, "rstd_b")
    colt_t = [sb.alloc([128, 8], F32, f"colt{i}") for i in range(2)]
    epsc = sb.alloc([128, 1], F32, "epsc")
    sqscr = sb.alloc([128, 128], BF16, "sqscr")

    R = Res
    r_consts = R("consts")
    r_vecs = R("vecs")
    r_vbias = R("vbias")
    r_wsdb = R("wsdb")

    ps = [nc.alloc_psum_tensor(f"ps{i}", [128, 512], F32) for i in range(8)]
    r_ps = [R(f"ps{i}") for i in range(8)]

    def dma(q, dst, src, reads=(), writes=(), rmw=(), deps=(), slow=False,
            bg=False, ch=None):
        if slow:
            fn = lambda e: e.dma_start(dst, src, allow_slow_non_contiguous=True)
        else:
            fn = lambda e: e.dma_start(dst, src)
        return kb.emit(q, fn, reads=reads, writes=writes, rmw=rmw, deps=deps,
                       dma=True, bg=bg, ch=ch)

    for dst, src in [(h128b, c_h128), (bh16b, c_bh16), (bh64b, c_bh64),
                     (identb, c_ident)]:
        dma("gpsimd", dst[:, :], src.ap(), rmw=[r_consts], ch="consts")
    dma("gpsimd", identf[:, :], c_idf.ap(), rmw=[r_consts], ch="consts")
    dma("gpsimd", mask64[:, :], c_mask.ap(), rmw=[r_consts], ch="consts")
    dma("gpsimd", onesf[:, :], c_ones.ap(), rmw=[r_consts], ch="consts")
    kb.emit("vector", lambda e: e.memset(zer512[:, :], 0.0), rmw=[r_consts])
    kb.emit("vector", lambda e: e.memset(epsc[:, :], EPS), rmw=[r_consts])
    kb.emit("vector", lambda e: e.memset(scanmask[:, :], 1.0), rmw=[r_consts])
    kb.emit("vector",
            lambda e: e.memset(AP(scanmask, 0, [[TBA, 128], [CH, CPB]]), 0.0),
            rmw=[r_consts])

    sb.mark()
    vrow = sb.alloc([64, 128], F32, "vrow")
    bvrow = sb.alloc([1, 1024], F32, "bvrow")
    wsdrow = sb.alloc([1, 2048], F32, "wsdrow")

    sb.release()
    sb.mark()
    xt = [sb.alloc([128, DM], F32, f"xt{i}") for i in range(2)]
    xsq = sb.alloc([128, DM], F32, "xsq")
    xnb16 = sb.alloc([128, DM], BF16, "xnb16")
    xnTblk = [sb.alloc([128, 16 * TBA], BF16, f"xnTblk{i}") for i in range(2)]

    r_xt = [R(), R()]
    r_xsq = R()
    r_xn = R()
    r_xnbuf = [R(), R()]
    r_xnT_h = [R(f"xnTh{b}") for b in range(NBA)]
    r_colt = [R(), R()]
    r_wtp = [R(), R()]
    r_natb = [R() for _ in range(4)]
    r_stgb = [R() for _ in range(4)]
    r_wT = {m: R(f"wT_{m}") for m in ("u", "g", "d")}

    vbias = sb.alloc([128, 1024], F32, "vbias")
    wq_sb = sb.alloc([128, 16 * 1024], BF16, "wq_sb")
    wk_sb = sb.alloc([128, 16 * 512], BF16, "wk_sb")
    wv_sb = sb.alloc([128, 16 * 512], BF16, "wv_sb")
    wg1_sb = sb.alloc([128, 16 * 16], BF16, "wg1_sb")
    wg2_sb = sb.alloc([16, 512], BF16, "wg2_sb")
    qb = [sb.alloc([128, QHG * TBA], BF16, f"qb{i}") for i in range(2)]
    kraw = [sb.alloc([128, KVG * TBA], BF16, f"kraw{i}") for i in range(2)]
    kfm = [sb.alloc([128, KVG * TBA], BF16, f"kfm{i}") for i in range(2)]
    kpm = [sb.alloc([128, KVG * TBA], BF16, f"kpm{i}") for i in range(2)]
    vb = [sb.alloc([128, KVG * TBA], BF16, f"vb{i}") for i in range(2)]
    gt1 = sb.alloc([128, KVG * TBA], F32, "gt1")
    gt2 = sb.alloc([128, KVG * TBA], F32, "gt2")
    z1sb = sb.alloc([16, TBA], BF16, "z1sb")
    bbuf = sb.alloc([128, KVG * TBA], F32, "bbuf")
    expb = [sb.alloc([128, KVG * TBA], F32, f"expb{i}") for i in range(2)]
    expmb = sb.alloc([128, KVG * TBA], F32, "expmb")
    qs = [sb.alloc([128, QHG * CH], BF16, f"qs{i}") for i in range(2)]
    ssnap = [sb.alloc([128, KVG * 128], BF16, f"ssnap{i}") for i in range(2)]
    ats = sb.alloc([128, QHG * CH], BF16, "ats")
    osb = [sb.alloc([128, QHG * 128], BF16, f"osb{i}") for i in range(2)]
    oTb = [sb.alloc([128, QHG * 128], BF16, f"oTb{i}") for i in range(2)]

    r_wgrp = R("wgrp")
    r_qb = [R(), R()]
    r_kraw = [R(), R()]
    r_kfm = [R(), R()]
    r_kpm = [R(), R()]
    r_vb = [R(), R()]
    r_gt1 = R()
    r_gt2 = R()
    r_z1 = R()
    r_b = R()
    r_expb = [R(), R()]
    r_expmb = R()
    r_qs = [R(), R()]
    r_ssnap = [R(), R()]
    r_ats = [R(), R()]
    r_osb = [R(), R()]
    r_oTb = [R(), R()]
    r_ms = [R() for _ in range(4)]
    r_rstd = [R() for _ in range(4)]
    r_oT_h = R("oT_h")

    def vecload(dst_ap, src_t, src_off, n, scale=None, mult_ps=False):
        rows = n // 128
        dma("gpsimd", vrow[0:rows, :], AP(src_t, src_off, [[128, rows], [1, 128]]),
            reads=[r_consts], rmw=[r_vecs], ch="vec")
        kb.emit("tensor",
                lambda e, rows=rows: e.matmul(ps[6][:, 0:rows], vrow[0:rows, :],
                                              identf[0:rows, 0:rows],
                                              start=True, stop=True),
                reads=[r_vecs, r_consts], writes=[r_ps[6]])
        if mult_ps:
            fn = lambda e, rows=rows: e.tensor_mul(dst_ap, dst_ap, ps[6][:, 0:rows])
        elif scale is not None:
            fn = lambda e, rows=rows, s=float(scale): e.tensor_scalar_mul(
                dst_ap, ps[6][:, 0:rows], s)
        else:
            fn = lambda e, rows=rows: e.tensor_copy(dst_ap, ps[6][:, 0:rows])
        kb.emit("vector", fn, reads=[r_ps[6]], rmw=[r_vecs])

    c2048 = 1.0 / np.sqrt(2048.0)
    vecload(bq_s[:, :], din["bq"], 0, DM, scale=RS128)
    vecload(bk_s[:, :], din["bk"], 0, 1024)
    vecload(bg2_s[:, :], din["bg2"], 0, 1024)
    vecload(gnw_s[:, :], din["gnw"], 0, HD)
    vecload(suug_s[:, :], din["su_ug"], 0, DM)
    vecload(sup_s[:, :], din["wsug"], 0, IN, scale=c2048)
    vecload(sup_s[:, :], din["su_dp"], 0, IN, mult_ps=True)
    vecload(sgate_s[:, :], din["wsug"], IN, IN, scale=c2048)

    dma("gpsimd", bvrow[:, :], din["bv"].ap(), rmw=[r_vbias], ch="bvrow")
    dma("gpsimd", wsdrow[:, :], din["wsdp"].ap(), rmw=[r_wsdb], ch="wsdrow")
    c8192 = 1.0 / np.sqrt(8192.0)
    for i in range(2):
        kb.emit("tensor",
                lambda e, i=i: e.matmul(ps[7][:, 0:512], onesf[:, :],
                                        bvrow[:, i * 512:(i + 1) * 512],
                                        start=True, stop=True),
                reads=[r_vbias], writes=[r_ps[7]])
        kb.emit("vector",
                lambda e, i=i: e.tensor_copy(vbias[:, i * 512:(i + 1) * 512],
                                             ps[7][:, 0:512]),
                reads=[r_ps[7]], rmw=[r_vbias])
    for i in range(4):
        kb.emit("tensor",
                lambda e, i=i: e.matmul(ps[7][:, 0:512], onesf[:, :],
                                        wsdrow[:, i * 512:(i + 1) * 512],
                                        start=True, stop=True),
                reads=[r_wsdb], writes=[r_ps[7]])
        kb.emit("vector",
                lambda e, i=i: e.tensor_scalar_mul(wsdb[:, i * 512:(i + 1) * 512],
                                                   ps[7][:, 0:512], c8192),
                reads=[r_ps[7]], rmw=[r_wsdb])

    def emit_gweights(g):
        for q4 in range(4):
            dma("gpsimd",
                AP(wq_sb, q4 * 4 * 1024, [[16 * 1024, 128], [1024, 4], [1, 1024]]),
                AP(din["wq"], q4 * 4 * 128 * DM + g * 1024,
                   [[DM, 128], [128 * DM, 4], [1, 1024]]),
                rmw=[r_wgrp], ch="gw")
            dma("gpsimd",
                AP(wk_sb, q4 * 4 * 512, [[16 * 512, 128], [512, 4], [1, 512]]),
                AP(din["wk"], q4 * 4 * 128 * 1024 + g * 512,
                   [[1024, 128], [128 * 1024, 4], [1, 512]]),
                rmw=[r_wgrp], ch="gw")
            dma("gpsimd",
                AP(wv_sb, q4 * 4 * 512, [[16 * 512, 128], [512, 4], [1, 512]]),
                AP(din["wv"], q4 * 4 * 128 * 1024 + g * 512,
                   [[1024, 128], [128 * 1024, 4], [1, 512]]),
                rmw=[r_wgrp], ch="gw")
        dma("gpsimd", AP(wg1_sb, 0, [[256, 128], [16, 16], [1, 16]]),
            AP(din["wg1"], 0, [[16, 128], [128 * 16, 16], [1, 16]]),
            rmw=[r_wgrp], ch="gw")
        dma("gpsimd", wg2_sb[:, :],
            AP(din["wg2"], g * 512, [[1024, 16], [1, 512]]),
            rmw=[r_wgrp], ch="gw")

    emit_gweights(0)

    r_wnat = {m: R(f"wnat_{m}") for m in ("u", "g", "d")}
    for m, src, dst in (("d", din["wdown"], wd_nat),):
        rows, cols = src.shape
        nsp = 8
        rr = rows // nsp
        for r8 in range(nsp):
            dma("gpsimd", AP(dst, r8 * rr * cols, [[cols, rr], [1, cols]]),
                AP(src, r8 * rr * cols, [[cols, rr], [1, cols]]),
                rmw=[r_wnat[m]], bg=True, ch="wnat_" + m)

    kb.barrier()

    for i in range(T // 128):
        pi = i % 2
        blk = i // 2
        sub = i % 2
        bi = blk % 2
        dma("sync", xt[pi][:, :], AP(din["x"], i * 128 * DM, [[DM, 128], [1, DM]]),
            writes=[r_xt[pi]], ch=f"xt{pi}")
        kb.emit("vector",
                lambda e, pi=pi: e.reduce_sum(colt_t[pi][:, 0:1], xt[pi][:, :], axis=AX.X),
                reads=[r_xt[pi]], rmw=[r_colt[pi]])
        kb.emit("vector",
                lambda e, pi=pi: e.scalar_tensor_tensor(
                    xsq[:, :], xt[pi][:, :], 1.0, xt[pi][:, :], ALU.mult, ALU.mult,
                    accum_out=colt_t[pi][:, 1:2]),
                reads=[r_xt[pi]], writes=[r_xsq], rmw=[r_colt[pi]])

        def stats1(e, pi=pi):
            e.tensor_scalar_mul(colt_t[pi][:, 2:3], colt_t[pi][:, 0:1], 1.0 / DM)
            return e.tensor_scalar_mul(colt_t[pi][:, 3:4], colt_t[pi][:, 1:2], 1.0 / DM)
        kb.emit("vector", stats1, rmw=[r_colt[pi]])
        kb.emit("vector",
                lambda e, pi=pi: e.scalar_tensor_tensor(
                    colt_t[pi][:, 4:5], colt_t[pi][:, 2:3], -1.0,
                    colt_t[pi][:, 2:3], ALU.mult, ALU.mult),
                rmw=[r_colt[pi]])
        kb.emit("vector",
                lambda e, pi=pi: e.tensor_add(colt_t[pi][:, 5:6],
                                              colt_t[pi][:, 3:4],
                                              colt_t[pi][:, 4:5]),
                rmw=[r_colt[pi]])
        kb.emit("scalar",
                lambda e, pi=pi: e.activation(colt_t[pi][:, 6:7],
                                              colt_t[pi][:, 5:6], AF.Ln,
                                              bias=epsc[:, 0:1]),
                rmw=[r_colt[pi]])
        kb.emit("scalar",
                lambda e, pi=pi: e.activation(colt_t[pi][:, 6:7],
                                              colt_t[pi][:, 6:7], AF.Exp,
                                              scale=-0.5),
                rmw=[r_colt[pi]])
        kb.emit("vector",
                lambda e, pi=pi: e.scalar_tensor_tensor(
                    colt_t[pi][:, 7:8], colt_t[pi][:, 2:3], -1.0,
                    colt_t[pi][:, 6:7], ALU.mult, ALU.mult),
                rmw=[r_colt[pi]])
        kb.emit("vector",
                lambda e, pi=pi: e.tensor_scalar(xnb16[:, :], xt[pi][:, :],
                                                 colt_t[pi][:, 6:7], colt_t[pi][:, 7:8],
                                                 ALU.mult, ALU.add),
                reads=[r_xt[pi], r_colt[pi]], writes=[r_xn])
        if debug == 3 and i == 0:
            dma("gpsimd", xt_d.ap(), xt[0][:, :], reads=[r_xt[0]], ch="dbg2")
            dma("gpsimd", colt_d.ap(), colt_t[pi][:, :], reads=[r_colt[pi]], ch="dbg2")
            dma("gpsimd", xn_d.ap(), xnb16[:, :], reads=[r_xn], ch="dbg2")
        for g4 in range(4):
            pb = 6 + (g4 % 2)

            def tmm(e, g4=g4, pb=pb):
                last = None
                for k in range(4):
                    c = g4 * 4 + k
                    last = e.matmul(ps[pb][:, k * 128:(k + 1) * 128],
                                    xnb16[:, c * 128:(c + 1) * 128],
                                    identb[:, :], start=True, stop=True)
                return last
            kb.emit("tensor", tmm, reads=[r_xn, r_consts], writes=[r_ps[pb]])
            dst = AP(xnTblk[bi], g4 * 4 * TBA + sub * 128,
                     [[16 * TBA, 128], [TBA, 4], [1, 128]])
            src = AP(ps[pb], 0, [[512, 128], [128, 4], [1, 128]])
            kb.emit("vector", lambda e, dst=dst, src=src: e.tensor_copy(dst, src),
                    reads=[r_ps[pb]], rmw=[r_xnbuf[bi]])
        if sub == 1:
            dma("sync", AP(xnT_h, blk * TBA, [[16 * T, 128], [T, 16], [1, TBA]]),
                AP(xnTblk[bi], 0, [[16 * TBA, 128], [TBA, 16], [1, TBA]]),
                reads=[r_xnbuf[bi]], writes=[r_xnT_h[blk]], ch=f"xnst{bi}")

    prep_jobs = []
    for m, nat, dstT in (("u", din["wup"], wuT_h), ("g", din["wgate"], wgT_h)):
        for ic in range(64):
            def job(m=m, nat=nat, dstT=dstT, ic=ic):
                pp = ic % 4
                dma("gpsimd", natb[pp][:, :],
                    AP(nat, ic * 128 * DM, [[DM, 128], [1, DM]]),
                    writes=[r_natb[pp]], ch=f"natb{pp}")
                for g4 in range(4):
                    pb = 1 + (g4 % 2)

                    def tmms(e, pp=pp, g4=g4, pb=pb):
                        last = None
                        for k in range(4):
                            hc = g4 * 4 + k
                            last = e.matmul(ps[pb][:, k * 128:(k + 1) * 128],
                                            natb[pp][:, hc * 128:(hc + 1) * 128],
                                            identb[:, :], start=True, stop=True)
                        return last
                    kb.emit("tensor", tmms, reads=[r_natb[pp], r_consts],
                            writes=[r_ps[pb]])
                    kb.emit("vector",
                            lambda e, pp=pp, g4=g4, pb=pb: e.tensor_copy(
                                stgb[pp][:, g4 * 512:(g4 + 1) * 512],
                                ps[pb][:, 0:512]),
                            reads=[r_ps[pb]], rmw=[r_stgb[pp]])
                dma("sync", AP(dstT, ic * 2048, [[64 * 2048, 128], [1, 2048]]),
                    stgb[pp][:, :], reads=[r_stgb[pp]], rmw=[r_wT[m]],
                    ch=f"stg{pp}")
            prep_jobs.append(job)
    trip_jobs = []
    for ic in range(64):
        pi = ic % 2
        tsrc = AP(wd_nat, ic * 128, [[IN, DM], [1, 128]])
        tdst = AP(wdT_h, ic * 512, [[64 * DM, 128], [64 * 512, 4], [1, 512]])

        def job(pi=pi, tsrc=tsrc, tdst=tdst):
            kb.emit("scalar",
                    lambda e: e.dma_start(wtp[pi][:, :], tsrc, transpose=True),
                    reads=[r_wnat["d"]], writes=[r_wtp[pi]], dma=True,
                    ch=f"wtp{pi}")
            dma("scalar", tdst,
                AP(wtp[pi], 0, [[2048, 128], [512, 4], [1, 512]]),
                reads=[r_wtp[pi]], rmw=[r_wT["d"]], ch=f"tripst{pi}")
        trip_jobs.append(job)


    def emit_prologue(g, blk):
        bi = blk % 2
        xnb = xnTblk[bi]
        bi = blk % 2
        xnb = xnTblk[bi]
        dma("sync", AP(xnTblk[bi], 0, [[16 * TBA, 128], [TBA, 16], [1, TBA]]),
            AP(xnT_h, blk * TBA, [[16 * T, 128], [T, 16], [1, TBA]]),
            reads=[r_xnT_h[blk]], writes=[r_xnbuf[bi]], ch=f"xnld{bi}")

        pjc = [0]

        def projmm(mm_fn, ev_eng, ev_fn, wr, extra_reads=(), bi=bi):
            pb = 6 + (pjc[0] % 2)
            pjc[0] += 1
            kb.emit("tensor", lambda e, pb=pb, f=mm_fn: f(e, pb),
                    reads=[r_xnbuf[bi], r_wgrp] + list(extra_reads),
                    writes=[r_ps[pb]])
            kb.emit(ev_eng, lambda e, pb=pb, f=ev_fn: f(e, pb),
                    reads=[r_ps[pb], r_vecs, r_vbias], rmw=[wr])

        for hq in range(QHG):
            def mmq(e, pb, hq=hq, xnb=xnb):
                last = None
                for hc in range(16):
                    last = e.matmul(
                        ps[pb][:, 0:TBA],
                        wq_sb[:, hc * 1024 + hq * 128: hc * 1024 + hq * 128 + 128],
                        AP(xnb, hc * TBA, [[16 * TBA, 128], [1, TBA]]),
                        start=(hc == 0), stop=(hc == 15))
                return last

            def evq(e, pb, hq=hq, bi=bi, g=g):
                return e.activation(qb[bi][:, hq * TBA:(hq + 1) * TBA],
                                    ps[pb][:, 0:TBA], AF.Relu,
                                    bias=bq_s[:, g * 8 + hq:g * 8 + hq + 1],
                                    scale=RS128)
            projmm(mmq, "scalar", evq, r_qb[bi])
        for kv in range(KVG):
            def mmk(e, pb, kv=kv, xnb=xnb):
                last = None
                for hc in range(16):
                    last = e.matmul(
                        ps[pb][:, 0:TBA],
                        wk_sb[:, hc * 512 + kv * 128: hc * 512 + kv * 128 + 128],
                        AP(xnb, hc * TBA, [[16 * TBA, 128], [1, TBA]]),
                        start=(hc == 0), stop=(hc == 15))
                return last

            def evk(e, pb, kv=kv, bi=bi, g=g):
                return e.activation(kraw[bi][:, kv * TBA:(kv + 1) * TBA],
                                    ps[pb][:, 0:TBA], AF.Relu,
                                    bias=bk_s[:, g * 4 + kv:g * 4 + kv + 1])
            projmm(mmk, "scalar", evk, r_kraw[bi])
        for tt in range(2):
            def mmv(e, pb, tt=tt, xnb=xnb):
                last = None
                for hc in range(16):
                    lhs = AP(xnb, hc * TBA + tt * 128, [[16 * TBA, 128], [1, 128]])
                    last = e.matmul(ps[pb][:, 0:512], lhs,
                                    wv_sb[:, hc * 512:(hc + 1) * 512],
                                    start=(hc == 0), stop=(hc == 15))
                return last

            def evv(e, pb, tt=tt, bi=bi, g=g):
                return e.tensor_add(vb[bi][:, tt * 512:(tt + 1) * 512],
                                    ps[pb][:, 0:512],
                                    vbias[:, g * 512:(g + 1) * 512])
            projmm(mmv, "vector", evv, r_vb[bi])

        def mmz1(e, pb, xnb=xnb):
            last = None
            for hc in range(16):
                last = e.matmul(ps[pb][0:16, 0:TBA],
                                wg1_sb[:, hc * 16:(hc + 1) * 16],
                                AP(xnb, hc * TBA, [[16 * TBA, 128], [1, TBA]]),
                                start=(hc == 0), stop=(hc == 15))
            return last

        def evz1(e, pb):
            return e.tensor_copy(z1sb[:, :], ps[pb][0:16, 0:TBA])
        projmm(mmz1, "vector", evz1, r_z1)
        for kv in range(KVG):
            def mmz2(e, pb, kv=kv):
                return e.matmul(ps[pb][:, 0:TBA],
                                wg2_sb[:, kv * 128:(kv + 1) * 128],
                                z1sb[:, :], start=True, stop=True)

            def evz2(e, pb, kv=kv, g=g):
                return e.tensor_scalar_add(gt1[:, kv * TBA:(kv + 1) * TBA],
                                           ps[pb][:, 0:TBA],
                                           bg2_s[:, g * 4 + kv:g * 4 + kv + 1])
            pb = 6 + (pjc[0] % 2)
            pjc[0] += 1
            kb.emit("tensor", lambda e, pb=pb, f=mmz2: f(e, pb),
                    reads=[r_z1, r_wgrp], writes=[r_ps[pb]])
            kb.emit("vector", lambda e, pb=pb, f=evz2: f(e, pb),
                    reads=[r_ps[pb], r_vecs], rmw=[r_gt1])

        kb.emit("scalar", lambda e: e.activation(gt2[:, :], gt1[:, :], AF.Abs),
                reads=[r_gt1], writes=[r_gt2])
        kb.emit("scalar",
                lambda e: e.activation(gt2[:, :], gt2[:, :], AF.Exp, scale=-1.0),
                rmw=[r_gt2])
        kb.emit("scalar",
                lambda e: e.activation(gt2[:, :], gt2[:, :], AF.Ln, bias=1.0),
                rmw=[r_gt2])
        kb.emit("vector",
                lambda e: e.tensor_scalar_min(gt1[:, :], gt1[:, :], 0.0),
                rmw=[r_gt1])
        kb.emit("vector",
                lambda e: e.tensor_sub(gt1[:, :], gt1[:, :], gt2[:, :]),
                reads=[r_gt2], rmw=[r_gt1])
        for kv in range(KVG):
            kb.emit("vector",
                    lambda e, kv=kv, bi=bi: e.tensor_tensor_scan(
                        bbuf[:, kv * TBA:(kv + 1) * TBA],
                        scanmask[:, :], gt1[:, kv * TBA:(kv + 1) * TBA],
                        0.0, ALU.mult, ALU.add),
                    reads=[r_gt1, r_consts], rmw=[r_b])
        kb.emit("scalar",
                lambda e, bi=bi: e.activation(expb[bi][:, :], bbuf[:, :],
                                              AF.Exp, scale=1.0 / 16.0),
                reads=[r_b], writes=[r_expb[bi]])
        kb.emit("scalar",
                lambda e, bi=bi: e.activation(expmb[:, :], bbuf[:, :],
                                              AF.Exp, scale=-1.0 / 16.0),
                reads=[r_b], writes=[r_expmb])
        kb.emit("vector",
                lambda e, bi=bi: e.tensor_mul(kfm[bi][:, :], kraw[bi][:, :],
                                              expmb[:, :]),
                reads=[r_kraw[bi], r_expmb], writes=[r_kfm[bi]])

        def qexp(e, bi=bi):
            last = None
            for hq in range(QHG):
                kv = hq // 2
                last = e.tensor_mul(
                    qb[bi][:, hq * TBA:(hq + 1) * TBA],
                    qb[bi][:, hq * TBA:(hq + 1) * TBA],
                    expb[bi][:, kv * TBA:(kv + 1) * TBA])
            return last
        kb.emit("vector", qexp, reads=[r_expb[bi]], rmw=[r_qb[bi]])
        for half in range(2):
            pb = 6 + (half % 2)

            def mmt(e, pb=pb, half=half, bi=bi):
                last = None
                for k in range(4):
                    j = half * 4 + k
                    last = e.matmul(ps[pb][:, k * 128:(k + 1) * 128],
                                    kfm[bi][:, j * 128:(j + 1) * 128],
                                    identb[:, :], start=True, stop=True)
                return last
            kb.emit("tensor", mmt, reads=[r_kfm[bi], r_consts], writes=[r_ps[pb]])

            def evt(e, pb=pb, half=half, bi=bi):
                last = None
                for k in range(4):
                    j = half * 4 + k
                    kv, tt = j // 2, j % 2
                    last = e.tensor_copy(
                        kpm[bi][:, tt * 512 + kv * 128: tt * 512 + kv * 128 + 128],
                        ps[pb][:, k * 128:(k + 1) * 128])
                return last
            kb.emit("vector", evt, reads=[r_ps[pb]], rmw=[r_kpm[bi]])


    def emit_scan(g, blk):
        bi = blk % 2
        tj0 = [0]
        for cl in range(CPB):
            cpar = cl % 2
            tt, rr = cl // 2, (cl % 2) * 64

            def sdelta(e, bi=bi, tt=tt, rr=rr):
                last = None
                for kv in range(KVG):
                    c0 = tt * 512 + kv * 128
                    last = e.matmul(ps[0][:, kv * 128:(kv + 1) * 128],
                                    kpm[bi][rr:rr + 64, c0:c0 + 128],
                                    vb[bi][rr:rr + 64, c0:c0 + 128],
                                    start=False, stop=False,
                                    skip_group_check=True)
                return last
            kb.emit("tensor", sdelta, reads=[r_kpm[bi], r_vb[bi]], rmw=[r_ps[0]])

            def sdecay(e, cl=cl, bi=bi, cpar=cpar):
                last = None
                for kv in range(KVG):
                    cc = kv * TBA + (cl + 1) * CH - 1
                    last = e.tensor_scalar_mul(
                        ps[0][:, kv * 128:(kv + 1) * 128],
                        ps[0][:, kv * 128:(kv + 1) * 128],
                        expb[bi][:, cc:cc + 1])
                return e.tensor_copy(ssnap[1 - cpar][:, :], ps[0][:, :])
            kb.emit("vector", sdecay, reads=[r_expb[bi]], rmw=[r_ps[0]],
                    writes=[r_ssnap[1 - cpar]])

            def atmm(e, cl=cl, bi=bi, cpar=cpar, rr=rr):
                last = None
                for hq in range(QHG):
                    kv = hq // 2
                    last = e.matmul(
                        ps[5][rr:rr + 64, hq * CH:(hq + 1) * CH],
                        kfm[bi][:, kv * TBA + cl * CH: kv * TBA + (cl + 1) * CH],
                        qb[bi][:, hq * TBA + cl * CH: hq * TBA + (cl + 1) * CH],
                        start=True, stop=True,
                        tile_position=(0, rr))
                return last
            kb.emit("tensor", atmm, reads=[r_kfm[bi], r_qb[bi]],
                    writes=[r_ps[5]])

            def atev(e, cpar=cpar, rr=rr):
                last = None
                for hq in range(QHG):
                    last = e.tensor_mul(ats[rr:rr + 64, hq * CH:(hq + 1) * CH],
                                        ps[5][rr:rr + 64, hq * CH:(hq + 1) * CH],
                                        mask64[rr:rr + 64, :])
                return last
            kb.emit("vector", atev, reads=[r_ps[5], r_consts],
                    writes=[r_ats[cpar]])

            ob = (1 + cpar, 3 + cpar)

            def omm(e, bi=bi, cpar=cpar, tt=tt, rr=rr, ob=ob, cl=cl):
                last = None
                for hq in range(QHG):
                    kv = hq // 2
                    pb = ob[hq // 4]
                    cc = (hq % 4) * 128
                    c0 = tt * 512 + kv * 128
                    e.matmul(ps[pb][rr:rr + 64, cc:cc + 128],
                             ats[rr:rr + 64, hq * CH:(hq + 1) * CH],
                             vb[bi][rr:rr + 64, c0:c0 + 128],
                             start=True, stop=False, skip_group_check=True,
                             tile_position=(rr, rr))
                    last = e.matmul(ps[pb][rr:rr + 64, cc:cc + 128],
                                    qb[bi][:, hq * TBA + cl * CH: hq * TBA + (cl + 1) * CH],
                                    ssnap[cpar][:, kv * 128:(kv + 1) * 128],
                                    start=False, stop=True,
                                    skip_group_check=True,
                                    tile_position=(0, rr))
                return last
            kb.emit("tensor", omm,
                    reads=[r_ats[cpar], r_vb[bi], r_qb[bi], r_ssnap[cpar]],
                    writes=[r_ps[ob[0]], r_ps[ob[1]]])

            def msq(e, cl=cl, ob=ob, rr=rr):
                last = None
                m0 = (cl // 2) * QHG
                for hq in range(QHG):
                    pb = ob[hq // 4]
                    cc = (hq % 4) * 128
                    last = e.activation(
                        sqscr[rr:rr + 64, :], ps[pb][rr:rr + 64, cc:cc + 128],
                        AF.Square,
                        accum_out=ms_b[rr:rr + 64, m0 + hq: m0 + hq + 1])
                return last
            kb.emit("scalar", msq, reads=[r_ps[ob[0]], r_ps[ob[1]]],
                    rmw=[r_ms[cl]])

            m0 = (cl // 2) * QHG
            kb.emit("scalar",
                    lambda e, rr=rr, m0=m0: e.activation(
                        rstd_b[rr:rr + 64, m0:m0 + QHG],
                        ms_b[rr:rr + 64, m0:m0 + QHG], AF.Ln,
                        scale=1.0 / HD, bias=epsc[rr:rr + 64, 0:1]),
                    reads=[r_ms[cl]], writes=[r_rstd[cl]])
            kb.emit("scalar",
                    lambda e, rr=rr, m0=m0: e.activation(
                        rstd_b[rr:rr + 64, m0:m0 + QHG],
                        rstd_b[rr:rr + 64, m0:m0 + QHG], AF.Exp,
                        scale=-0.5),
                    rmw=[r_rstd[cl]])

            opar = (cl // 2) % 2

            def oev(e, cl=cl, ob=ob, opar=opar, rr=rr):
                last = None
                m0 = (cl // 2) * QHG
                for hq in range(QHG):
                    pb = ob[hq // 4]
                    cc = (hq % 4) * 128
                    last = e.tensor_scalar_mul(
                        osb[opar][rr:rr + 64, hq * 128:(hq + 1) * 128],
                        ps[pb][rr:rr + 64, cc:cc + 128],
                        rstd_b[rr:rr + 64, m0 + hq: m0 + hq + 1])
                return last
            kb.emit("vector", oev, reads=[r_ps[ob[0]], r_ps[ob[1]],
                                           r_rstd[cl]],
                    rmw=[r_osb[opar]])
            if cl % 2 == 1:
                for hh in range(2):
                    pb = 5

                    def otmm(e, pb=pb, hh=hh, opar=opar):
                        last = None
                        for k in range(4):
                            hq = hh * 4 + k
                            last = e.matmul(
                                ps[pb][:, k * 128:(k + 1) * 128],
                                osb[opar][:, hq * 128:(hq + 1) * 128],
                                identb[:, :], start=True, stop=True)
                        return last
                    kb.emit("tensor", otmm, reads=[r_osb[opar], r_consts],
                            writes=[r_ps[pb]])

                    def otev(e, pb=pb, hh=hh, opar=opar):
                        last = None
                        for k in range(4):
                            hq = hh * 4 + k
                            last = e.tensor_scalar_mul(
                                oTb[opar][:, hq * 128:(hq + 1) * 128],
                                ps[pb][:, k * 128:(k + 1) * 128],
                                gnw_s[:, 0:1])
                        return last
                    kb.emit("vector", otev, reads=[r_ps[pb], r_vecs],
                            rmw=[r_oTb[opar]])
                t0 = blk * TBA + (cl - 1) * CH
                dst = AP(oT_h, (t0 // 128) * 2048 + g * QHG * 128,
                         [[NH * T, 128], [1, QHG * 128]])
                src = AP(oTb[opar], 0, [[QHG * 128, 128], [1, QHG * 128]])
                dma("sync", dst, src, reads=[r_oTb[opar]], rmw=[r_oT_h],
                    ch=f"oT{opar}")


    for g in range(NG):
        if g > 0:
            emit_gweights(g)
        kb.emit("tensor",
                lambda e: e.matmul(ps[0][:, :], zer512[:, 0:128], zer512[:, :],
                                   start=True, stop=False, skip_group_check=True),
                reads=[r_consts], writes=[r_ps[0]])
        kb.emit("vector", lambda e: e.tensor_copy(ssnap[0][:, :], ps[0][:, :]),
                rmw=[r_ps[0]], writes=[r_ssnap[0]])

        prev = None
        for blk in range(NBA):
            emit_prologue(g, blk)
            if prev is not None:
                emit_scan(g, prev)
            prev = blk
        emit_scan(g, prev)

    kb.barrier()
    sb.release()
    sb.mark()
    wo_sb = sb.alloc([128, 16 * 2048], BF16, "wo_sb")
    wtp = [sb.alloc([128, 2048], BF16, f"wtp{i}") for i in range(2)]
    natb = [sb.alloc([128, 2048], BF16, f"natb{i}") for i in range(4)]
    stgb = [sb.alloc([128, 2048], BF16, f"stgb{i}") for i in range(4)]
    oTt = [sb.alloc([128, NH * 128], BF16, f"oTt{i}") for i in range(2)]
    sst = [sb.alloc([128, 2048], F32, f"sst{i}") for i in range(2)]
    r_wo = R("wo")
    r_oTt = [R(), R()]
    r_sst = [R(), R()]
    r_s_h = [R(f"s_h{i}") for i in range(T // 128)]

    for q4 in range(4):
        dma("gpsimd",
            AP(wo_sb, q4 * 4 * 2048, [[16 * 2048, 128], [2048, 4], [1, 2048]]),
            AP(din["wo"], q4 * 4 * 128 * DM, [[DM, 128], [128 * DM, 4], [1, 2048]]),
            rmw=[r_wo], ch="wo")
    for tc in range(T // 128):
        pi = tc % 2
        dma("sync", AP(oTt[pi], 0, [[NH * 128, 128], [1, NH * 128]]),
            AP(oT_h, tc * 2048, [[NH * T, 128], [1, NH * 128]]),
            reads=[r_oT_h], writes=[r_oTt[pi]], ch=f"oTt{pi}")
        for _ in range(4):
            if trip_jobs:
                trip_jobs.pop(0)()
        for _ in range(8):
            if prep_jobs:
                prep_jobs.pop(0)()
        for hb in range(4):
            pb = 6 + (hb % 2)

            def wom(e, pi=pi, hb=hb, pb=pb):
                last = None
                for fc in range(16):
                    last = e.matmul(
                        ps[pb][:, 0:512],
                        oTt[pi][:, fc * 128:(fc + 1) * 128],
                        wo_sb[:, fc * 2048 + hb * 512: fc * 2048 + (hb + 1) * 512],
                        start=(fc == 0), stop=(fc == 15))
                return last
            kb.emit("tensor", wom, reads=[r_oTt[pi], r_wo], writes=[r_ps[pb]])
            kb.emit("vector",
                    lambda e, pi=pi, hb=hb, pb=pb: e.tensor_copy(
                        sst[pi][:, hb * 512:(hb + 1) * 512], ps[pb][:, 0:512]),
                    reads=[r_ps[pb]], rmw=[r_sst[pi]])
        dma("sync", AP(s_h, tc * 128 * DM, [[DM, 128], [1, DM]]), sst[pi][:, :],
            reads=[r_sst[pi]], writes=[r_s_h[tc]], ch=f"sst{pi}")

    while trip_jobs:
        trip_jobs.pop(0)()
    while prep_jobs:
        prep_jobs.pop(0)()

    kb.barrier()
    sb.release()
    sb.mark()
    x1 = sb.alloc([128, 16 * TBF], BF16, "x1")
    x2 = sb.alloc([128, 64 * TBF], BF16, "x2")
    wus = [sb.alloc([128, 16 * 256], BF16, f"wus{i}") for i in range(2)]
    wgs = [sb.alloc([128, 16 * 256], BF16, f"wgs{i}") for i in range(2)]
    wds = [sb.alloc([128, 32 * 256], BF16, f"wds{i}") for i in range(2)]
    sgt = [sb.alloc([128, TBF], BF16, f"sgt{i}") for i in range(2)]
    gst = [sb.alloc([128, 2048], BF16, f"gst{i}") for i in range(4)]
    x1b = sb.alloc([128, 16 * TBF], BF16, "x1b")
    sblk = sb.alloc([128, 2048], BF16, "sblk")
    xblk = [sb.alloc([128, 512], F32, f"xblk{i}") for i in range(2)]
    outst = [sb.alloc([128, 512], F32, f"outst{i}") for i in range(2)]
    dsc = sb.alloc([128, 512], F32, "dsc")

    r_x1 = R()
    rbuf = [{"c": [R() for _ in range(16)], "w": R()} for _ in range(2)]
    r_x2 = R()
    r_x2g = [R() for _ in range(64)]
    r_wus = [R(), R()]
    r_wgs = [R(), R()]
    r_wds = [R(), R()]
    r_sgt = [R(), R()]
    r_gst = [R(), R(), R(), R()]
    r_sblk = R()
    r_xblk = [R(), R()]
    r_outst = [R(), R()]
    r_dsc = R()

    def ffn_head(blk):
        t0 = blk * TBF
        TB = blk % 2
        xa, xo = (x1, x1b) if TB == 0 else (x1b, x1)
        ra, ro = rbuf[TB], rbuf[1 - TB]
        dma("sync", AP(xa, 0, [[16 * TBF, 128], [TBF, 16], [1, TBF]]),
            AP(xnT_h, t0, [[16 * T, 128], [T, 16], [1, TBF]]),
            reads=[r_xnT_h[2 * blk], r_xnT_h[2 * blk + 1]],
            writes=[ra["w"]] + ra["c"], ch=f"x1ld{TB}")

        def sumul(e, xa=xa):
            last = None
            for c in range(16):
                a = xa[:, c * TBF:(c + 1) * TBF]
                last = e.tensor_scalar_mul(a, a, suug_s[:, c:c + 1])
            return last
        kb.emit("vector", sumul, reads=[r_vecs], rmw=ra["c"])
        for cb in range(16):
            pb = 6 + (cb % 2)
            kb.emit("tensor",
                    lambda e, cb=cb, pb=pb, xa=xa: e.matmul(
                        ps[pb][:, 0:TBF], h128b[:, :],
                        xa[:, cb * TBF:(cb + 1) * TBF], start=True, stop=True),
                    reads=[ra["c"][cb], r_consts], writes=[r_ps[pb]])
            kb.emit("vector",
                    lambda e, cb=cb, pb=pb, xa=xa: e.tensor_copy(
                        xa[:, cb * TBF:(cb + 1) * TBF], ps[pb][:, 0:TBF]),
                    reads=[r_ps[pb]], rmw=[ra["c"][cb]])
        bufs = [xa, xo]
        rr_ = [ra, ro]
        for s in range(4):
            h = 1 << s
            src_b, dst_b = bufs[s % 2], bufs[(s + 1) % 2]
            ng = 16 // (2 * h)
            dims = [[16 * TBF, 128], [2 * h * TBF, ng], [TBF, h], [1, TBF]]
            aA = AP(src_b, 0, dims)
            aB = AP(src_b, h * TBF, dims)
            oA = AP(dst_b, 0, dims)
            oB = AP(dst_b, h * TBF, dims)
            rd = ra["c"] if s == 0 else [rr_[s % 2]["w"]]
            wr = [rr_[(s + 1) % 2]["w"]] + (ra["c"] if s == 3 else [])

            def bf1(e, aA=aA, aB=aB, oA=oA, oB=oB):
                e.tensor_add(oA, aA, aB)
                return e.tensor_sub(oB, aA, aB)
            kb.emit("vector", bf1, reads=list(rd), writes=list(wr))

    def ffn_ug_load(sup):
        pp = sup % 2
        dma("sync", wus[pp][:, :],
            AP(wuT_h, sup * 4096, [[64 * 2048, 128], [1, 4096]]),
            reads=[r_wT["u"]], writes=[r_wus[pp]], ch=f"wus{pp}")
        dma("sync", wgs[pp][:, :],
            AP(wgT_h, sup * 4096, [[64 * 2048, 128], [1, 4096]]),
            reads=[r_wT["g"]], writes=[r_wgs[pp]], ch=f"wgs{pp}")

    def ffn_ug(blk, npre=0):
        TB = blk % 2
        xa = (x1, x1b)[TB]
        ra = rbuf[TB]
        for sup in range(IN // 256):
            pp = sup % 2
            if sup >= npre:
                ffn_ug_load(sup)
            for isl in range(2):
                ic = sup * 2 + isl
                pbu = pp * 4 + isl
                pbg = pp * 4 + 2 + isl

                def upm(e, isl=isl, pbu=pbu, pp=pp, xa=xa):
                    last = None
                    for hc in range(16):
                        last = e.matmul(
                            ps[pbu][:, 0:TBF],
                            wus[pp][:, isl * 2048 + hc * 128: isl * 2048 + hc * 128 + 128],
                            xa[:, hc * TBF:(hc + 1) * TBF],
                            start=(hc == 0), stop=(hc == 15))
                    return last
                kb.emit("tensor", upm, reads=[r_wus[pp], ra["w"]],
                        writes=[r_ps[pbu]])

                def gam(e, isl=isl, pbg=pbg, pp=pp, xa=xa):
                    last = None
                    for hc in range(16):
                        last = e.matmul(
                            ps[pbg][:, 0:TBF],
                            wgs[pp][:, isl * 2048 + hc * 128: isl * 2048 + hc * 128 + 128],
                            xa[:, hc * TBF:(hc + 1) * TBF],
                            start=(hc == 0), stop=(hc == 15))
                    return last
                kb.emit("tensor", gam, reads=[r_wgs[pp], ra["w"]],
                        writes=[r_ps[pbg]])
                kb.emit("scalar",
                        lambda e, ic=ic, pbg=pbg, isl=isl: e.activation(
                            sgt[isl][:, :], ps[pbg][:, 0:TBF], AF.Silu,
                            scale=sgate_s[:, ic:ic + 1]),
                        reads=[r_ps[pbg], r_vecs], writes=[r_sgt[isl]])
                dstx2 = AP(x2, ic, [[64 * TBF, 128], [64, TBF]])
                kb.emit("vector",
                        lambda e, ic=ic, pbu=pbu, isl=isl, dstx2=dstx2:
                        e.scalar_tensor_tensor(dstx2, ps[pbu][:, 0:TBF],
                                               sup_s[:, ic:ic + 1],
                                               sgt[isl][:, :],
                                               ALU.mult, ALU.mult),
                        reads=[r_ps[pbu], r_sgt[isl], r_vecs], rmw=[r_x2])

    def ffn_sandwich(blk):
        for hmat in (h128b, bh64b):
            for g4 in range(64):
                pb = g4 % 2

                def fmm2(e, g4=g4, pb=pb, hmat=hmat):
                    last = None
                    for k in range(4):
                        j = g4 * 4 + k
                        last = e.matmul(ps[pb][:, k * 128:(k + 1) * 128],
                                        x2[:, j * 128:(j + 1) * 128],
                                        hmat[:, :], start=True, stop=True)
                    return last
                kb.emit("tensor", fmm2, reads=[r_x2, r_x2g[g4], r_consts],
                        writes=[r_ps[pb]])
                kb.emit("vector",
                        lambda e, g4=g4, pb=pb: e.tensor_copy(
                            x2[:, g4 * 512:(g4 + 1) * 512], ps[pb][:, 0:512]),
                        reads=[r_ps[pb]], rmw=[r_x2g[g4]])

    def ffn_down(blk):
        for ob in range(4):
            for pc in range(4):
                wpp = (ob * 4 + pc) % 2
                dma("sync", wds[wpp][:, :],
                    AP(wdT_h, ob * 64 * 512 + pc * 16 * 512,
                       [[64 * DM, 128], [1, 16 * 512]]),
                    reads=[r_wT["d"]], writes=[r_wds[wpp]], ch=f"wds{wpp}")
                for tc4 in range(TBF // 128):
                    pb = (ob % 2) * 4 + tc4

                    def dwm(e, pc=pc, wpp=wpp, tc4=tc4, pb=pb):
                        last = None
                        for icl in range(16):
                            ic = pc * 16 + icl
                            lhs = AP(x2, tc4 * 128 * 64 + ic,
                                     [[64 * TBF, 128], [64, 128]])
                            last = e.matmul(
                                ps[pb][:, 0:512], lhs,
                                AP(wds[wpp], icl * 512,
                                   [[16 * 512, 128], [1, 512]]),
                                start=(ic == 0), stop=(ic == 63))
                        return last
                    rds = [r_x2, r_wds[wpp]] + r_x2g[tc4 * 16:(tc4 + 1) * 16]
                    if pc == 0:
                        kb.emit("tensor", dwm, reads=rds, writes=[r_ps[pb]])
                    else:
                        kb.emit("tensor", dwm, reads=rds, rmw=[r_ps[pb]])
            for tc4 in range(TBF // 128):
                pb = (ob % 2) * 4 + tc4
                kb.emit("vector",
                        lambda e, ob=ob, pb=pb: e.scalar_tensor_tensor(
                            dsc[:, 0:512], ps[pb][:, 0:512], 1.0,
                            wsdb[:, ob * 512:(ob + 1) * 512], ALU.mult,
                            ALU.mult),
                        reads=[r_ps[pb], r_wsdb], writes=[r_dsc])
                kb.emit("scalar",
                        lambda e, ob=ob, tc4=tc4: e.activation(
                            gst[tc4][:, ob * 512:(ob + 1) * 512], dsc[:, 0:512],
                            AF.Silu),
                        reads=[r_dsc], rmw=[r_gst[tc4]])
        for tc4 in range(TBF // 128):
            tc = blk * (TBF // 128) + tc4
            dma("gpsimd", sblk[:, :], AP(s_h, tc * 128 * DM, [[DM, 128], [1, DM]]),
                reads=[r_s_h[tc]], writes=[r_sblk], ch="sblk")
            if fulldbg:
                dma("sync", AP(g_d, tc * 128 * DM, [[DM, 128], [1, DM]]),
                    gst[tc4][:, :], reads=[r_gst[tc4]], ch="dbg")
            for h4 in range(4):
                po = h4 % 2
                dma("gpsimd", xblk[po][:, :],
                    AP(din["x"], tc * 128 * DM + h4 * 512, [[DM, 128], [1, 512]]),
                    writes=[r_xblk[po]], ch=f"xblk{po}")
                kb.emit("vector",
                        lambda e, tc4=tc4, h4=h4, po=po: e.tensor_mul(
                            outst[po][:, :], gst[tc4][:, h4 * 512:(h4 + 1) * 512],
                            sblk[:, h4 * 512:(h4 + 1) * 512]),
                        reads=[r_gst[tc4], r_sblk], writes=[r_outst[po]])
                kb.emit("vector",
                        lambda e, h4=h4, po=po: e.tensor_add(
                            outst[po][:, :], outst[po][:, :], xblk[po][:, :]),
                        reads=[r_xblk[po]], rmw=[r_outst[po]])
                dma("gpsimd", AP(y, tc * 128 * DM + h4 * 512, [[DM, 128], [1, 512]]),
                    outst[po][:, :], reads=[r_outst[po]], ch=f"y{po}")

    for blk in range(NBF):
        if blk == 0:
            ffn_head(0)
            ffn_ug_load(0)
            ffn_ug_load(1)
        ffn_ug(blk, npre=2)
        ffn_sandwich(blk)
        if blk + 1 < NBF:
            ffn_head(blk + 1)
            ffn_ug_load(0)
            ffn_ug_load(1)
        ffn_down(blk)

    with nc.Block() as block:
        kb.emit_all(nc, block)
    return nc


_NC_CACHE = None


def kernel(**inputs):
    global _NC_CACHE
    if _NC_CACHE is None:
        _NC_CACHE = build_program()
    nc = _NC_CACHE
    x = np.ascontiguousarray(inputs["L0_attn_hidden_states"], dtype=np.float32)
    base = {
        "wq": np.ascontiguousarray(inputs["L0_attn_Wq"], dtype=np.float32),
        "bq": np.ascontiguousarray(inputs["L0_attn_bq"], dtype=np.float32).reshape(1, -1),
        "wk": np.ascontiguousarray(inputs["L0_attn_Wk"], dtype=np.float32),
        "bk": np.ascontiguousarray(inputs["L0_attn_bk"], dtype=np.float32).reshape(1, -1),
        "wv": np.ascontiguousarray(inputs["L0_attn_Wv"], dtype=np.float32),
        "bv": np.ascontiguousarray(inputs["L0_attn_bv"], dtype=np.float32).reshape(1, -1),
        "wg1": np.ascontiguousarray(inputs["L0_attn_Wg1"], dtype=np.float32),
        "wg2": np.ascontiguousarray(inputs["L0_attn_Wg2"], dtype=np.float32),
        "bg2": np.ascontiguousarray(inputs["L0_attn_bg2"], dtype=np.float32).reshape(1, -1),
        "gnw": np.ascontiguousarray(inputs["L0_attn_g_norm_w"], dtype=np.float32).reshape(1, -1),
        "wo": np.ascontiguousarray(inputs["L0_attn_Wo"], dtype=np.float32),
        "su_ug": np.ascontiguousarray(inputs["L0_ffn_SU_ug"], dtype=np.float32).reshape(1, -1),
        "su_dp": np.ascontiguousarray(inputs["L0_ffn_SU_dp"], dtype=np.float32).reshape(1, -1),
        "wsug": np.ascontiguousarray(inputs["L0_ffn_Wscale_ug"], dtype=np.float32).reshape(1, -1),
        "wsdp": np.ascontiguousarray(inputs["L0_ffn_Wscale_dp"], dtype=np.float32).reshape(1, -1),
        "wup": np.ascontiguousarray(inputs["L0_ffn_W_up"], dtype=np.float32),
        "wgate": np.ascontiguousarray(inputs["L0_ffn_W_gate"], dtype=np.float32),
        "wdown": np.ascontiguousarray(inputs["L0_ffn_W_down"], dtype=np.float32),
    }
    in_maps = []
    for c in range(8):
        m = dict(base)
        m["x"] = np.ascontiguousarray(x[c])
        in_maps.append(m)
    res = run_bass_kernel_spmd(nc, in_maps, core_ids=list(range(8)))
    return np.stack([np.asarray(r["y"], dtype=np.float32) for r in res.results], axis=0)
